# version 01
# baseline (speedup 1.0000x reference)
import numpy as np
import ml_dtypes
import concourse.bass as bass
import concourse.mybir as mybir
from concourse.bass_utils import run_bass_kernel_spmd

F32 = mybir.dt.float32
BF16 = mybir.dt.bfloat16
U8 = mybir.dt.uint8
AF = mybir.ActivationFunctionType
ALU = mybir.AluOpType
AX = mybir.AxisListType
BF = ml_dtypes.bfloat16

NCORES = 8
SEQ = 2048
DM = 1024
NC_TOK = 2 * SEQ
NT = NC_TOK // 512
NCH = DM // 128
NH = 16
HD = 64
EPS = 1e-5

SUBLAYERS = ["moe0a", "attn0", "moe0b", "ffn1a", "attn1", "ffn1b"]
ATTN_STAGE = 3
ATTN_FEAT = "rntsb"



def tile_lhsT(wT):
    Kt, Mt = wT.shape
    a = wT.reshape(Kt // 128, 128, Mt // 128, 128)
    return np.ascontiguousarray(a.transpose(2, 1, 0, 3))


def host_prepare(g):
    d = {}
    ew = np.asarray(g["L0_ffn_expert_w"], np.float32)
    t = np.stack([tile_lhsT(ew[e]) for e in range(8)])
    d["W_exp"] = np.ascontiguousarray(t.transpose(0, 2, 1, 3, 4))
    d["W_expb"] = np.ascontiguousarray(
        np.asarray(g["L0_ffn_expert_b"], np.float32).reshape(8, 8, 128).transpose(1, 0, 2))
    d["W_rw"] = np.ascontiguousarray(
        np.asarray(g["L0_ffn_router_w"], np.float32).reshape(8, 128, 8).transpose(1, 0, 2))
    d["W_rb"] = np.asarray(g["L0_ffn_router_b"], np.float32).reshape(1, 8)
    for n in ["q", "k", "v", "o"]:
        W = np.asarray(g[f"L0_attn_W{n}"], np.float32)
        d[f"W_a0{n}"] = tile_lhsT(np.ascontiguousarray(W.T))
        A = np.asarray(g[f"L0_attn_A{n}"], np.float32)
        d[f"W_a0A{n}"] = np.ascontiguousarray(A.T.reshape(8, 128, 8).transpose(1, 0, 2))
        Bm = np.asarray(g[f"L0_attn_B{n}"], np.float32)
        d[f"W_a0B{n}"] = np.ascontiguousarray(Bm.T.reshape(8, 8, 128))
        b = np.asarray(g[f"L0_attn_b{n}"], np.float32)
        d[f"W_a0b{n}"] = np.ascontiguousarray(b.reshape(8, 128).T)
        W1 = np.asarray(g[f"L1_attn_w{n}"], np.float32)
        d[f"W_a1{n}"] = tile_lhsT(W1)
    d["W_rez"] = np.asarray(g["L0_attn_rezero"], np.float32).reshape(1, 1)
    for n in ["w1", "w2", "h1", "h2", "w3"]:
        W = np.asarray(g[f"L1_ffn_{n}_w"], np.float32)
        d[f"W_f{n}"] = tile_lhsT(np.ascontiguousarray(W.T))
        s = np.asarray(g[f"L1_ffn_{n}_s"], np.float32)
        d[f"W_f{n}s"] = np.ascontiguousarray(s.reshape(-1, 128).T)
    d["C_ones"] = np.ones((128, 1), BF)
    d["C_ones1"] = np.ones((1, 128), BF)
    d["C_ones1f"] = np.ones((1, 128), np.float32)
    d["C_ident"] = np.eye(128, dtype=BF)
    inv = 1.0 / (10000.0 ** (np.arange(0, HD, 2, dtype=np.float64) / HD))
    ang = np.arange(SEQ, dtype=np.float64)[:, None] * inv[None, :]
    c64 = np.repeat(np.cos(ang).T, 2, axis=0)
    s64 = np.repeat(np.sin(ang).T, 2, axis=0)
    d["C_cos"] = np.concatenate([c64, c64], 0).astype(BF)
    d["C_sin"] = np.concatenate([s64, s64], 0).astype(BF)
    S = np.zeros((HD, HD), np.float32)
    for i in range(0, HD, 2):
        S[i, i + 1] = -1.0
        S[i + 1, i] = 1.0
    S128 = np.zeros((128, 128), np.float32)
    S128[:64, :64] = S
    S128[64:, 64:] = S
    d["C_swapT"] = np.ascontiguousarray(S128.T).astype(BF)
    slopes = 0.5 ** np.arange(NH)
    al = np.zeros((NH, 3, SEQ), np.float64)
    kpos = np.arange(SEQ, dtype=np.float64)
    for h in range(NH):
        v = np.asarray(8.0 * slopes[h] * kpos, np.float32).astype(np.float64)
        hi = np.asarray(v, np.float32).astype(BF).astype(np.float64)
        mid = np.asarray(v - hi, np.float32).astype(BF).astype(np.float64)
        al[h, 0], al[h, 1], al[h, 2] = hi, mid, v - hi - mid
    d["C_alibi"] = al.astype(BF)
    m = np.zeros((128, 128), np.float32)
    m[np.triu_indices(128, 1)] = -30000.0
    d["C_mask"] = m.astype(BF)
    d["C_identf"] = np.eye(128, dtype=np.float32)
    ones2 = np.ones((2, 128), np.float32)
    d["C_ones2f"] = ones2
    h2 = np.zeros((128, 2), np.float32)
    h2[0:64, 0] = 1.0
    h2[64:128, 1] = 1.0
    d["C_h2"] = h2.astype(BF)
    ab2 = np.zeros((2, 16), np.float32)
    for h in range(16):
        ab2[h % 2, h] = -(0.5 ** h) * 2047.0 * 8.0 * 0.125
    d["C_ab2"] = ab2
    m2 = np.zeros((2, 16), np.float32)
    for h in range(16):
        m2[h % 2, h] = 1.0
    d["C_m2"] = m2
    sel = np.zeros((8, 8, 128), np.float32)
    for e in range(8):
        sel[e, e, :] = 1.0
    d["C_sel"] = np.ascontiguousarray(sel.transpose(1, 0, 2)).astype(BF)
    return d


def shard_x(x):
    return [np.ascontiguousarray(np.asarray(x[2 * c:2 * c + 2], np.float32).transpose(0, 2, 1))
            for c in range(NCORES)]


def unshard_y(parts):
    out = np.empty((16, SEQ, DM), np.float32)
    for c, p in enumerate(parts):
        out[2 * c:2 * c + 2] = np.asarray(p).transpose(0, 2, 1)
    return out



class Sem:
    def __init__(self, h, step=1):
        self.h = h
        self.n = 0
        self.step = step

    def inc(self, instr):
        instr.then_inc(self.h, self.step)
        self.n += self.step
        return (self.h, self.n)

    def tok(self):
        return (self.h, self.n)


class Plan:
    def __init__(self, engs):
        self.sems = dict(engs)
        self.base = {e: s.n for e, s in self.sems.items()}
        self.order = {e: [] for e in self.sems}
        self.idx = {}
        self.emitted = {e: 0 for e in self.sems}

    def add(self, eng, name):
        assert name not in self.idx, name
        self.idx[name] = (eng, len(self.order[eng]))
        self.order[eng].append(name)

    def tok(self, name):
        eng, i = self.idx[name]
        s = self.sems[eng]
        return (s.h, self.base[eng] + (i + 1) * s.step)

    def inc(self, eng, instr, name):
        s = self.sems[eng]
        i = self.emitted[eng]
        assert self.order[eng][i] == name, \
            f"plan order mismatch on {eng}: expected {self.order[eng][i]}, got {name}"
        self.emitted[eng] += 1
        t = s.inc(instr)
        assert t == self.tok(name)
        return t

    def done(self):
        for e in self.sems:
            assert self.emitted[e] == len(self.order[e]), \
                f"engine {e}: emitted {self.emitted[e]} of {len(self.order[e])}"


class Ctx:
    pass


def _selfsync(c, eng, instr):
    t = c.X.inc(instr)
    eng.wait_ge(t[0], t[1])


def _wait(eng, cache, tok):
    if tok is None:
        return
    h, v = tok
    if v <= 0:
        return
    if cache.get(h, 0) >= v:
        return
    eng.wait_ge(h, v)
    cache[h] = v


def weights_needed(sublayers):
    need = set()
    for sl in sublayers:
        if sl.startswith("moe"):
            need |= {"W_exp", "W_expb", "W_rw", "W_rb"}
        if sl == "attn0":
            for n in "qkvo":
                need |= {f"W_a0{n}", f"W_a0A{n}", f"W_a0B{n}", f"W_a0b{n}"}
            need.add("W_rez")
        if sl == "attn1":
            need |= {f"W_a1{n}" for n in "qkvo"}
        if sl.startswith("ffn1"):
            need |= {"W_fw1", "W_fw2", "W_fh1", "W_fh2", "W_fw3",
                     "W_fw1s", "W_fw2s", "W_fh1s", "W_fh2s", "W_fw3s"}
    return need


WSHAPES = {
    "W_exp": [8, 128, 8, 8, 128], "W_expb": [8, 8, 128], "W_rw": [128, 8, 8],
    "W_rb": [1, 8], "W_rez": [1, 1],
    "W_fw1": [32, 128, 8, 128], "W_fw2": [32, 128, 8, 128],
    "W_fh1": [32, 128, 32, 128], "W_fh2": [32, 128, 32, 128], "W_fw3": [8, 128, 32, 128],
    "W_fw1s": [128, 32], "W_fw2s": [128, 32], "W_fh1s": [128, 32],
    "W_fh2s": [128, 32], "W_fw3s": [128, 8],
}
for _n in "qkvo":
    WSHAPES[f"W_a0{_n}"] = [8, 128, 8, 128]
    WSHAPES[f"W_a0A{_n}"] = [128, 8, 8]
    WSHAPES[f"W_a0B{_n}"] = [8, 8, 128]
    WSHAPES[f"W_a0b{_n}"] = [128, 8]
    WSHAPES[f"W_a1{_n}"] = [8, 128, 8, 128]
CSHAPES = {"C_ones": ([128, 1], BF16), "C_ones1": ([1, 128], BF16),
           "C_ident": ([128, 128], BF16), "C_cos": ([128, SEQ], BF16),
           "C_sin": ([128, SEQ], BF16), "C_swapT": ([128, 128], BF16),
           "C_alibi": ([NH, 3, SEQ], BF16), "C_mask": ([128, 128], BF16),
           "C_sel": ([8, 8, 128], BF16), "C_ones1f": ([1, 128], F32),
           "C_identf": ([128, 128], F32), "C_ones2f": ([2, 128], F32),
           "C_h2": ([128, 2], BF16), "C_ab2": ([2, 16], F32), "C_m2": ([2, 16], F32)}
CAST_W = ["W_exp", "W_fw1", "W_fw2", "W_fh1", "W_fh2", "W_fw3",
          "W_a0q", "W_a0k", "W_a0v", "W_a0o", "W_a1q", "W_a1k", "W_a1v", "W_a1o"]


def build_program(sublayers):
    nc = bass.Bass("TRN2", target_bir_lowering=False, dynamic_dma_scratch_size=16384)
    c = Ctx()
    c.nc = nc
    need = weights_needed(sublayers)
    c.need = need

    c.x_in = nc.dram_tensor("x", [2, DM, SEQ], F32, kind="ExternalInput").ap()
    c.y_out = nc.dram_tensor("y", [2, DM, SEQ], F32, kind="ExternalOutput").ap()
    c.win = {n: nc.dram_tensor(n, s, F32, kind="ExternalInput").ap()
             for n, s in WSHAPES.items() if n in need}
    c.cin = {n: nc.dram_tensor(n, s, dt, kind="ExternalInput").ap()
             for n, (s, dt) in CSHAPES.items()}
    c.wsc = {n: nc.dram_tensor(n + "_b", WSHAPES[n], BF16).ap()
             for n in CAST_W if n in need}
    c.q_sp = nc.dram_tensor("q_sp", [2, NCH, 128, SEQ], BF16).ap()
    c.k_sp = nc.dram_tensor("k_sp", [2, NCH, 128, SEQ], BF16).ap()
    c.v_sp = nc.dram_tensor("v_sp", [2, NCH, 128, SEQ], BF16).ap()
    c.xln_spd = nc.dram_tensor("xln_spd", [2, NCH, 128, SEQ], BF16).ap()

    import contextlib
    with contextlib.ExitStack() as st:
        e = st.enter_context
        c.R = e(nc.sbuf_tensor("R", [128, NCH, NC_TOK], BF16))
        c.XLN = e(nc.sbuf_tensor("XLN", [128, NCH, 512], BF16))
        c.HA = e(nc.sbuf_tensor("HA", [128, 32, 512], BF16))
        c.HB = e(nc.sbuf_tensor("HB", [128, 32, 512], BF16))
        c.HC = e(nc.sbuf_tensor("HC", [128, 32, 512], BF16))
        c.WB = e(nc.sbuf_tensor("WB", [128, 2, 4096], BF16))
        c.XSQ = e(nc.sbuf_tensor("XSQ", [128, 2, 512], BF16))
        c.STT = e(nc.sbuf_tensor("STT", [1, 1024], F32))
        c.BCA = e(nc.sbuf_tensor("BCA", [128, 2, 512], BF16))
        c.POOL = e(nc.sbuf_tensor("POOL", [128, 5120], U8))
        c.LG = e(nc.sbuf_tensor("LG", [128, 4, 8], F32))
        c.LGS = e(nc.sbuf_tensor("LGS", [128, 4, 8], F32))
        c.CFB = e(nc.sbuf_tensor("CFB", [128, 4, 8], BF16))
        c.RWS = e(nc.sbuf_tensor("RWS", [128, 8, 8], BF16))
        c.RBS = e(nc.sbuf_tensor("RBS", [1, 8], BF16))
        c.ONES = e(nc.sbuf_tensor("ONES", [128, 1], BF16))
        c.ONES1 = e(nc.sbuf_tensor("ONES1", [1, 128], BF16))
        c.ONES1F = e(nc.sbuf_tensor("ONES1F", [1, 128], F32))
        c.IDENT = e(nc.sbuf_tensor("IDENT", [128, 128], BF16))
        c.MASKC = e(nc.sbuf_tensor("MASKC", [128, 128], BF16))
        c.SWT = e(nc.sbuf_tensor("SWT", [128, 128], BF16))
        c.SCL = e(nc.sbuf_tensor("SCL", [128, 5, 32], F32))
        c.ACC4 = e(nc.sbuf_tensor("ACC4", [128, 8], F32))
        c.RD = e(nc.sbuf_tensor("RD", [128, 4], F32))
        c.LORA = e(nc.sbuf_tensor("LORA", [128, 8, 8], BF16))
        c.LBIA = e(nc.sbuf_tensor("LBIA", [128, 8], F32))
        c.REZ = e(nc.sbuf_tensor("REZ", [128, 1], F32))
        c.OSTM = e(nc.sbuf_tensor("OSTM", [128, 2, 512], F32))
        c.ABIA = e(nc.sbuf_tensor("ABIA", [128, NH], F32))
        c.NRM2 = e(nc.sbuf_tensor("NRM2", [2, 64], F32))
        c.QM = e(nc.sbuf_tensor("QM", [2, 16], F32))
        c.BHE = e(nc.sbuf_tensor("BHE", [2, 16], F32))
        c.IDENTF = e(nc.sbuf_tensor("IDENTF", [128, 128], F32))
        c.ONES2F = e(nc.sbuf_tensor("ONES2F", [2, 128], F32))
        c.H2 = e(nc.sbuf_tensor("H2", [128, 2], BF16))
        c.AB2 = e(nc.sbuf_tensor("AB2", [2, 16], F32))
        c.M2 = e(nc.sbuf_tensor("M2", [2, 16], F32))
        c.REZI = e(nc.sbuf_tensor("REZI", [1, 1], F32))
        pv = c.POOL[:]
        c.CT8 = pv.bitcast(BF16)[0:8, 0:512]
        c.EBS = pv.bitcast(BF16)[0:8, 512:1536].rearrange("r (m o) -> r m o", m=8)
        c.SEL = pv.bitcast(BF16)[0:8, 1536:2560].rearrange("r (s o) -> r s o", s=8)
        c.LORB = pv.bitcast(BF16)[0:8, 0:1024].rearrange("r (m o) -> r m o", m=8)
        c.USB = pv.bitcast(BF16)[0:8, 1024:1536]
        c.AF_ = pv.bitcast(BF16)[:, 1536:2560].rearrange("p (a b) -> p a b", a=8)
        c.PB = [e(nc.psum_tensor(f"PBm{i}", [128, 512], F32)) for i in range(4)]
        c.PA4 = e(nc.psum_tensor("PA4", [128, 512], F32))
        c.PA5 = e(nc.psum_tensor("PA5", [128, 512], F32))
        c.PO3 = e(nc.psum_tensor("PO3", [128, 512], F32))
        c.PT2 = e(nc.psum_tensor("PT2", [128, 1024], BF16))
        c.X = Sem(e(nc.semaphore("s_x")))
        c.SY = Sem(e(nc.semaphore("s_sy")), 16)
        c.G = Sem(e(nc.semaphore("s_g")), 16)
        c.T = Sem(e(nc.semaphore("s_t")))
        c.V = Sem(e(nc.semaphore("s_v")))
        c.S = Sem(e(nc.semaphore("s_s")))
        c.wc = {k: {} for k in ["sync", "gpsimd", "scalar", "vector", "tensor"]}
        emit_all(c, sublayers)
    return nc


def emit_all(c, sublayers):
    nc = c.nc
    c.cast_tok = {}
    with nc.Block("init") as blk:
        def _g(g):
            for sq in range(2):
                for ch in range(NCH):
                    c.rload = c.G.inc(g.dma_start(
                        out=c.R[:, ch, sq * SEQ:(sq + 1) * SEQ],
                        in_=c.x_in[sq, ch * 128:(ch + 1) * 128, :]))
            if "W_rw" in c.win:
                c.G.inc(g.dma_start(out=c.RWS[:], in_=c.win["W_rw"]))
                c.G.inc(g.dma_start(out=c.RBS[:], in_=c.win["W_rb"]))
            if "W_fw1s" in c.win:
                for i, n in enumerate(["W_fw1s", "W_fw2s", "W_fh1s", "W_fh2s"]):
                    c.G.inc(g.dma_start(out=c.SCL[:, i, :], in_=c.win[n]))
                c.G.inc(g.dma_start(out=c.SCL[:, 4, 0:8], in_=c.win["W_fw3s"]))
            c.small_tok = c.G.tok()
            order = [n for n in CAST_W if n in c.wsc]
            for n in order:
                src, dst = c.win[n], c.wsc[n]
                per = int(np.prod(src.shape[1:])) * 4
                step = max(1, (8 << 20) // per)
                tk = None
                for i0 in range(0, src.shape[0], step):
                    i1 = min(src.shape[0], i0 + step)
                    tk = c.G.inc(g.dma_start(out=dst[i0:i1], in_=src[i0:i1]))
                c.cast_tok[n] = tk
        blk.gpsimd(_g)

        def _sy(sy):
            for dst, src in [(c.ONES, c.cin["C_ones"]), (c.ONES1, c.cin["C_ones1"]),
                             (c.ONES1F, c.cin["C_ones1f"]),
                             (c.IDENT, c.cin["C_ident"]), (c.MASKC, c.cin["C_mask"]),
                             (c.SWT, c.cin["C_swapT"]), (c.IDENTF, c.cin["C_identf"]),
                             (c.ONES2F, c.cin["C_ones2f"]), (c.H2, c.cin["C_h2"]),
                             (c.AB2, c.cin["C_ab2"]), (c.M2, c.cin["C_m2"])]:
                c.SY.inc(sy.dma_start(out=dst[:], in_=src))
            c.const_tok = c.SY.tok()
        blk.sync(_sy)

    gfin = c.G.tok()
    c.rload = gfin
    c.small_tok = gfin
    for k in list(c.cast_tok):
        c.cast_tok[k] = gfin
    c.rtok = [c.rload] * NT
    c.rfree = [None] * NT
    c.xln_free = None
    c.hbuf_free = None
    c.wb_free = None
    c.stat_free = None
    c.pool_free = None
    c.pb_free = [None] * 4
    c.pa_free = None
    c.pt_free = None
    c.po_free = None
    final = len(sublayers) - 1

    for i, sl in enumerate(sublayers):
        if sl.startswith("moe"):
            emit_moe(c, is_final=(i == final))
        elif sl.startswith("ffn1"):
            emit_ffn1(c, is_final=(i == final))
        elif sl == "attn0":
            emit_attn(c, 0)
        elif sl == "attn1":
            emit_attn(c, 1)
    if sublayers[-1].startswith("attn"):
        emit_store(c)

    with nc.Block("fin") as blk:
        def _g(g):
            _wait(g, c.wc["gpsimd"], c.G.tok())
        blk.gpsimd(_g)

        def _sy(sy):
            _wait(sy, c.wc["sync"], c.SY.tok())
        blk.sync(_sy)



def plan_ln(P, p, tt):
    for ch in range(NCH):
        P.add("S", f"{p}sq{tt}_{ch}")
    P.add("S", f"{p}rstd{tt}")
    for ch in range(NCH):
        P.add("T", f"{p}sqmm{tt}_{ch}")
    P.add("T", f"{p}bc{tt}")
    P.add("V", f"{p}comb1_{tt}")
    P.add("V", f"{p}comb2_{tt}")
    P.add("V", f"{p}xln{tt}")


def ln_scalar(c, P, p, s, tt, prev_sq_reader=None, rtok=None):
    wc = c.wc["scalar"]
    _wait(s, wc, rtok if rtok is not None else c.rtok[tt])
    if tt == 0 and c.stat_free is not None:
        _wait(s, wc, c.stat_free)
    for ch in range(NCH):
        j = tt * NCH + ch
        if j >= 2:
            t2, c2 = divmod(j - 2, NCH)
            _wait(s, wc, P.tok(f"{p}sqmm{t2}_{c2}"))
        elif prev_sq_reader is not None:
            _wait(s, wc, prev_sq_reader)
        ins = s.activation(out=c.XSQ[:, j % 2, :], in_=c.R[:, ch, tt * 512:(tt + 1) * 512],
                           func=AF.Square)
        P.inc("S", ins, f"{p}sq{tt}_{ch}")
    _wait(s, wc, P.tok(f"{p}comb1_{tt}"))
    if tt > 0:
        _wait(s, wc, P.tok(f"{p}bc{tt - 1}"))
    s.activation(out=c.STT[0:1, 512:1024], in_=c.STT[0:1, 512:1024], func=AF.Ln)
    ins = s.activation(out=c.STT[0:1, 512:1024], in_=c.STT[0:1, 512:1024], func=AF.Exp,
                       scale=-0.5)
    P.inc("S", ins, f"{p}rstd{tt}")


def ln_tensor(c, P, p, t, tt, aux_war=None, rtok=None):
    wc = c.wc["tensor"]
    _wait(t, wc, c.const_tok)
    _wait(t, wc, rtok if rtok is not None else c.rtok[tt])
    _wait(t, wc, aux_war)
    if tt == 0:
        _wait(t, wc, c.stat_free)
        _wait(t, wc, c.pa_free)
    for ch in range(NCH):
        t.matmul(out=c.PA4[0:1, :], lhsT=c.ONES[:], rhs=c.R[:, ch, tt * 512:(tt + 1) * 512],
                 start=(ch == 0), stop=(ch == NCH - 1))
        _wait(t, wc, P.tok(f"{p}sq{tt}_{ch}"))
        ins = t.matmul(out=c.PA5[0:1, :], lhsT=c.ONES[:],
                       rhs=c.XSQ[:, (tt * NCH + ch) % 2, :],
                       start=(ch == 0), stop=(ch == NCH - 1))
        P.inc("T", ins, f"{p}sqmm{tt}_{ch}")


def ln_tensor_bcast(c, P, p, t, tt):
    wc = c.wc["tensor"]
    _wait(t, wc, P.tok(f"{p}comb2_{tt}"))
    _wait(t, wc, P.tok(f"{p}rstd{tt}"))
    t.matmul(out=c.PA4[:], lhsT=c.ONES1F[:], rhs=c.STT[0:1, 512:1024], start=True, stop=True)
    ins = t.matmul(out=c.PA5[:], lhsT=c.ONES1F[:], rhs=c.STT[0:1, 0:512], start=True, stop=True)
    P.inc("T", ins, f"{p}bc{tt}")


def ln_vector(c, P, p, v, tt, xln_war=None, rsl=None):
    wc = c.wc["vector"]
    _wait(v, wc, P.tok(f"{p}sqmm{tt}_{NCH - 1}"))
    if tt > 0:
        _wait(v, wc, P.tok(f"{p}bc{tt - 1}"))
    v.tensor_scalar(out=c.STT[0:1, 0:512], in0=c.PA4[0:1, :], scalar1=1.0 / DM,
                    scalar2=None, op0=ALU.mult)
    v.tensor_scalar(out=c.STT[0:1, 512:1024], in0=c.PA5[0:1, :], scalar1=1.0 / DM,
                    scalar2=None, op0=ALU.mult)
    v.tensor_tensor(out=c.PA4[0:1, :], in0=c.STT[0:1, 0:512], in1=c.STT[0:1, 0:512],
                    op=ALU.mult)
    v.tensor_tensor(out=c.STT[0:1, 512:1024], in0=c.STT[0:1, 512:1024], in1=c.PA4[0:1, :],
                    op=ALU.subtract)
    ins = v.tensor_scalar(out=c.STT[0:1, 512:1024], in0=c.STT[0:1, 512:1024], scalar1=EPS,
                          scalar2=None, op0=ALU.add)
    P.inc("V", ins, f"{p}comb1_{tt}")
    _wait(v, wc, P.tok(f"{p}rstd{tt}"))
    v.tensor_tensor(out=c.STT[0:1, 0:512], in0=c.STT[0:1, 0:512], in1=c.STT[0:1, 512:1024],
                    op=ALU.mult)
    ins = v.tensor_scalar(out=c.STT[0:1, 0:512], in0=c.STT[0:1, 0:512], scalar1=-1.0,
                          scalar2=None, op0=ALU.mult)
    P.inc("V", ins, f"{p}comb2_{tt}")
    _wait(v, wc, P.tok(f"{p}bc{tt}"))
    _wait(v, wc, xln_war)
    v.tensor_copy(out=c.BCA[:, 0, :], in_=c.PA4[:])
    v.tensor_copy(out=c.BCA[:, 1, :], in_=c.PA5[:])
    rs = rsl if rsl is not None else (lambda ch: c.R[:, ch, tt * 512:(tt + 1) * 512])
    for ch in range(NCH):
        v.tensor_tensor(out=c.XLN[:, ch, :], in0=rs(ch), in1=c.BCA[:, 0, :], op=ALU.mult)
        ins = v.tensor_tensor(out=c.XLN[:, ch, :], in0=c.XLN[:, ch, :], in1=c.BCA[:, 1, :],
                              op=ALU.add)
    P.inc("V", ins, f"{p}xln{tt}")



FFN_NG = 136
FFN_NLD = 88


def ffn_group_stage(g):
    if g < 32:
        return "w1", g
    if g < 64:
        return "w2", g - 32
    if g < 96:
        return "h1", g - 64
    if g < 128:
        return "h2", g - 96
    return "w3", g - 128


def ffn_load_of_group(g):
    st, m = ffn_group_stage(g)
    if st == "w1":
        return m // 4
    if st == "w2":
        return 8 + m // 4
    if st == "h1":
        return 16 + m
    if st == "h2":
        return 48 + m
    return 80 + m


def ffn_load_spec(li):
    if li < 8:
        return "w1", list(range(li * 4, li * 4 + 4)), li
    if li < 16:
        j = li - 8
        return "w2", list(range(32 + j * 4, 32 + j * 4 + 4)), j
    if li < 48:
        j = li - 16
        return "h1", [64 + j], j
    if li < 80:
        j = li - 48
        return "h2", [96 + j], j
    j = li - 80
    return "w3", [128 + j], j


def emit_ffn1(c, is_final):
    nc = c.nc
    P = Plan({"T": c.T, "S": c.S, "V": c.V, "SY": c.SY, "G": c.G})
    emit_ffn1.count = getattr(emit_ffn1, "count", 0) + 1
    p = f"f{emit_ffn1.count}_"
    for tt in range(NT):
        plan_ln(P, p, tt)
        for li in range(FFN_NLD):
            P.add("SY", f"{p}ld{tt}_{li}")
        for g in range(FFN_NG):
            P.add("T", f"{p}g{tt}_{g}")
        for g in range(128):
            P.add("S", f"{p}ev{tt}_{g}")
        for m in range(32):
            P.add("V", f"{p}h0_{tt}_{m}")
        for m in range(8):
            P.add("V", f"{p}res{tt}_{m}")
        if is_final:
            for m in range(8):
                P.add("G", f"{p}out{tt}_{m}")

    prev_hfree = c.hbuf_free
    prev_xfree = c.xln_free
    prev_wbfree = c.wb_free
    prev_pbfree = list(c.pb_free)
    prev_poolfree = c.pool_free

    def reader_tok(tt, g):
        st, m = ffn_group_stage(g)
        if st == "w3":
            return P.tok(f"{p}res{tt}_{m}")
        return P.tok(f"{p}ev{tt}_{g}")

    with nc.Block("ffn1") as blk:
        def _sy(sy):
            wc = c.wc["sync"]
            for tt in range(NT):
                for li in range(FFN_NLD):
                    st, gs, j = ffn_load_spec(li)
                    gli = tt * FFN_NLD + li
                    if gli >= 2:
                        ptt, pli = divmod(gli - 2, FFN_NLD)
                        _, pgs, _ = ffn_load_spec(pli)
                        _wait(sy, wc, reader_tok(ptt, pgs[-1]))
                    else:
                        _wait(sy, wc, prev_wbfree)
                    _wait(sy, wc, c.cast_tok["W_f" + st])
                    src = c.wsc["W_f" + st]
                    if st in ("w1", "w2"):
                        sl = src[j * 4:(j + 1) * 4]
                        dst = c.WB[:, gli % 2].rearrange("p (m ko) -> p m ko", m=4)
                    else:
                        sl = src[j:j + 1]
                        dst = c.WB[:, gli % 2].rearrange("p (m ko) -> p m ko", m=1)
                    ins = sy.dma_start(out=dst, in_=sl.rearrange("m p k o -> p m (k o)"))
                    P.inc("SY", ins, f"{p}ld{tt}_{li}")
        blk.sync(_sy)

        def _s(s):
            wc = c.wc["scalar"]
            _wait(s, wc, c.small_tok)
            for tt in range(NT):
                ln_scalar(c, P, p, s, tt,
                          prev_sq_reader=(None if tt > 0 else c.stat_free))
                for g in range(128):
                    st, m = ffn_group_stage(g)
                    _wait(s, wc, P.tok(f"{p}g{tt}_{g}"))
                    pb = c.PB[g % 4]
                    if st == "w1":
                        ins = s.activation(out=c.HA[:, m, :], in_=pb[:], func=AF.Silu,
                                           scale=c.SCL[:, 0, m:m + 1])
                    elif st == "w2":
                        ins = s.activation(out=c.HB[:, m, :], in_=pb[:], func=AF.Identity,
                                           scale=c.SCL[:, 1, m:m + 1])
                    elif st == "h1":
                        _wait(s, wc, P.tok(f"{p}h0_{tt}_{m}"))
                        ins = s.activation(out=c.HA[:, m, :], in_=pb[:], func=AF.Gelu,
                                           scale=c.SCL[:, 2, m:m + 1])
                    else:
                        ins = s.activation(out=c.HB[:, m, :], in_=pb[:], func=AF.Gelu,
                                           scale=c.SCL[:, 3, m:m + 1])
                    P.inc("S", ins, f"{p}ev{tt}_{g}")
        blk.scalar(_s)

        def _v(v):
            wc = c.wc["vector"]
            for tt in range(NT):
                xw = prev_xfree if tt == 0 else P.tok(f"{p}g{tt - 1}_{63}")
                ln_vector(c, P, p, v, tt, xln_war=xw)
                for m in range(32):
                    _wait(v, wc, P.tok(f"{p}ev{tt}_{m}"))
                    _wait(v, wc, P.tok(f"{p}ev{tt}_{32 + m}"))
                    if tt > 0:
                        _wait(v, wc, P.tok(f"{p}g{tt - 1}_{95}"))
                    else:
                        _wait(v, wc, prev_hfree)
                    ins = v.tensor_tensor(out=c.HC[:, m, :], in0=c.HA[:, m, :],
                                          in1=c.HB[:, m, :], op=ALU.mult)
                    P.inc("V", ins, f"{p}h0_{tt}_{m}")
                for m in range(8):
                    g = 128 + m
                    _wait(v, wc, P.tok(f"{p}g{tt}_{g}"))
                    _wait(v, wc, c.rfree[tt])
                    pb = c.PB[g % 4]
                    if is_final:
                        if tt == 0 and m < 2:
                            _wait(v, wc, prev_poolfree)
                        if m >= 2:
                            _wait(v, wc, P.tok(f"{p}out{tt}_{m - 2}"))
                        elif tt > 0:
                            _wait(v, wc, P.tok(f"{p}out{tt - 1}_{m + 6}"))
                        v.tensor_scalar(out=c.OSTM[:, m % 2, :], in0=pb[:],
                                        scalar1=c.SCL[:, 4, m:m + 1], scalar2=0.5,
                                        op0=ALU.mult, op1=ALU.mult)
                        ins = v.tensor_tensor(out=c.OSTM[:, m % 2, :], in0=c.OSTM[:, m % 2, :],
                                              in1=c.R[:, m, tt * 512:(tt + 1) * 512],
                                              op=ALU.add)
                    else:
                        v.tensor_scalar(out=pb[:], in0=pb[:],
                                        scalar1=c.SCL[:, 4, m:m + 1], scalar2=0.5,
                                        op0=ALU.mult, op1=ALU.mult)
                        ins = v.tensor_tensor(out=c.R[:, m, tt * 512:(tt + 1) * 512],
                                              in0=c.R[:, m, tt * 512:(tt + 1) * 512],
                                              in1=pb[:], op=ALU.add)
                    P.inc("V", ins, f"{p}res{tt}_{m}")
        blk.vector(_v)

        def _t(t):
            wc = c.wc["tensor"]
            for tt in range(NT):
                aux_war = P.tok(f"{p}xln{tt - 1}") if tt > 0 else c.stat_free
                ln_tensor(c, P, p, t, tt, aux_war=aux_war)
                ln_tensor_bcast(c, P, p, t, tt)
                for g in range(FFN_NG):
                    st, m = ffn_group_stage(g)
                    li = ffn_load_of_group(g)
                    _wait(t, wc, P.tok(f"{p}ld{tt}_{li}"))
                    pg = g - 4
                    if pg >= 0:
                        _wait(t, wc, reader_tok(tt, pg))
                    elif tt > 0:
                        _wait(t, wc, reader_tok(tt - 1, FFN_NG - 4 + g))
                    else:
                        _wait(t, wc, prev_pbfree[g % 4])
                    if st in ("w1", "w2"):
                        _wait(t, wc, P.tok(f"{p}xln{tt}"))
                        rhs = [c.XLN[:, k, :] for k in range(8)]
                    elif st == "h1":
                        _wait(t, wc, P.tok(f"{p}h0_{tt}_{31}"))
                        rhs = [c.HC[:, k, :] for k in range(32)]
                    elif st == "h2":
                        _wait(t, wc, P.tok(f"{p}ev{tt}_{64 + 31}"))
                        rhs = [c.HA[:, k, :] for k in range(32)]
                    else:
                        _wait(t, wc, P.tok(f"{p}ev{tt}_{96 + 31}"))
                        rhs = [c.HB[:, k, :] for k in range(32)]
                    W = c.WB[:, (tt * FFN_NLD + li) % 2]
                    if st in ("w1", "w2"):
                        off = (m % 4) * 1024
                    else:
                        off = 0
                    pb = c.PB[g % 4]
                    nk = len(rhs)
                    for k in range(nk):
                        ins = t.matmul(out=pb[:], lhsT=W[:, off + k * 128: off + (k + 1) * 128],
                                       rhs=rhs[k], start=(k == 0), stop=(k == nk - 1))
                    P.inc("T", ins, f"{p}g{tt}_{g}")
        blk.tensor(_t)

        if is_final:
            def _g(g):
                wc = c.wc["gpsimd"]
                yv = c.y_out.rearrange("s (ch q) t -> q ch s t", q=128)
                for tt in range(NT):
                    for m in range(8):
                        _wait(g, wc, P.tok(f"{p}res{tt}_{m}"))
                        dst = yv[:, m, tt // 4, (tt % 4) * 512:(tt % 4) * 512 + 512]
                        ins = g.dma_start(out=dst, in_=c.OSTM[:, m % 2, :])
                        P.inc("G", ins, f"{p}out{tt}_{m}")
            blk.gpsimd(_g)

    P.done()
    for tt in range(NT):
        c.rtok[tt] = P.tok(f"{p}res{tt}_{7}")
        c.rfree[tt] = P.tok(f"{p}res{tt}_{7}")
    c.xln_free = P.tok(f"{p}g{NT - 1}_{63}")
    c.hbuf_free = P.tok(f"{p}res{NT - 1}_{7}")
    c.wb_free = P.tok(f"{p}res{NT - 1}_{7}")
    c.stat_free = P.tok(f"{p}xln{NT - 1}")
    for b in range(4):
        g_last = max(g for g in range(FFN_NG) if g % 4 == b)
        c.pb_free[b] = reader_tok(NT - 1, g_last)
    c.pa_free = P.tok(f"{p}xln{NT - 1}")
    if is_final:
        c.pool_free = P.tok(f"{p}out{NT - 1}_{7}")



def emit_moe(c, is_final):
    nc = c.nc
    P = Plan({"T": c.T, "S": c.S, "V": c.V, "SY": c.SY, "G": c.G})
    emit_moe.count = getattr(emit_moe, "count", 0) + 1
    p = f"m{emit_moe.count}_"
    P.add("G", f"{p}poolA")
    P.add("G", f"{p}pool")
    for tt in range(NT):
        plan_ln(P, p, tt)
        for gr in range(4):
            P.add("T", f"{p}lg{tt}_{gr}")
        for gr in range(4):
            P.add("V", f"{p}cfc{tt}_{gr}")
        P.add("V", f"{p}cfA{tt}")
        for gr in range(4):
            P.add("S", f"{p}cex{tt}_{gr}")
        P.add("V", f"{p}cf2A{tt}")
        for gr in range(4):
            P.add("T", f"{p}ct{tt}_{gr}")
        for gr in range(4):
            P.add("V", f"{p}cte{tt}_{gr}")
        for e in range(8):
            P.add("T", f"{p}cb{tt}_{e}")
            P.add("V", f"{p}cbe{tt}_{e}")
        for mh in range(2):
            for e in range(8):
                P.add("SY", f"{p}ld{tt}_{mh}_{e}")
                P.add("V", f"{p}xe{tt}_{mh}_{e}")
                P.add("T", f"{p}em{tt}_{mh}_{e}")
            for m in range(4):
                P.add("V", f"{p}res{tt}_{mh}_{m}")
                if is_final:
                    P.add("G", f"{p}out{tt}_{mh}_{m}")

    prev_hfree = c.hbuf_free
    prev_xfree = c.xln_free
    prev_wbfree = c.wb_free
    prev_pbfree = list(c.pb_free)
    prev_pafree = c.pa_free
    prev_ptfree = c.pt_free
    prev_poolfree = c.pool_free

    with nc.Block("moe") as blk:
        def _g(g):
            wc = c.wc["gpsimd"]
            _wait(g, wc, c.G.tok())
            _wait(g, wc, prev_poolfree)
            ins = g.dma_start(out=c.EBS, in_=c.win["W_expb"].rearrange("m r o -> r m o"))
            P.inc("G", ins, f"{p}poolA")
            ins = g.dma_start(out=c.SEL, in_=c.cin["C_sel"])
            P.inc("G", ins, f"{p}pool")
        blk.gpsimd(_g)

        def _sy(sy):
            wc = c.wc["sync"]
            for tt in range(NT):
                for mh in range(2):
                    for e in range(8):
                        gli = (tt * 2 + mh) * 8 + e
                        if gli >= 2:
                            ptt, r = divmod(gli - 2, 16)
                            pmh, pe = divmod(r, 8)
                            _wait(sy, wc, P.tok(f"{p}em{ptt}_{pmh}_{pe}"))
                        else:
                            _wait(sy, wc, prev_wbfree)
                        _wait(sy, wc, c.cast_tok["W_exp"])
                        src = c.wsc["W_exp"][e, :, mh * 4:(mh + 1) * 4]
                        ins = sy.dma_start(out=c.WB[:, gli % 2],
                                           in_=src.rearrange("p m k o -> p (m k o)"))
                        P.inc("SY", ins, f"{p}ld{tt}_{mh}_{e}")
        blk.sync(_sy)

        def _s(s):
            wc = c.wc["scalar"]
            for tt in range(NT):
                ln_scalar(c, P, p, s, tt,
                          prev_sq_reader=(None if tt > 0 else c.stat_free))
                for gr in range(4):
                    _wait(s, wc, P.tok(f"{p}cfA{tt}"))
                    ins = s.activation(out=c.LGS[:, gr, :], in_=c.LGS[:, gr, :], func=AF.Exp,
                                       accum_out=c.ACC4[:, gr:gr + 1])
                    P.inc("S", ins, f"{p}cex{tt}_{gr}")
        blk.scalar(_s)

        def _v(v):
            wc = c.wc["vector"]
            for tt in range(NT):
                xw = prev_xfree if tt == 0 else P.tok(f"{p}lg{tt - 1}_{3}")
                ln_vector(c, P, p, v, tt, xln_war=xw)
                for gr in range(4):
                    _wait(v, wc, P.tok(f"{p}lg{tt}_{gr}"))
                    pa = c.PA4 if gr % 2 == 0 else c.PA5
                    ins = v.tensor_copy(out=c.LG[:, gr, :], in_=pa[:, 0:8])
                    P.inc("V", ins, f"{p}cfc{tt}_{gr}")
                _wait(v, wc, P.tok(f"{p}cfc{tt}_{3}"))
                for gr in range(4):
                    ins = v.reduce_max(out=c.RD[:, gr:gr + 1], in_=c.LG[:, gr, :],
                                       axis=AX.X, negate=True)
                _selfsync(c, v, ins)
                for gr in range(4):
                    v.tensor_scalar(out=c.LGS[:, gr, :], in0=c.LG[:, gr, :],
                                    scalar1=c.RD[:, gr:gr + 1], scalar2=None, op0=ALU.add)
                for gr in range(4):
                    ins = v.tensor_scalar(out=c.LG[:, gr, :], in0=c.LG[:, gr, :],
                                          scalar1=c.RD[:, gr:gr + 1], scalar2=0.0,
                                          op0=ALU.add, op1=ALU.is_ge)
                P.inc("V", ins, f"{p}cfA{tt}")
                for gr in range(4):
                    _wait(v, wc, P.tok(f"{p}cex{tt}_{gr}"))
                ins = v.reciprocal(out=c.RD[:, 0:4], in_=c.ACC4[:, 0:4])
                _selfsync(c, v, ins)
                for gr in range(4):
                    ins = v.tensor_scalar(out=c.LG[:, gr, :], in0=c.LG[:, gr, :],
                                          scalar1=c.RD[:, gr:gr + 1], scalar2=None,
                                          op0=ALU.mult)
                _selfsync(c, v, ins)
                for gr in range(4):
                    ins = v.tensor_copy(out=c.CFB[:, gr, :], in_=c.LG[:, gr, :])
                P.inc("V", ins, f"{p}cf2A{tt}")
                for gr in range(4):
                    _wait(v, wc, P.tok(f"{p}ct{tt}_{gr}"))
                    if tt > 0:
                        _wait(v, wc, P.tok(f"{p}em{tt - 1}_{1}_{0}"))
                    else:
                        _wait(v, wc, prev_poolfree)
                    ins = v.tensor_copy(out=c.CT8[:, gr * 128:(gr + 1) * 128],
                                        in_=c.PT2[0:8, gr * 128:(gr + 1) * 128])
                    P.inc("V", ins, f"{p}cte{tt}_{gr}")
                for e in range(8):
                    _wait(v, wc, P.tok(f"{p}cb{tt}_{e}"))
                    pa = c.PA4 if e % 2 == 0 else c.PA5
                    if tt > 0 and e < 2:
                        _wait(v, wc, P.tok(f"{p}xe{tt - 1}_{1}_{7}"))
                    elif tt == 0 and e < 2:
                        _wait(v, wc, prev_hfree)
                    ins = v.tensor_copy(out=c.HB[:, e, :], in_=pa[:])
                    P.inc("V", ins, f"{p}cbe{tt}_{e}")
                for mh in range(2):
                    for e in range(8):
                        _wait(v, wc, P.tok(f"{p}cbe{tt}_{e}"))
                        gx = (tt * 2 + mh) * 8 + e
                        if gx >= 1:
                            ptt, r = divmod(gx - 1, 16)
                            pmh, pe = divmod(r, 8)
                            _wait(v, wc, P.tok(f"{p}em{ptt}_{pmh}_{pe}"))
                        else:
                            _wait(v, wc, prev_hfree)
                        for ch in range(8):
                            ins = v.tensor_tensor(out=c.HA[:, ch, :], in0=c.XLN[:, ch, :],
                                                  in1=c.HB[:, e, :], op=ALU.mult)
                        P.inc("V", ins, f"{p}xe{tt}_{mh}_{e}")
                    for m in range(4):
                        _wait(v, wc, P.tok(f"{p}em{tt}_{mh}_{7}"))
                        _wait(v, wc, c.rfree[tt])
                        cglob = mh * 4 + m
                        if is_final:
                            jj = (tt * 2 + mh) * 4 + m
                            if jj >= 2:
                                pj = jj - 2
                                ptt2, r2 = divmod(pj, 8)
                                pmh2, pm2 = divmod(r2, 4)
                                _wait(v, wc, P.tok(f"{p}out{ptt2}_{pmh2}_{pm2}"))
                            ins = v.scalar_tensor_tensor(
                                out=c.OSTM[:, m % 2, :], in0=c.PB[m][:],
                                scalar=0.5, in1=c.R[:, cglob, tt * 512:(tt + 1) * 512],
                                op0=ALU.mult, op1=ALU.add)
                        else:
                            ins = v.scalar_tensor_tensor(
                                out=c.R[:, cglob, tt * 512:(tt + 1) * 512], in0=c.PB[m][:],
                                scalar=0.5, in1=c.R[:, cglob, tt * 512:(tt + 1) * 512],
                                op0=ALU.mult, op1=ALU.add)
                        P.inc("V", ins, f"{p}res{tt}_{mh}_{m}")
        blk.vector(_v)

        def _t(t):
            wc = c.wc["tensor"]
            _wait(t, wc, c.small_tok)
            for tt in range(NT):
                aux_war = P.tok(f"{p}cbe{tt - 1}_{7}") if tt > 0 else c.stat_free
                ln_tensor(c, P, p, t, tt, aux_war=aux_war)
                ln_tensor_bcast(c, P, p, t, tt)
                for gr in range(4):
                    _wait(t, wc, P.tok(f"{p}xln{tt}"))
                    pa = c.PA4 if gr % 2 == 0 else c.PA5
                    if gr >= 2:
                        _wait(t, wc, P.tok(f"{p}cfc{tt}_{gr - 2}"))
                    for ch in range(8):
                        t.matmul(out=pa[:, 0:8],
                                 lhsT=c.XLN[:, ch, gr * 128:(gr + 1) * 128],
                                 rhs=c.RWS[:, ch, :], start=(ch == 0), stop=False)
                    ins = t.matmul(out=pa[:, 0:8], lhsT=c.ONES1[:], rhs=c.RBS[:],
                                   start=False, stop=True)
                    P.inc("T", ins, f"{p}lg{tt}_{gr}")
                for gr in range(4):
                    _wait(t, wc, P.tok(f"{p}cf2A{tt}"))
                    if gr > 0:
                        _wait(t, wc, P.tok(f"{p}cte{tt}_{gr - 1}"))
                    elif tt > 0:
                        _wait(t, wc, P.tok(f"{p}cte{tt - 1}_{3}"))
                    else:
                        _wait(t, wc, prev_ptfree)
                    ins = t.transpose(out=c.PT2[0:8, gr * 128:(gr + 1) * 128],
                                      in_=c.CFB[:, gr, :], identity=c.IDENT[:])
                    P.inc("T", ins, f"{p}ct{tt}_{gr}")
                for e in range(8):
                    _wait(t, wc, P.tok(f"{p}cte{tt}_{3}"))
                    _wait(t, wc, P.tok(f"{p}pool"))
                    pa = c.PA4 if e % 2 == 0 else c.PA5
                    if e >= 2:
                        _wait(t, wc, P.tok(f"{p}cbe{tt}_{e - 2}"))
                    else:
                        _wait(t, wc, P.tok(f"{p}cfA{tt}"))
                    ins = t.matmul(out=pa[:], lhsT=c.SEL[:, e, :], rhs=c.CT8[:],
                                   start=True, stop=True)
                    P.inc("T", ins, f"{p}cb{tt}_{e}")
                for mh in range(2):
                    for e in range(8):
                        _wait(t, wc, P.tok(f"{p}ld{tt}_{mh}_{e}"))
                        _wait(t, wc, P.tok(f"{p}xe{tt}_{mh}_{e}"))
                        gli = (tt * 2 + mh) * 8 + e
                        if e == 0:
                            if mh == 1:
                                for m in range(4):
                                    _wait(t, wc, P.tok(f"{p}res{tt}_{0}_{m}"))
                            elif tt > 0:
                                for m in range(4):
                                    _wait(t, wc, P.tok(f"{p}res{tt - 1}_{1}_{m}"))
                            else:
                                for b in range(4):
                                    _wait(t, wc, prev_pbfree[b])
                            for m in range(4):
                                t.matmul(out=c.PB[m][:], lhsT=c.EBS[:, mh * 4 + m, :],
                                         rhs=c.CT8[:], start=True, stop=False)
                        W = c.WB[:, gli % 2].rearrange("p (m k o) -> p m k o", m=4, k=8)
                        for m in range(4):
                            for k in range(8):
                                ins = t.matmul(out=c.PB[m][:], lhsT=W[:, m, k],
                                               rhs=c.HA[:, k, :],
                                               start=False, stop=(e == 7 and k == 7))
                        P.inc("T", ins, f"{p}em{tt}_{mh}_{e}")
        blk.tensor(_t)

        if is_final:
            def _go(g):
                wc = c.wc["gpsimd"]
                yv = c.y_out.rearrange("s (ch q) t -> q ch s t", q=128)
                for tt in range(NT):
                    for mh in range(2):
                        for m in range(4):
                            _wait(g, wc, P.tok(f"{p}res{tt}_{mh}_{m}"))
                            cglob = mh * 4 + m
                            dst = yv[:, cglob, tt // 4, (tt % 4) * 512:(tt % 4) * 512 + 512]
                            ins = g.dma_start(out=dst, in_=c.OSTM[:, m % 2, :])
                            P.inc("G", ins, f"{p}out{tt}_{mh}_{m}")
            blk.gpsimd(_go)

    P.done()
    for tt in range(NT):
        c.rtok[tt] = P.tok(f"{p}res{tt}_{1}_{3}")
        c.rfree[tt] = P.tok(f"{p}res{tt}_{1}_{3}")
    c.xln_free = P.tok(f"{p}xe{NT - 1}_{1}_{7}")
    c.hbuf_free = P.tok(f"{p}em{NT - 1}_{1}_{7}")
    c.wb_free = P.tok(f"{p}em{NT - 1}_{1}_{7}")
    c.stat_free = P.tok(f"{p}xln{NT - 1}")
    for b in range(4):
        c.pb_free[b] = P.tok(f"{p}res{NT - 1}_{1}_{b}")
    c.pa_free = P.tok(f"{p}cbe{NT - 1}_{7}")
    c.pt_free = P.tok(f"{p}cte{NT - 1}_{3}")
    c.pool_free = P.tok(f"{p}em{NT - 1}_{1}_{0}")



def emit_store(c):
    nc = c.nc
    P = Plan({"V": c.V, "G": c.G})
    emit_store.count = getattr(emit_store, "count", 0) + 1
    p = f"st{emit_store.count}_"
    for tt in range(NT):
        for ch in range(NCH):
            P.add("V", f"{p}c{tt}_{ch}")
            P.add("G", f"{p}o{tt}_{ch}")
    with nc.Block("store") as blk:
        def _v(v):
            wc = c.wc["vector"]
            for tt in range(NT):
                _wait(v, wc, c.rtok[tt])
                for ch in range(NCH):
                    j = tt * NCH + ch
                    if j >= 2:
                        pt, pc = divmod(j - 2, NCH)
                        _wait(v, wc, P.tok(f"{p}o{pt}_{pc}"))
                    ins = v.tensor_copy(out=c.OSTM[:, j % 2, :],
                                        in_=c.R[:, ch, tt * 512:(tt + 1) * 512])
                    P.inc("V", ins, f"{p}c{tt}_{ch}")
        blk.vector(_v)

        def _g(g):
            wc = c.wc["gpsimd"]
            _wait(g, wc, c.G.tok())
            yv = c.y_out.rearrange("s (ch q) t -> q ch s t", q=128)
            for tt in range(NT):
                for ch in range(NCH):
                    j = tt * NCH + ch
                    _wait(g, wc, P.tok(f"{p}c{tt}_{ch}"))
                    dst = yv[:, ch, tt // 4, (tt % 4) * 512:(tt % 4) * 512 + 512]
                    ins = g.dma_start(out=dst, in_=c.OSTM[:, j % 2, :])
                    P.inc("G", ins, f"{p}o{tt}_{ch}")
        blk.gpsimd(_g)
    P.done()


def attn_kbs(layer, qi):
    if layer == 0:
        return [list(range(4 * kb, 4 * kb + 4)) for kb in range(4)]
    kcs = list(range(qi + 1))
    return [kcs[i:i + 4] for i in range(0, len(kcs), 4)]


def emit_attn(c, layer):
    nc = c.nc
    P = Plan({"T": c.T, "S": c.S, "V": c.V, "SY": c.SY, "G": c.G})
    emit_attn.count = getattr(emit_attn, "count", 0) + 1
    p = f"a{layer}_{emit_attn.count}_"
    L0 = (layer == 0)
    projs = ["q", "k", "v"]
    KR = 67 if L0 else 64
    wsc = {n: c.wsc[f"W_a{layer}{n}"] for n in "qkvo"}

    QAv = c.HC[:, 0:8, :].rearrange("p a b -> p (a b)").rearrange("p (b t) -> p b t", b=2)
    KAv = c.HC[:, 8:16, :].rearrange("p a b -> p (a b)").rearrange("p (b t) -> p b t", b=2)
    CS2v = c.HC[:, 16:24, :].rearrange("p a b -> p (a b)").rearrange("p (b t) -> p b t", b=2)
    STG = c.HC[:, 24:32, :]
    VT = c.HB[:].rearrange("p a b -> p (a b)").rearrange("p (h k d) -> p h k d", h=16, k=16)
    ATT = c.HA[:].rearrange("p a b -> p (a b)").rearrange("p (q d) -> p q d", q=16)
    OSB = c.OSTM[:].rearrange("p a b -> p (a b)").bitcast(BF16)
    PSC = c.WB[:, 0, 0:2048]
    PTS = c.WB[:, 1, 0:2048]

    ld_counter = [0]
    ld_slot = {}
    ld_war = {}

    def plan_load(name, reader_name):
        i = ld_counter[0]
        ld_slot[name] = i % 2
        ld_war[name] = None if i < 2 else plan_load.hist[i - 2][1]
        plan_load.hist.append((name, reader_name))
        ld_counter[0] += 1
        P.add("SY", name)
    plan_load.hist = []

    lora_prev = []
    psc_war = {}
    acc_war = {}
    pts_war = {}
    ptc_war = {}
    _psc_prev = {}
    _acc_prev = {}
    _pts_prev = {}
    _ptc_prev = {}

    if L0:
        P.add("G", p + "rez_ld")
        P.add("T", p + "rez_mm")
        P.add("V", p + "rez_ev")
        P.add("V", p + "ones0")
        P.add("V", p + "ones1")
    lwo_waits = {}
    lw_waits = {}
    for s in range(2):
        if not L0:
            P.add("SY", p + f"cs2a_{s}")
            P.add("SY", p + f"cs2_{s}")
        for tl in range(4):
            gt = s * 4 + tl
            plan_ln(P, p, gt)
            for proj in projs:
                if L0:
                    P.add("G", p + f"lwA{gt}_{proj}")
                    P.add("G", p + f"lwB{gt}_{proj}")
                    P.add("G", p + f"lw{gt}_{proj}")
                    lw_waits[p + f"lw{gt}_{proj}"] = list(lora_prev)
                    lora_prev = [p + f"ev{gt}_{proj}_{7}", p + f"g{gt}_{proj}_{7}"]
                for hf in range(2):
                    plan_load(p + f"wl{gt}_{proj}_{hf}", p + f"g{gt}_{proj}_{3 + 4 * hf}")
                if L0:
                    P.add("T", p + f"u{gt}_{proj}")
                    P.add("S", p + f"ue{gt}_{proj}")
                for m in range(8):
                    P.add("T", p + f"g{gt}_{proj}_{m}")
                    P.add("S", p + f"ev{gt}_{proj}_{m}")
                    if ATTN_STAGE < 1:
                        continue
                    if (not L0) and proj != "v" and "r" in ATTN_FEAT:
                        P.add("T", p + f"sw{gt}_{proj}_{m}")
                        P.add("V", p + f"rp{gt}_{proj}_{m}")
                    if proj != "v":
                        if "n" in ATTN_FEAT:
                            P.add("S", p + f"nsq{gt}_{proj}_{m}")
                            P.add("T", p + f"nmm{gt}_{proj}_{m}")
                            P.add("V", p + f"nrd{gt}_{proj}_{m}")
                    elif "t" in ATTN_FEAT:
                        P.add("T", p + f"vt{gt}_{m}")
                        P.add("V", p + f"vte{gt}_{m}")
                if proj != "v" and ATTN_STAGE >= 1 and "s" in ATTN_FEAT:
                    for ch in range(NCH - 1):
                        P.add("G", p + f"sp{gt}_{proj}c{ch}")
                    P.add("G", p + f"sp{gt}_{proj}")
            if L0 and ATTN_STAGE >= 1 and "s" in ATTN_FEAT:
                for ch in range(NCH - 1):
                    P.add("G", p + f"spx{gt}c{ch}")
                P.add("G", p + f"spx{gt}")
        if ATTN_STAGE >= 1 and "b" in ATTN_FEAT:
            P.add("V", p + f"nb{s}")
            P.add("T", p + f"nbm{s}")
            P.add("V", p + f"nbe{s}")
        if ATTN_STAGE < 2:
            continue
        for h in range(NH):
            P.add("SY", p + f"hq{s}_{h}")
            P.add("SY", p + f"hk{s}_{h}")
            if L0:
                P.add("SY", p + f"ha{s}_{h}")
            for qi in range(16):
                kbs = attn_kbs(layer, qi)
                for kb, kcs in enumerate(kbs):
                    nm = p + f"ex{s}_{h}_{qi}_{kb}"
                    P.add("T", p + f"qk{s}_{h}_{qi}_{kb}")
                    if (not L0) and kb == len(kbs) - 1:
                        P.add("V", p + f"dg{s}_{h}_{qi}")
                    P.add("S", nm)
                    psc_war[nm] = _psc_prev.get(kb)
                    acc_war[nm] = _acc_prev.get(kb)
                    _psc_prev[kb] = p + f"pt{s}_{h}_{qi}_{kb}"
                    _acc_prev[kb] = p + f"o{s}_{h}_{qi}"
                    ptn = p + f"pt{s}_{h}_{qi}_{kb}"
                    P.add("T", ptn)
                    ptc_war[ptn] = _ptc_prev.get("g")
                    _ptc_prev["g"] = p + f"pte{s}_{h}_{qi}_{kb}"
                    pte = p + f"pte{s}_{h}_{qi}_{kb}"
                    P.add("V", pte)
                    pts_war[pte] = _pts_prev.get(kb)
                    _pts_prev[kb] = p + f"pv{s}_{h}_{qi}_{kb}"
                    P.add("T", p + f"pv{s}_{h}_{qi}_{kb}")
                P.add("V", p + f"o{s}_{h}_{qi}")
        if ATTN_STAGE < 3:
            continue
        for hf in range(2):
            plan_load(p + f"wo{s}_{hf}", p + f"go{s}_{15}_{3 + 4 * hf}")
        if L0 and ATTN_STAGE >= 3:
            P.add("G", p + f"lwoA{s}")
            P.add("G", p + f"lwoB{s}")
            P.add("G", p + f"lwo{s}")
            lwo_waits[p + f"lwo{s}"] = list(lora_prev)
            lora_prev = [p + f"uoe{s}_{15}", p + f"go{s}_{15}_{7}"]
        for qi in range(16):
            if L0:
                P.add("SY", p + f"xr{s}_{qi}")
            P.add("T", p + f"at{s}_{qi}")
            P.add("V", p + f"ate{s}_{qi}")
            if L0:
                P.add("T", p + f"uo{s}_{qi}")
                P.add("S", p + f"uoe{s}_{qi}")
            for m in range(8):
                P.add("T", p + f"go{s}_{qi}_{m}")
                P.add("V", p + f"ro{s}_{qi}_{m}")
    at_war = {}
    for s in range(2):
        for qi in range(16):
            nm = p + f"at{s}_{qi}"
            at_war[nm] = [_ptc_prev.get(0), _ptc_prev.get(1)] if qi == 0 and s == 0 else None

    prev_hfree = c.hbuf_free
    prev_xfree = c.xln_free
    prev_wbfree = c.wb_free
    prev_pbfree = list(c.pb_free)
    prev_pafree = c.pa_free
    prev_ptfree = c.pt_free
    prev_pofree = c.po_free
    prev_poolfree = c.pool_free
    prev_statfree = c.stat_free
    prev_lorafree = getattr(c, "lora_free", None)
    rfree0 = list(c.rfree)

    pbseq = {0: [], 1: []}
    for s in range(2):
        gidx = 0
        for tl in range(4):
            gt = s * 4 + tl
            for proj in projs:
                for m in range(8):
                    gi = (gt * 24 + projs.index(proj) * 8 + m)
                    pbseq[gi % 2].append((p + f"g{gt}_{proj}_{m}", p + f"ev{gt}_{proj}_{m}"))
        if ATTN_STAGE >= 2:
            for h in range(NH):
                for qi in range(16):
                    for kb in range(len(attn_kbs(layer, qi))):
                        pbseq[kb % 2].append((p + f"qk{s}_{h}_{qi}_{kb}",
                                              p + f"ex{s}_{h}_{qi}_{kb}"))
        if ATTN_STAGE >= 3:
            for qi in range(16):
                for m in range(8):
                    pbseq[(qi * 8 + m) % 2].append((p + f"go{s}_{qi}_{m}",
                                                    p + f"ro{s}_{qi}_{m}"))
    pb_war = {}
    for b in (0, 1):
        prev = None
        for w, r in pbseq[b]:
            pb_war[w] = prev
            prev = r
    pb_last_reader = {b: pbseq[b][-1][1] for b in (0, 1)}

    def tokm(name):
        return None if name is None else P.tok(name)

    def pbwar_tok(writer, bank):
        r = pb_war.get(writer)
        if r is None:
            return prev_pbfree[bank]
        return P.tok(r)

    def proj_bank(gt, proj, m):
        return (gt * 24 + projs.index(proj) * 8 + m) % 2

    def proj_bank(gt, proj, m):
        return (gt * 24 + projs.index(proj) * 8 + m) % 2

    with nc.Block(f"attn{layer}") as blk:
        def _g(g):
            wc = c.wc["gpsimd"]
            _wait(g, wc, c.G.tok())
            if L0:
                _wait(g, wc, prev_poolfree)
                _wait(g, wc, prev_lorafree)
                ins = g.dma_start(out=c.REZI[:], in_=c.win["W_rez"])
                P.inc("G", ins, p + "rez_ld")
            for s in range(2):
                for tl in range(4):
                    gt = s * 4 + tl
                    for proj in projs:
                        if L0:
                            for wn in lw_waits[p + f"lw{gt}_{proj}"]:
                                _wait(g, wc, P.tok(wn))
                            if gt == 0 and proj == "q":
                                _wait(g, wc, prev_lorafree)
                                _wait(g, wc, prev_poolfree)
                            ins = g.dma_start(out=c.LORA[:], in_=c.win[f"W_a0A{proj}"])
                            P.inc("G", ins, p + f"lwA{gt}_{proj}")
                            ins = g.dma_start(out=c.LORB, in_=c.win[f"W_a0B{proj}"])
                            P.inc("G", ins, p + f"lwB{gt}_{proj}")
                            ins = g.dma_start(out=c.LBIA[:], in_=c.win[f"W_a0b{proj}"])
                            P.inc("G", ins, p + f"lw{gt}_{proj}")
                        if proj != "v" and ATTN_STAGE >= 1 and "s" in ATTN_FEAT:
                            for m in range(8):
                                if (not L0) and "r" in ATTN_FEAT:
                                    _wait(g, wc, P.tok(p + f"rp{gt}_{proj}_{m}"))
                                else:
                                    _wait(g, wc, P.tok(p + f"ev{gt}_{proj}_{m}"))
                            dst = (c.q_sp if proj == "q" else c.k_sp)[s]
                            for ch in range(NCH):
                                ins = g.dma_start(
                                    out=dst[ch, :, tl * 512:(tl + 1) * 512],
                                    in_=STG[:, ch, :])
                                if ch < NCH - 1:
                                    P.inc("G", ins, p + f"sp{gt}_{proj}c{ch}")
                            P.inc("G", ins, p + f"sp{gt}_{proj}")
                    if L0 and ATTN_STAGE >= 1 and "s" in ATTN_FEAT:
                        _wait(g, wc, P.tok(p + f"xln{gt}"))
                        for ch in range(NCH):
                            ins = g.dma_start(
                                out=c.xln_spd[s][ch, :, tl * 512:(tl + 1) * 512],
                                in_=c.XLN[:, ch, :])
                            if ch < NCH - 1:
                                P.inc("G", ins, p + f"spx{gt}c{ch}")
                        P.inc("G", ins, p + f"spx{gt}")
                if L0 and ATTN_STAGE >= 3:
                    for wn in lwo_waits[p + f"lwo{s}"]:
                        _wait(g, wc, P.tok(wn))
                    ins = g.dma_start(out=c.LORA[:], in_=c.win["W_a0Ao"])
                    P.inc("G", ins, p + f"lwoA{s}")
                    ins = g.dma_start(out=c.LORB, in_=c.win["W_a0Bo"])
                    P.inc("G", ins, p + f"lwoB{s}")
                    ins = g.dma_start(out=c.LBIA[:], in_=c.win["W_a0bo"])
                    P.inc("G", ins, p + f"lwo{s}")
        blk.gpsimd(_g)

        def _sy(sy):
            wc = c.wc["sync"]
            for s in range(2):
                if not L0:
                    if s > 0 and ATTN_STAGE >= 1 and "r" in ATTN_FEAT:
                        _wait(sy, wc, P.tok(p + f"rp{(s - 1) * 4 + 3}_k_{7}"))
                    else:
                        _wait(sy, wc, prev_hfree)
                    ins = sy.dma_start(out=CS2v[:, 0, :], in_=c.cin["C_cos"])
                    P.inc("SY", ins, p + f"cs2a_{s}")
                    ins = sy.dma_start(out=CS2v[:, 1, :], in_=c.cin["C_sin"])
                    P.inc("SY", ins, p + f"cs2_{s}")
                for tl in range(4):
                    gt = s * 4 + tl
                    for proj in projs:
                        for hf in range(2):
                            lname = p + f"wl{gt}_{proj}_{hf}"
                            war = ld_war[lname]
                            _wait(sy, wc, P.tok(war) if war else prev_wbfree)
                            _wait(sy, wc, c.cast_tok[f"W_a{layer}{proj}"])
                            sl = wsc[proj][hf * 4:(hf + 1) * 4]
                            dst = c.WB[:, ld_slot[lname]].rearrange("p (m ko) -> p m ko", m=4)
                            ins = sy.dma_start(out=dst,
                                               in_=sl.rearrange("m p k o -> p m (k o)"))
                            P.inc("SY", ins, lname)
                for h in (range(NH) if ATTN_STAGE >= 2 else []):
                    if h >= 2:
                        pq = len(attn_kbs(layer, 15))
                        _wait(sy, wc, P.tok(p + f"qk{s}_{h - 2}_{15}_{pq - 1}"))
                    elif s > 0:
                        pq = len(attn_kbs(layer, 15))
                        _wait(sy, wc, P.tok(p + f"qk{s - 1}_{14 + h}_{15}_{pq - 1}"))
                    else:
                        _wait(sy, wc, prev_hfree)
                    _wait(sy, wc, P.tok(p + f"sp{s * 4 + 3}_q"))
                    ins = sy.dma_start(out=QAv[0:64, h % 2, :],
                                       in_=c.q_sp[s][h // 2, 64 * (h % 2):64 * (h % 2) + 64, :])
                    P.inc("SY", ins, p + f"hq{s}_{h}")
                    _wait(sy, wc, P.tok(p + f"sp{s * 4 + 3}_k"))
                    ins = sy.dma_start(out=KAv[0:64, h % 2, :],
                                       in_=c.k_sp[s][h // 2, 64 * (h % 2):64 * (h % 2) + 64, :])
                    P.inc("SY", ins, p + f"hk{s}_{h}")
                    if L0:
                        ins = sy.dma_start(out=KAv[64:67, h % 2, :], in_=c.cin["C_alibi"][h])
                        P.inc("SY", ins, p + f"ha{s}_{h}")
                for hf in (range(2) if ATTN_STAGE >= 3 else []):
                    lname = p + f"wo{s}_{hf}"
                    war = ld_war[lname]
                    _wait(sy, wc, P.tok(war) if war else prev_wbfree)
                    if ATTN_STAGE >= 2:
                        lkb = len(attn_kbs(layer, 15)) - 1
                        _wait(sy, wc, P.tok(p + f"pv{s}_{15}_{15}_{lkb}"))
                    _wait(sy, wc, c.cast_tok[f"W_a{layer}o"])
                    sl = wsc["o"][hf * 4:(hf + 1) * 4]
                    dst = c.WB[:, ld_slot[lname]].rearrange("p (m ko) -> p m ko", m=4)
                    ins = sy.dma_start(out=dst, in_=sl.rearrange("m p k o -> p m (k o)"))
                    P.inc("SY", ins, lname)
                if L0 and ATTN_STAGE >= 3:
                    for qi in range(16):
                        if qi > 0:
                            _wait(sy, wc, P.tok(p + f"ro{s}_{qi - 1}_{7}"))
                        else:
                            _wait(sy, wc, P.tok(p + f"spx{s * 4 + 3}"))
                            _wait(sy, wc, P.tok(p + f"g{s * 4 + 3}_v_{7}"))
                            if L0:
                                _wait(sy, wc, P.tok(p + f"u{s * 4 + 3}_v"))
                        ins = sy.dma_start(
                            out=c.XLN[:, :, 0:128],
                            in_=c.xln_spd[s][:, :, qi * 128:(qi + 1) * 128].rearrange(
                                "m q t -> q m t"))
                        P.inc("SY", ins, p + f"xr{s}_{qi}")
        blk.sync(_sy)

        def _s(sc):
            wc = c.wc["scalar"]
            for s in range(2):
                for tl in range(4):
                    gt = s * 4 + tl
                    ln_scalar(c, P, p, sc, gt,
                              prev_sq_reader=(None if gt > 0 else prev_statfree))
                    for proj in projs:
                        if L0:
                            _wait(sc, wc, P.tok(p + f"u{gt}_{proj}"))
                            ins = sc.activation(out=c.USB[:, :], in_=c.PA4[0:8, :],
                                                func=AF.Identity, scale=0.125)
                            P.inc("S", ins, p + f"ue{gt}_{proj}")
                        for m in range(8):
                            _wait(sc, wc, P.tok(p + f"g{gt}_{proj}_{m}"))
                            if ATTN_STAGE >= 1 and "s" in ATTN_FEAT:
                                if proj == "k":
                                    _wait(sc, wc, P.tok(p + f"sp{gt}_q"))
                                elif proj == "v":
                                    _wait(sc, wc, P.tok(p + f"sp{gt}_k"))
                            pb = c.PB[proj_bank(gt, proj, m)]
                            if L0:
                                _wait(sc, wc, P.tok(p + f"lw{gt}_{proj}"))
                                ins = sc.activation(out=STG[:, m, :], in_=pb[:],
                                                    func=AF.Identity,
                                                    bias=c.LBIA[:, m:m + 1])
                            else:
                                ins = sc.activation(out=STG[:, m, :], in_=pb[:],
                                                    func=AF.Identity)
                            P.inc("S", ins, p + f"ev{gt}_{proj}_{m}")
                            if proj != "v" and ATTN_STAGE >= 1 and "n" in ATTN_FEAT:
                                if (not L0) and "r" in ATTN_FEAT:
                                    _wait(sc, wc, P.tok(p + f"rp{gt}_{proj}_{m}"))
                                if m > 0:
                                    _wait(sc, wc, P.tok(p + f"nmm{gt}_{proj}_{m - 1}"))
                                ins = sc.activation(out=OSB[:, 1024:1536],
                                                    in_=STG[:, m, :], func=AF.Square)
                                P.inc("S", ins, p + f"nsq{gt}_{proj}_{m}")
                for h in (range(NH) if ATTN_STAGE >= 2 else []):
                    for qi in range(16):
                        kbs = attn_kbs(layer, qi)
                        for kb, kcs in enumerate(kbs):
                            nm = p + f"ex{s}_{h}_{qi}_{kb}"
                            Lkb = len(kcs) * 128
                            _wait(sc, wc, P.tok(p + f"nbe{s}"))
                            if (not L0) and (qi in kcs):
                                _wait(sc, wc, P.tok(p + f"dg{s}_{h}_{qi}"))
                            else:
                                _wait(sc, wc, P.tok(p + f"qk{s}_{h}_{qi}_{kb}"))
                            if psc_war[nm] is None:
                                _wait(sc, wc, prev_wbfree)
                            _wait(sc, wc, P.tok(p + f"g{s * 4 + 3}_v_{7}"))
                            _wait(sc, wc, tokm(psc_war[nm]))
                            _wait(sc, wc, tokm(acc_war[nm]))
                            ins = sc.activation(out=PSC[:, kb * 512:kb * 512 + Lkb],
                                                in_=c.PB[kb % 2][:, 0:Lkb], func=AF.Exp,
                                                scale=0.125, bias=c.ABIA[:, h:h + 1],
                                                accum_out=c.ACC4[:, kb:kb + 1])
                            P.inc("S", ins, nm)
                if L0 and ATTN_STAGE >= 3:
                    for qi in range(16):
                        _wait(sc, wc, P.tok(p + f"uo{s}_{qi}"))
                        ins = sc.activation(out=c.USB[:, 0:128], in_=c.PA4[0:8, 0:128],
                                            func=AF.Identity, scale=0.125)
                        P.inc("S", ins, p + f"uoe{s}_{qi}")
        blk.scalar(_s)

        def _v(v):
            wc = c.wc["vector"]
            if L0:
                _wait(v, wc, P.tok(p + "rez_mm"))
                ins = v.tensor_copy(out=c.REZ[:], in_=c.PA4[:, 0:1])
                P.inc("V", ins, p + "rez_ev")
                _wait(v, wc, prev_hfree)
                for b in range(2):
                    ins = v.memset(QAv[64:67, b, :], 1.0)
                    P.inc("V", ins, p + f"ones{b}")
            for s in range(2):
                for tl in range(4):
                    gt = s * 4 + tl
                    if gt == 0:
                        xw = [prev_xfree]
                    else:
                        xw = [P.tok(p + f"g{gt - 1}_v_{7}")]
                        if ATTN_STAGE >= 1 and "s" in ATTN_FEAT:
                            xw.append(P.tok(p + f"spx{gt - 1}") if L0
                                      else P.tok(p + f"sp{gt - 1}_k"))
                        if L0:
                            xw.append(P.tok(p + f"u{gt - 1}_v"))
                    for w in xw:
                        _wait(v, wc, w)
                    ln_vector(c, P, p, v, gt, xln_war=None)
                    for proj in (projs if ATTN_STAGE >= 1 else []):
                        for m in range(8):
                            if (not L0) and proj != "v" and "r" in ATTN_FEAT:
                                _wait(v, wc, P.tok(p + f"sw{gt}_{proj}_{m}"))
                                _wait(v, wc, P.tok(p + f"ev{gt}_{proj}_{m}"))
                                tloc = tl
                                cosl = CS2v[:, 0, tloc * 512:(tloc + 1) * 512]
                                sinl = CS2v[:, 1, tloc * 512:(tloc + 1) * 512]
                                v.tensor_tensor(out=OSB[:, 0:512], in0=STG[:, m, :],
                                                in1=cosl, op=ALU.mult)
                                v.tensor_tensor(out=OSB[:, 512:1024], in0=c.PA5[:],
                                                in1=sinl, op=ALU.mult)
                                ins = v.tensor_tensor(out=STG[:, m, :], in0=OSB[:, 0:512],
                                                      in1=OSB[:, 512:1024], op=ALU.add)
                                P.inc("V", ins, p + f"rp{gt}_{proj}_{m}")
                            if proj != "v":
                                if "n" not in ATTN_FEAT:
                                    continue
                                _wait(v, wc, P.tok(p + f"nmm{gt}_{proj}_{m}"))
                                col = projs.index(proj) * 32 + m * 4 + tl
                                ins = v.reduce_max(out=c.NRM2[0:2, col:col + 1],
                                                   in_=c.PO3[0:2, :], axis=AX.X)
                                P.inc("V", ins, p + f"nrd{gt}_{proj}_{m}")
                            else:
                                if "t" not in ATTN_FEAT:
                                    continue
                                _wait(v, wc, P.tok(p + f"vt{gt}_{m}"))
                                dstv = VT[:, 2 * m:2 * m + 2, tl * 4:(tl + 1) * 4, :]
                                ins = v.tensor_copy(
                                    out=dstv,
                                    in_=c.PT2[:, (m % 2) * 512:(m % 2) * 512 + 512]
                                    .rearrange("p (k h d) -> p h k d", k=4, h=2))
                                P.inc("V", ins, p + f"vte{gt}_{m}")
                if ATTN_STAGE < 1 or "b" not in ATTN_FEAT:
                    continue
                for proj in ["q", "k"]:
                    for m in range(8):
                        _wait(v, wc, P.tok(p + f"nrd{s * 4 + 3}_{proj}_{m}"))
                ins = v.engine_nop()
                _selfsync(c, v, ins)
                for j in range(16):
                    pi, m = divmod(j, 8)
                    ins = v.reduce_max(out=c.QM[0:2, j:j + 1],
                                       in_=c.NRM2[0:2, pi * 32 + m * 4:pi * 32 + m * 4 + 4],
                                       axis=AX.X)
                _selfsync(c, v, ins)
                v.tensor_tensor(out=c.BHE[0:2, 0:8], in0=c.QM[0:2, 0:8],
                                in1=c.QM[0:2, 8:16], op=ALU.add)
                ins = v.tensor_scalar(out=c.BHE[0:2, 0:8], in0=c.BHE[0:2, 0:8],
                                      scalar1=-1.0 / 16.0, scalar2=None, op0=ALU.mult)
                _selfsync(c, v, ins)
                ins = v.tensor_copy(out=c.QM[0:2, 0:16].rearrange("p (a b) -> p a b", b=2),
                                    in_=c.BHE[0:2, 0:8].unsqueeze(2).broadcast_to([2, 8, 2]))
                _selfsync(c, v, ins)
                if L0:
                    ins = v.tensor_tensor(out=c.QM[0:2, 0:16], in0=c.QM[0:2, 0:16],
                                          in1=c.M2[0:2, :], op=ALU.mult)
                    _selfsync(c, v, ins)
                    ins = v.tensor_tensor(out=c.QM[0:2, 0:16], in0=c.QM[0:2, 0:16],
                                          in1=c.AB2[0:2, :], op=ALU.add)
                else:
                    ins = v.tensor_tensor(out=c.QM[0:2, 0:16], in0=c.QM[0:2, 0:16],
                                          in1=c.M2[0:2, :], op=ALU.mult)
                P.inc("V", ins, p + f"nb{s}")
                _wait(v, wc, P.tok(p + f"nbm{s}"))
                ins = v.tensor_copy(out=c.ABIA[:], in_=c.PA4[:, 0:16])
                P.inc("V", ins, p + f"nbe{s}")
                for h in (range(NH) if ATTN_STAGE >= 2 else []):
                    for qi in range(16):
                        kbs = attn_kbs(layer, qi)
                        for kb, kcs in enumerate(kbs):
                            if (not L0) and kb == len(kbs) - 1:
                                dkb = qi // 4
                                dcol = (qi - 4 * dkb) * 128
                                _wait(v, wc, P.tok(p + f"qk{s}_{h}_{qi}_{dkb}"))
                                ins = v.tensor_tensor(out=c.PB[dkb % 2][:, dcol:dcol + 128],
                                                      in0=c.PB[dkb % 2][:, dcol:dcol + 128],
                                                      in1=c.MASKC[:], op=ALU.add)
                                P.inc("V", ins, p + f"dg{s}_{h}_{qi}")
                            pte = p + f"pte{s}_{h}_{qi}_{kb}"
                            Lkb = len(kcs) * 128
                            _wait(v, wc, P.tok(p + f"pt{s}_{h}_{qi}_{kb}"))
                            _wait(v, wc, tokm(pts_war[pte]))
                            ins = v.tensor_copy(
                                out=PTS[:, kb * 512:kb * 512 + Lkb],
                                in_=c.PT2[:, (kb % 2) * 512:(kb % 2) * 512 + Lkb])
                            P.inc("V", ins, pte)
                        _wait(v, wc, P.tok(p + f"pv{s}_{h}_{qi}_{len(kbs) - 1}"))
                        if h == 0 and qi == 0 and s == 0:
                            _wait(v, wc, prev_hfree)
                        ins = v.reduce_sum(out=c.RD[:, 0:1], in_=c.ACC4[:, 0:len(kbs)],
                                           axis=AX.X)
                        _selfsync(c, v, ins)
                        ins = v.reciprocal(out=c.RD[:, 1:2], in_=c.RD[:, 0:1])
                        _selfsync(c, v, ins)
                        ins = v.tensor_scalar(out=ATT[:, qi, h * 64:(h + 1) * 64],
                                              in0=c.PO3[:, 0:64], scalar1=c.RD[:, 1:2],
                                              scalar2=None, op0=ALU.mult)
                        P.inc("V", ins, p + f"o{s}_{h}_{qi}")
                for qi in (range(16) if ATTN_STAGE >= 3 else []):
                    _wait(v, wc, P.tok(p + f"at{s}_{qi}"))
                    if qi >= 1:
                        _wait(v, wc, P.tok(p + f"go{s}_{qi - 1}_{7}"))
                    elif s == 0:
                        _wait(v, wc, prev_poolfree)
                    ins = v.tensor_copy(out=c.AF_[:], in_=c.PT2[:].rearrange(
                        "p (a b) -> p a b", a=8))
                    P.inc("V", ins, p + f"ate{s}_{qi}")
                    goff = s * SEQ + qi * 128
                    for m in range(8):
                        _wait(v, wc, P.tok(p + f"go{s}_{qi}_{m}"))
                        _wait(v, wc, rfree0[goff // 512])
                        pb = c.PB[(qi * 8 + m) % 2]
                        if L0:
                            _wait(v, wc, P.tok(p + f"xr{s}_{qi}"))
                            v.tensor_tensor(out=c.R[:, m, goff:goff + 128],
                                            in0=c.R[:, m, goff:goff + 128],
                                            in1=c.XLN[:, m, 0:128], op=ALU.add)
                            ins = v.scalar_tensor_tensor(
                                out=c.R[:, m, goff:goff + 128], in0=pb[:, 0:128],
                                scalar=c.REZ[:, 0:1], in1=c.R[:, m, goff:goff + 128],
                                op0=ALU.mult, op1=ALU.add)
                        else:
                            ins = v.tensor_tensor(out=c.R[:, m, goff:goff + 128],
                                                  in0=c.R[:, m, goff:goff + 128],
                                                  in1=pb[:, 0:128], op=ALU.add)
                        P.inc("V", ins, p + f"ro{s}_{qi}_{m}")
        blk.vector(_v)

        def _t(t):
            wc = c.wc["tensor"]
            if L0:
                _wait(t, wc, P.tok(p + "rez_ld"))
                _wait(t, wc, prev_pafree)
                ins = t.matmul(out=c.PA4[:, 0:1], lhsT=c.ONES1F[:], rhs=c.REZI[:],
                               start=True, stop=True)
                P.inc("T", ins, p + "rez_mm")
            for s in range(2):
                for tl in range(4):
                    gt = s * 4 + tl
                    aux = [P.tok(p + "rez_ev")] if (L0 and gt == 0) else []
                    if gt > 0:
                        if L0:
                            aux.append(P.tok(p + f"ue{gt - 1}_v"))
                        else:
                            aux.append(P.tok(p + f"xln{gt - 1}"))
                        if (not L0) and ATTN_STAGE >= 1 and "r" in ATTN_FEAT:
                            aux.append(P.tok(p + f"rp{gt - 1}_k_{7}"))
                    elif s == 0:
                        aux.append(prev_pafree)
                        aux.append(prev_statfree)
                    else:
                        pass
                    if s > 0 and tl == 0:
                        if L0 and ATTN_STAGE >= 3:
                            aux.append(P.tok(p + f"uoe{s - 1}_{15}"))
                        if ATTN_STAGE >= 1 and "b" in ATTN_FEAT:
                            aux.append(P.tok(p + f"nbe{s - 1}"))
                    for a in aux:
                        _wait(t, wc, a)
                    ln_tensor(c, P, p, t, gt, aux_war=None)
                    ln_tensor_bcast(c, P, p, t, gt)
                    for proj in projs:
                        if L0:
                            _wait(t, wc, P.tok(p + f"lw{gt}_{proj}"))
                            _wait(t, wc, P.tok(p + f"xln{gt}"))
                            if projs.index(proj) > 0:
                                _wait(t, wc, P.tok(p + f"ue{gt}_{projs[projs.index(proj) - 1]}"))
                            for k in range(8):
                                ins = t.matmul(out=c.PA4[0:8, :], lhsT=c.LORA[:, k, :],
                                               rhs=c.XLN[:, k, :], start=(k == 0),
                                               stop=(k == 7))
                            P.inc("T", ins, p + f"u{gt}_{proj}")
                        for m in range(8):
                            gname = p + f"g{gt}_{proj}_{m}"
                            hf = m // 4
                            lname = p + f"wl{gt}_{proj}_{hf}"
                            _wait(t, wc, P.tok(lname))
                            _wait(t, wc, P.tok(p + f"xln{gt}"))
                            _wait(t, wc, pbwar_tok(gname, proj_bank(gt, proj, m)))
                            if L0:
                                _wait(t, wc, P.tok(p + f"ue{gt}_{proj}"))
                            lhs = wl_lhsT_mk(c, ld_slot[lname], m)
                            pb = c.PB[proj_bank(gt, proj, m)]
                            for k in range(8):
                                ins = t.matmul(out=pb[:], lhsT=lhs[k], rhs=c.XLN[:, k, :],
                                               start=(k == 0), stop=(False if L0 else k == 7))
                            if L0:
                                ins = t.matmul(out=pb[:], lhsT=c.LORB[:, m, :],
                                               rhs=c.USB[:, :], start=False, stop=True)
                            P.inc("T", ins, gname)
                            if ATTN_STAGE < 1:
                                continue
                            if (not L0) and proj != "v" and "r" in ATTN_FEAT:
                                _wait(t, wc, P.tok(p + f"ev{gt}_{proj}_{m}"))
                                _wait(t, wc, P.tok(p + f"cs2_{s}"))
                                if m > 0:
                                    _wait(t, wc, P.tok(p + f"rp{gt}_{proj}_{m - 1}"))
                                elif not (gt == 0 and proj == "q"):
                                    pp = "q" if proj == "k" else "k"
                                    pgt = gt if proj == "k" else gt - 1
                                    _wait(t, wc, P.tok(p + f"rp{pgt}_{pp}_{7}"))
                                ins = t.matmul(out=c.PA5[:], lhsT=c.SWT[:],
                                               rhs=STG[:, m, :], start=True, stop=True)
                                P.inc("T", ins, p + f"sw{gt}_{proj}_{m}")
                            if proj != "v":
                                if "n" not in ATTN_FEAT:
                                    continue
                                _wait(t, wc, P.tok(p + f"nsq{gt}_{proj}_{m}"))
                                if m > 0:
                                    _wait(t, wc, P.tok(p + f"nrd{gt}_{proj}_{m - 1}"))
                                else:
                                    prevnrd = None
                                    if proj == "k":
                                        prevnrd = p + f"nrd{gt}_q_{7}"
                                    elif gt > 0:
                                        prevnrd = p + f"nrd{gt - 1}_k_{7}"
                                    elif s > 0:
                                        prevnrd = None
                                    if prevnrd:
                                        _wait(t, wc, P.tok(prevnrd))
                                    elif s > 0:
                                        _wait(t, wc, P.tok(p + f"nb{s - 1}"))
                                    else:
                                        _wait(t, wc, prev_pofree)
                                ins = t.matmul(out=c.PO3[0:2, :], lhsT=c.H2[:],
                                               rhs=OSB[:, 1024:1536], start=True, stop=True)
                                P.inc("T", ins, p + f"nmm{gt}_{proj}_{m}")
                            else:
                                if "t" not in ATTN_FEAT:
                                    continue
                                _wait(t, wc, P.tok(p + f"ev{gt}_v_{m}"))
                                if m >= 1:
                                    _wait(t, wc, P.tok(p + f"vte{gt}_{m - 1}"))
                                elif gt > 0:
                                    _wait(t, wc, P.tok(p + f"vte{gt - 1}_{7}"))
                                elif s == 0:
                                    _wait(t, wc, prev_ptfree)
                                for kc in range(4):
                                    ins = t.transpose(
                                        out=c.PT2[:, (m % 2) * 512 + kc * 128:
                                                  (m % 2) * 512 + (kc + 1) * 128],
                                        in_=STG[:, m, kc * 128:(kc + 1) * 128],
                                        identity=c.IDENT[:])
                                P.inc("T", ins, p + f"vt{gt}_{m}")
                if ATTN_STAGE < 1 or "b" not in ATTN_FEAT:
                    continue
                _wait(t, wc, P.tok(p + f"nb{s}"))
                if L0:
                    _wait(t, wc, P.tok(p + f"ue{s * 4 + 3}_v"))
                ins = t.matmul(out=c.PA4[:, 0:16], lhsT=c.ONES2F[:], rhs=c.QM[0:2, 0:16],
                               start=True, stop=True)
                P.inc("T", ins, p + f"nbm{s}")
                for h in (range(NH) if ATTN_STAGE >= 2 else []):
                    for qi in range(16):
                        kbs = attn_kbs(layer, qi)
                        for kb, kcs in enumerate(kbs):
                            Lkb = len(kcs) * 128
                            qkn = p + f"qk{s}_{h}_{qi}_{kb}"
                            _wait(t, wc, P.tok(p + f"hq{s}_{h}"))
                            _wait(t, wc, P.tok(p + f"hk{s}_{h}"))
                            if L0:
                                _wait(t, wc, P.tok(p + f"ha{s}_{h}"))
                                _wait(t, wc, P.tok(p + "ones0"))
                                _wait(t, wc, P.tok(p + "ones1"))
                            _wait(t, wc, pbwar_tok(qkn, kb % 2))
                            ins = t.matmul(out=c.PB[kb % 2][:, 0:Lkb],
                                           lhsT=QAv[0:KR, h % 2, qi * 128:(qi + 1) * 128],
                                           rhs=KAv[0:KR, h % 2,
                                                   kcs[0] * 128:kcs[0] * 128 + Lkb],
                                           start=True, stop=True)
                            P.inc("T", ins, qkn)
                            ptn = p + f"pt{s}_{h}_{qi}_{kb}"
                            _wait(t, wc, P.tok(p + f"ex{s}_{h}_{qi}_{kb}"))
                            w_ = ptc_war[ptn]
                            _wait(t, wc, P.tok(p + f"vte{s * 4 + 3}_{7}"))
                            if w_ is None:
                                _wait(t, wc, prev_ptfree)
                            else:
                                _wait(t, wc, P.tok(w_))
                            for j in range(len(kcs)):
                                ins = t.transpose(
                                    out=c.PT2[:, (kb % 2) * 512 + j * 128:
                                              (kb % 2) * 512 + j * 128 + 128],
                                    in_=PSC[:, kb * 512 + j * 128:kb * 512 + j * 128 + 128],
                                    identity=c.IDENT[:])
                            P.inc("T", ins, ptn)
                            _wait(t, wc, P.tok(p + f"pte{s}_{h}_{qi}_{kb}"))
                            if kb == 0:
                                _wait(t, wc, P.tok(p + f"vte{s * 4 + 3}_{7}"))
                                _wait(t, wc, P.tok(p + f"nrd{s * 4 + 3}_k_{7}"))
                                if qi > 0:
                                    _wait(t, wc, P.tok(p + f"o{s}_{h}_{qi - 1}"))
                                elif h > 0:
                                    _wait(t, wc, P.tok(p + f"o{s}_{h - 1}_{15}"))
                                elif s > 0:
                                    _wait(t, wc, P.tok(p + f"o{s - 1}_{15}_{15}"))
                                else:
                                    _wait(t, wc, prev_pofree)
                            for j, kc in enumerate(kcs):
                                ins = t.matmul(out=c.PO3[:, 0:64],
                                               lhsT=PTS[:, (kb * 4 + j) * 128:
                                                        (kb * 4 + j) * 128 + 128],
                                               rhs=VT[:, h, kc, :],
                                               start=(kb == 0 and j == 0),
                                               stop=(kb == len(kbs) - 1 and
                                                     j == len(kcs) - 1))
                            P.inc("T", ins, p + f"pv{s}_{h}_{qi}_{kb}")
                for qi in (range(16) if ATTN_STAGE >= 3 else []):
                    atn = p + f"at{s}_{qi}"
                    _wait(t, wc, P.tok(p + f"o{s}_{15}_{qi}"))
                    if qi > 0:
                        _wait(t, wc, P.tok(p + f"ate{s}_{qi - 1}"))
                    else:
                        lkb = len(attn_kbs(layer, 15)) - 1
                        _wait(t, wc, P.tok(p + f"pte{s}_{15}_{15}_{lkb}"))
                    for m in range(8):
                        ins = t.transpose(out=c.PT2[:, m * 128:(m + 1) * 128],
                                          in_=ATT[:, qi, m * 128:(m + 1) * 128],
                                          identity=c.IDENT[:])
                    P.inc("T", ins, atn)
                    if L0:
                        _wait(t, wc, P.tok(p + f"lwo{s}"))
                        _wait(t, wc, P.tok(p + f"ate{s}_{qi}"))
                        if qi > 0:
                            _wait(t, wc, P.tok(p + f"uoe{s}_{qi - 1}"))
                        else:
                            _wait(t, wc, P.tok(p + f"nbe{s}"))
                        for k in range(8):
                            ins = t.matmul(out=c.PA4[0:8, 0:128], lhsT=c.LORA[:, k, :],
                                           rhs=c.AF_[:, k, :], start=(k == 0), stop=(k == 7))
                        P.inc("T", ins, p + f"uo{s}_{qi}")
                    for m in range(8):
                        gname = p + f"go{s}_{qi}_{m}"
                        hf = m // 4
                        lname = p + f"wo{s}_{hf}"
                        _wait(t, wc, P.tok(lname))
                        _wait(t, wc, P.tok(p + f"ate{s}_{qi}"))
                        _wait(t, wc, pbwar_tok(gname, (qi * 8 + m) % 2))
                        if L0:
                            _wait(t, wc, P.tok(p + f"uoe{s}_{qi}"))
                        lhs = wl_lhsT_mk(c, ld_slot[lname], m)
                        pb = c.PB[(qi * 8 + m) % 2]
                        for k in range(8):
                            ins = t.matmul(out=pb[:, 0:128], lhsT=lhs[k],
                                           rhs=c.AF_[:, k, :], start=(k == 0),
                                           stop=(False if L0 else k == 7))
                        if L0:
                            ins = t.matmul(out=pb[:, 0:128], lhsT=c.LORB[:, m, :],
                                           rhs=c.USB[:, 0:128], start=False, stop=True)
                        P.inc("T", ins, gname)
        blk.tensor(_t)

    P.done()
    if ATTN_STAGE >= 3:
        last = P.tok(p + f"ro{1}_{15}_{7}")
    elif ATTN_STAGE == 2:
        last = P.tok(p + f"o{1}_{15}_{15}")
    elif ATTN_STAGE == 1:
        last = P.tok(p + f"nbe{1}") if "b" in ATTN_FEAT else P.tok(p + f"ev{7}_v_{7}")
    else:
        last = P.tok(p + f"ev{7}_v_{7}")
    for tt in range(NT):
        c.rtok[tt] = last
        c.rfree[tt] = last
    c.xln_free = last
    c.hbuf_free = last
    c.wb_free = last
    c.stat_free = P.tok(p + f"xln{7}")
    for b in range(2):
        c.pb_free[b] = last
    c.pa_free = last
    c.pt_free = last
    c.po_free = last
    c.pool_free = last
    c.lora_free = last if L0 else getattr(c, "lora_free", None)


def wl_lhsT_mk(c, slot, m):
    off = (m % 4) * 1024
    W = c.WB[:, slot]
    return [W[:, off + k * 128: off + (k + 1) * 128] for k in range(8)]




def run_group(group, x_cur, wd):
    nc = build_program(group)
    need = weights_needed(group)
    in_maps = []
    for ci in range(NCORES):
        m = {"x": x_cur[ci]}
        for n in WSHAPES:
            if n in need:
                m[n] = wd[n]
        for n in CSHAPES:
            m[n] = wd[n]
        in_maps.append(m)
    res = run_bass_kernel_spmd(nc, in_maps, core_ids=list(range(NCORES)))
    return [np.asarray(res.results[ci]["y"]) for ci in range(NCORES)]


def kernel(**inputs):
    wd = host_prepare(inputs)
    xs = shard_x(np.asarray(inputs["L0_attn_x"], np.float32))
    groups = [[sl] for sl in SUBLAYERS]
    x_cur = xs
    for group in groups:
        x_cur = run_group(group, x_cur, wd)
    return unshard_y(x_cur)


# version 03
# speedup vs baseline: 1.0758x; 1.0758x over previous
import numpy as np
import ml_dtypes
import concourse.bass as bass
import concourse.mybir as mybir
from concourse.bass_utils import run_bass_kernel_spmd

F32 = mybir.dt.float32
BF16 = mybir.dt.bfloat16
U8 = mybir.dt.uint8
AF = mybir.ActivationFunctionType
ALU = mybir.AluOpType
AX = mybir.AxisListType
BF = ml_dtypes.bfloat16

NCORES = 8
SEQ = 2048
DM = 1024
NC_TOK = 2 * SEQ
NT = NC_TOK // 512
NCH = DM // 128
NH = 16
HD = 64
EPS = 1e-5

SUBLAYERS = ["moe0a", "attn0", "moe0b", "ffn1a", "attn1", "ffn1b"]
ATTN_STAGE = 3
ATTN_FEAT = "rntsb"



def tile_lhsT(wT):
    Kt, Mt = wT.shape
    a = wT.reshape(Kt // 128, 128, Mt // 128, 128)
    return np.ascontiguousarray(a.transpose(2, 1, 0, 3))


def host_prepare(g):
    d = {}
    ew = np.asarray(g["L0_ffn_expert_w"], np.float32)
    t = np.stack([tile_lhsT(ew[e]) for e in range(8)])
    d["W_exp"] = np.ascontiguousarray(t.transpose(0, 2, 1, 3, 4))
    d["W_expb"] = np.ascontiguousarray(
        np.asarray(g["L0_ffn_expert_b"], np.float32).reshape(8, 8, 128).transpose(1, 0, 2))
    d["W_rw"] = np.ascontiguousarray(
        np.asarray(g["L0_ffn_router_w"], np.float32).reshape(8, 128, 8).transpose(1, 0, 2))
    d["W_rb"] = np.asarray(g["L0_ffn_router_b"], np.float32).reshape(1, 8)
    for n in ["q", "k", "v", "o"]:
        W = np.asarray(g[f"L0_attn_W{n}"], np.float32)
        d[f"W_a0{n}"] = tile_lhsT(np.ascontiguousarray(W.T))
        A = np.asarray(g[f"L0_attn_A{n}"], np.float32)
        d[f"W_a0A{n}"] = np.ascontiguousarray(A.T.reshape(8, 128, 8).transpose(1, 0, 2))
        Bm = np.asarray(g[f"L0_attn_B{n}"], np.float32)
        d[f"W_a0B{n}"] = np.ascontiguousarray(Bm.T.reshape(8, 8, 128))
        b = np.asarray(g[f"L0_attn_b{n}"], np.float32)
        d[f"W_a0b{n}"] = np.ascontiguousarray(b.reshape(8, 128).T)
        W1 = np.asarray(g[f"L1_attn_w{n}"], np.float32)
        d[f"W_a1{n}"] = tile_lhsT(W1)
    d["W_rez"] = np.asarray(g["L0_attn_rezero"], np.float32).reshape(1, 1)
    for n in ["w1", "w2", "h1", "h2", "w3"]:
        W = np.asarray(g[f"L1_ffn_{n}_w"], np.float32)
        d[f"W_f{n}"] = tile_lhsT(np.ascontiguousarray(W.T))
        s = np.asarray(g[f"L1_ffn_{n}_s"], np.float32)
        d[f"W_f{n}s"] = np.ascontiguousarray(s.reshape(-1, 128).T)
    d["C_ones"] = np.ones((128, 1), BF)
    d["C_ones1"] = np.ones((1, 128), BF)
    d["C_ones1f"] = np.ones((1, 128), np.float32)
    d["C_ident"] = np.eye(128, dtype=BF)
    inv = 1.0 / (10000.0 ** (np.arange(0, HD, 2, dtype=np.float64) / HD))
    ang = np.arange(SEQ, dtype=np.float64)[:, None] * inv[None, :]
    c64 = np.repeat(np.cos(ang).T, 2, axis=0)
    s64 = np.repeat(np.sin(ang).T, 2, axis=0)
    d["C_cos"] = np.concatenate([c64, c64], 0).astype(BF)
    d["C_sin"] = np.concatenate([s64, s64], 0).astype(BF)
    S = np.zeros((HD, HD), np.float32)
    for i in range(0, HD, 2):
        S[i, i + 1] = -1.0
        S[i + 1, i] = 1.0
    S128 = np.zeros((128, 128), np.float32)
    S128[:64, :64] = S
    S128[64:, 64:] = S
    d["C_swapT"] = np.ascontiguousarray(S128.T).astype(BF)
    slopes = 0.5 ** np.arange(NH)
    al = np.zeros((NH, 3, SEQ), np.float64)
    kpos = np.arange(SEQ, dtype=np.float64)
    for h in range(NH):
        v = np.asarray(8.0 * slopes[h] * kpos, np.float32).astype(np.float64)
        hi = np.asarray(v, np.float32).astype(BF).astype(np.float64)
        mid = np.asarray(v - hi, np.float32).astype(BF).astype(np.float64)
        al[h, 0], al[h, 1], al[h, 2] = hi, mid, v - hi - mid
    d["C_alibi"] = al.astype(BF)
    m = np.zeros((128, 128), np.float32)
    m[np.triu_indices(128, 1)] = -30000.0
    d["C_mask"] = m.astype(BF)
    d["C_identf"] = np.eye(128, dtype=np.float32)
    ones2 = np.ones((2, 128), np.float32)
    d["C_ones2f"] = ones2
    h2 = np.zeros((128, 2), np.float32)
    h2[0:64, 0] = 1.0
    h2[64:128, 1] = 1.0
    d["C_h2"] = h2.astype(BF)
    ab2 = np.zeros((2, 16), np.float32)
    for h in range(16):
        ab2[h % 2, h] = -(0.5 ** h) * 2047.0 * 8.0 * 0.125
    d["C_ab2"] = ab2
    m2 = np.zeros((2, 16), np.float32)
    for h in range(16):
        m2[h % 2, h] = 1.0
    d["C_m2"] = m2
    sel = np.zeros((8, 8, 128), np.float32)
    for e in range(8):
        sel[e, e, :] = 1.0
    d["C_sel"] = np.ascontiguousarray(sel.transpose(1, 0, 2)).astype(BF)
    return d


def shard_x(x):
    return [np.ascontiguousarray(np.asarray(x[2 * c:2 * c + 2], np.float32).transpose(0, 2, 1))
            for c in range(NCORES)]


def unshard_y(parts):
    out = np.empty((16, SEQ, DM), np.float32)
    for c, p in enumerate(parts):
        out[2 * c:2 * c + 2] = np.asarray(p).transpose(0, 2, 1)
    return out



class Sem:
    def __init__(self, h, step=1):
        self.h = h
        self.n = 0
        self.step = step

    def inc(self, instr):
        instr.then_inc(self.h, self.step)
        self.n += self.step
        return (self.h, self.n)

    def tok(self):
        return (self.h, self.n)


class Plan:
    def __init__(self, engs):
        self.sems = dict(engs)
        self.base = {e: s.n for e, s in self.sems.items()}
        self.order = {e: [] for e in self.sems}
        self.idx = {}
        self.emitted = {e: 0 for e in self.sems}

    def add(self, eng, name):
        assert name not in self.idx, name
        self.idx[name] = (eng, len(self.order[eng]))
        self.order[eng].append(name)

    def tok(self, name):
        eng, i = self.idx[name]
        s = self.sems[eng]
        return (s.h, self.base[eng] + (i + 1) * s.step)

    def inc(self, eng, instr, name):
        s = self.sems[eng]
        i = self.emitted[eng]
        assert self.order[eng][i] == name, \
            f"plan order mismatch on {eng}: expected {self.order[eng][i]}, got {name}"
        self.emitted[eng] += 1
        t = s.inc(instr)
        assert t == self.tok(name)
        return t

    def done(self):
        for e in self.sems:
            assert self.emitted[e] == len(self.order[e]), \
                f"engine {e}: emitted {self.emitted[e]} of {len(self.order[e])}"


class Ctx:
    pass


def _selfsync(c, eng, instr):
    t = c.X.inc(instr)
    eng.wait_ge(t[0], t[1])


def _wait(eng, cache, tok):
    if tok is None:
        return
    h, v = tok
    if v <= 0:
        return
    if cache.get(h, 0) >= v:
        return
    eng.wait_ge(h, v)
    cache[h] = v


def weights_needed(sublayers):
    need = set()
    for sl in sublayers:
        if sl.startswith("moe"):
            need |= {"W_exp", "W_expb", "W_rw", "W_rb"}
        if sl == "attn0":
            for n in "qkvo":
                need |= {f"W_a0{n}", f"W_a0A{n}", f"W_a0B{n}", f"W_a0b{n}"}
            need.add("W_rez")
        if sl == "attn1":
            need |= {f"W_a1{n}" for n in "qkvo"}
        if sl.startswith("ffn1"):
            need |= {"W_fw1", "W_fw2", "W_fh1", "W_fh2", "W_fw3",
                     "W_fw1s", "W_fw2s", "W_fh1s", "W_fh2s", "W_fw3s"}
    return need


WSHAPES = {
    "W_exp": [8, 128, 8, 8, 128], "W_expb": [8, 8, 128], "W_rw": [128, 8, 8],
    "W_rb": [1, 8], "W_rez": [1, 1],
    "W_fw1": [32, 128, 8, 128], "W_fw2": [32, 128, 8, 128],
    "W_fh1": [32, 128, 32, 128], "W_fh2": [32, 128, 32, 128], "W_fw3": [8, 128, 32, 128],
    "W_fw1s": [128, 32], "W_fw2s": [128, 32], "W_fh1s": [128, 32],
    "W_fh2s": [128, 32], "W_fw3s": [128, 8],
}
for _n in "qkvo":
    WSHAPES[f"W_a0{_n}"] = [8, 128, 8, 128]
    WSHAPES[f"W_a0A{_n}"] = [128, 8, 8]
    WSHAPES[f"W_a0B{_n}"] = [8, 8, 128]
    WSHAPES[f"W_a0b{_n}"] = [128, 8]
    WSHAPES[f"W_a1{_n}"] = [8, 128, 8, 128]
CSHAPES = {"C_ones": ([128, 1], BF16), "C_ones1": ([1, 128], BF16),
           "C_ident": ([128, 128], BF16), "C_cos": ([128, SEQ], BF16),
           "C_sin": ([128, SEQ], BF16), "C_swapT": ([128, 128], BF16),
           "C_alibi": ([NH, 3, SEQ], BF16), "C_mask": ([128, 128], BF16),
           "C_sel": ([8, 8, 128], BF16), "C_ones1f": ([1, 128], F32),
           "C_identf": ([128, 128], F32), "C_ones2f": ([2, 128], F32),
           "C_h2": ([128, 2], BF16), "C_ab2": ([2, 16], F32), "C_m2": ([2, 16], F32)}
CAST_W = ["W_exp", "W_fw1", "W_fw2", "W_fh1", "W_fh2", "W_fw3",
          "W_a0q", "W_a0k", "W_a0v", "W_a0o", "W_a1q", "W_a1k", "W_a1v", "W_a1o"]


def build_program(sublayers):
    nc = bass.Bass("TRN2", target_bir_lowering=False, dynamic_dma_scratch_size=16384)
    c = Ctx()
    c.nc = nc
    need = weights_needed(sublayers)
    c.need = need

    c.x_in = nc.dram_tensor("x", [2, DM, SEQ], F32, kind="ExternalInput").ap()
    c.y_out = nc.dram_tensor("y", [2, DM, SEQ], F32, kind="ExternalOutput").ap()
    c.win = {n: nc.dram_tensor(n, s, F32, kind="ExternalInput").ap()
             for n, s in WSHAPES.items() if n in need}
    c.cin = {n: nc.dram_tensor(n, s, dt, kind="ExternalInput").ap()
             for n, (s, dt) in CSHAPES.items()}
    c.wsc = {n: nc.dram_tensor(n + "_b", WSHAPES[n], BF16).ap()
             for n in CAST_W if n in need}
    c.q_sp = nc.dram_tensor("q_sp", [2, NCH, 128, SEQ], BF16).ap()
    c.k_sp = nc.dram_tensor("k_sp", [2, NCH, 128, SEQ], BF16).ap()
    c.v_sp = nc.dram_tensor("v_sp", [2, NCH, 128, SEQ], BF16).ap()
    c.xln_spd = nc.dram_tensor("xln_spd", [2, NCH, 128, SEQ], BF16).ap()

    import contextlib
    with contextlib.ExitStack() as st:
        e = st.enter_context
        c.R = e(nc.sbuf_tensor("R", [128, NCH, NC_TOK], BF16))
        c.XLN = e(nc.sbuf_tensor("XLN", [128, NCH, 512], BF16))
        c.HA = e(nc.sbuf_tensor("HA", [128, 32, 512], BF16))
        c.HB = e(nc.sbuf_tensor("HB", [128, 32, 512], BF16))
        c.HC = e(nc.sbuf_tensor("HC", [128, 32, 512], BF16))
        c.WB = e(nc.sbuf_tensor("WB", [128, 2, 4096], BF16))
        c.XSQ = e(nc.sbuf_tensor("XSQ", [128, 2, 512], BF16))
        c.STT = e(nc.sbuf_tensor("STT", [1, 1024], F32))
        c.BCA = e(nc.sbuf_tensor("BCA", [128, 2, 512], BF16))
        c.POOL = e(nc.sbuf_tensor("POOL", [128, 5120], U8))
        c.LG = e(nc.sbuf_tensor("LG", [128, 4, 8], F32))
        c.LGS = e(nc.sbuf_tensor("LGS", [128, 4, 8], F32))
        c.CFB = e(nc.sbuf_tensor("CFB", [128, 4, 8], BF16))
        c.RWS = e(nc.sbuf_tensor("RWS", [128, 8, 8], BF16))
        c.RBS = e(nc.sbuf_tensor("RBS", [1, 8], BF16))
        c.ONES = e(nc.sbuf_tensor("ONES", [128, 1], BF16))
        c.ONES1 = e(nc.sbuf_tensor("ONES1", [1, 128], BF16))
        c.ONES1F = e(nc.sbuf_tensor("ONES1F", [1, 128], F32))
        c.IDENT = e(nc.sbuf_tensor("IDENT", [128, 128], BF16))
        c.MASKC = e(nc.sbuf_tensor("MASKC", [128, 128], BF16))
        c.SWT = e(nc.sbuf_tensor("SWT", [128, 128], BF16))
        c.SCL = e(nc.sbuf_tensor("SCL", [128, 5, 32], F32))
        c.ACC4 = e(nc.sbuf_tensor("ACC4", [128, 8], F32))
        c.RD = e(nc.sbuf_tensor("RD", [128, 4], F32))
        c.LORA = e(nc.sbuf_tensor("LORA", [128, 8, 8], BF16))
        c.LBIA = e(nc.sbuf_tensor("LBIA", [128, 8], F32))
        c.REZ = e(nc.sbuf_tensor("REZ", [128, 1], F32))
        c.OSTM = e(nc.sbuf_tensor("OSTM", [128, 2, 512], F32))
        c.ABIA = e(nc.sbuf_tensor("ABIA", [128, NH], F32))
        c.NRM2 = e(nc.sbuf_tensor("NRM2", [2, 64], F32))
        c.QM = e(nc.sbuf_tensor("QM", [2, 16], F32))
        c.BHE = e(nc.sbuf_tensor("BHE", [2, 16], F32))
        c.IDENTF = e(nc.sbuf_tensor("IDENTF", [128, 128], F32))
        c.ONES2F = e(nc.sbuf_tensor("ONES2F", [2, 128], F32))
        c.H2 = e(nc.sbuf_tensor("H2", [128, 2], BF16))
        c.AB2 = e(nc.sbuf_tensor("AB2", [2, 16], F32))
        c.M2 = e(nc.sbuf_tensor("M2", [2, 16], F32))
        c.REZI = e(nc.sbuf_tensor("REZI", [1, 1], F32))
        pv = c.POOL[:]
        c.CT8 = pv.bitcast(BF16)[0:8, 0:512]
        c.EBS = pv.bitcast(BF16)[0:8, 512:1536].rearrange("r (m o) -> r m o", m=8)
        c.SEL = pv.bitcast(BF16)[0:8, 1536:2560].rearrange("r (s o) -> r s o", s=8)
        c.LORB = pv.bitcast(BF16)[0:8, 0:1024].rearrange("r (m o) -> r m o", m=8)
        c.USB = pv.bitcast(BF16)[0:8, 1024:1536]
        c.AF_ = pv.bitcast(BF16)[:, 1536:2560].rearrange("p (a b) -> p a b", a=8)
        c.PB = [e(nc.psum_tensor(f"PBm{i}", [128, 512], F32)) for i in range(4)]
        c.PA4 = e(nc.psum_tensor("PA4", [128, 512], F32))
        c.PA5 = e(nc.psum_tensor("PA5", [128, 512], F32))
        c.PO3 = e(nc.psum_tensor("PO3", [128, 512], F32))
        c.PT2 = e(nc.psum_tensor("PT2", [128, 1024], BF16))
        c.X = Sem(e(nc.semaphore("s_x")))
        c.SY = Sem(e(nc.semaphore("s_sy")), 16)
        c.G = Sem(e(nc.semaphore("s_g")), 16)
        c.T = Sem(e(nc.semaphore("s_t")))
        c.V = Sem(e(nc.semaphore("s_v")))
        c.S = Sem(e(nc.semaphore("s_s")))
        c.wc = {k: {} for k in ["sync", "gpsimd", "scalar", "vector", "tensor"]}
        emit_all(c, sublayers)
    return nc


def emit_all(c, sublayers):
    nc = c.nc
    c.cast_tok = {}
    with nc.Block("init") as blk:
        def _g(g):
            for sq in range(2):
                for ch in range(NCH):
                    c.rload = c.G.inc(g.dma_start(
                        out=c.R[:, ch, sq * SEQ:(sq + 1) * SEQ],
                        in_=c.x_in[sq, ch * 128:(ch + 1) * 128, :]))
            if "W_rw" in c.win:
                c.G.inc(g.dma_start(out=c.RWS[:], in_=c.win["W_rw"]))
                c.G.inc(g.dma_start(out=c.RBS[:], in_=c.win["W_rb"]))
            if "W_fw1s" in c.win:
                for i, n in enumerate(["W_fw1s", "W_fw2s", "W_fh1s", "W_fh2s"]):
                    c.G.inc(g.dma_start(out=c.SCL[:, i, :], in_=c.win[n]))
                c.G.inc(g.dma_start(out=c.SCL[:, 4, 0:8], in_=c.win["W_fw3s"]))
            c.small_tok = c.G.tok()
            order = [n for n in CAST_W if n in c.wsc]
            for n in order:
                src, dst = c.win[n], c.wsc[n]
                per = int(np.prod(src.shape[1:])) * 4
                step = max(1, (8 << 20) // per)
                tk = None
                for i0 in range(0, src.shape[0], step):
                    i1 = min(src.shape[0], i0 + step)
                    tk = c.G.inc(g.dma_start(out=dst[i0:i1], in_=src[i0:i1]))
                c.cast_tok[n] = tk
        blk.gpsimd(_g)

        def _sy(sy):
            for dst, src in [(c.ONES, c.cin["C_ones"]), (c.ONES1, c.cin["C_ones1"]),
                             (c.ONES1F, c.cin["C_ones1f"]),
                             (c.IDENT, c.cin["C_ident"]), (c.MASKC, c.cin["C_mask"]),
                             (c.SWT, c.cin["C_swapT"]), (c.IDENTF, c.cin["C_identf"]),
                             (c.ONES2F, c.cin["C_ones2f"]), (c.H2, c.cin["C_h2"]),
                             (c.AB2, c.cin["C_ab2"]), (c.M2, c.cin["C_m2"])]:
                c.SY.inc(sy.dma_start(out=dst[:], in_=src))
            c.const_tok = c.SY.tok()
        blk.sync(_sy)

    gfin = c.G.tok()
    c.rload = gfin
    c.small_tok = gfin
    for k in list(c.cast_tok):
        c.cast_tok[k] = gfin
    c.rtok = [c.rload] * NT
    c.rfree = [None] * NT
    c.xln_free = None
    c.hbuf_free = None
    c.wb_free = None
    c.stat_free = None
    c.pool_free = None
    c.pb_free = [None] * 4
    c.pa_free = None
    c.pt_free = None
    c.po_free = None
    final = len(sublayers) - 1

    for i, sl in enumerate(sublayers):
        if sl.startswith("moe"):
            emit_moe(c, is_final=(i == final))
        elif sl.startswith("ffn1"):
            emit_ffn1(c, is_final=(i == final))
        elif sl == "attn0":
            emit_attn(c, 0)
        elif sl == "attn1":
            emit_attn(c, 1)
    if sublayers[-1].startswith("attn"):
        emit_store(c)

    with nc.Block("fin") as blk:
        def _g(g):
            _wait(g, c.wc["gpsimd"], c.G.tok())
        blk.gpsimd(_g)

        def _sy(sy):
            _wait(sy, c.wc["sync"], c.SY.tok())
        blk.sync(_sy)



def plan_ln(P, p, tt):
    for ch in range(NCH):
        P.add("S", f"{p}sq{tt}_{ch}")
    P.add("S", f"{p}rstd{tt}")
    for ch in range(NCH):
        P.add("T", f"{p}sqmm{tt}_{ch}")
    P.add("T", f"{p}bc{tt}")
    P.add("V", f"{p}comb1_{tt}")
    P.add("V", f"{p}comb2_{tt}")
    P.add("V", f"{p}xln{tt}")


def ln_scalar(c, P, p, s, tt, prev_sq_reader=None, rtok=None):
    wc = c.wc["scalar"]
    _wait(s, wc, rtok if rtok is not None else c.rtok[tt])
    if tt == 0 and c.stat_free is not None:
        _wait(s, wc, c.stat_free)
    for ch in range(NCH):
        j = tt * NCH + ch
        if j >= 2:
            t2, c2 = divmod(j - 2, NCH)
            _wait(s, wc, P.tok(f"{p}sqmm{t2}_{c2}"))
        elif prev_sq_reader is not None:
            _wait(s, wc, prev_sq_reader)
        ins = s.activation(out=c.XSQ[:, j % 2, :], in_=c.R[:, ch, tt * 512:(tt + 1) * 512],
                           func=AF.Square)
        P.inc("S", ins, f"{p}sq{tt}_{ch}")
    _wait(s, wc, P.tok(f"{p}comb1_{tt}"))
    if tt > 0:
        _wait(s, wc, P.tok(f"{p}bc{tt - 1}"))
    s.activation(out=c.STT[0:1, 512:1024], in_=c.STT[0:1, 512:1024], func=AF.Ln)
    ins = s.activation(out=c.STT[0:1, 512:1024], in_=c.STT[0:1, 512:1024], func=AF.Exp,
                       scale=-0.5)
    P.inc("S", ins, f"{p}rstd{tt}")


def ln_tensor(c, P, p, t, tt, aux_war=None, rtok=None):
    wc = c.wc["tensor"]
    _wait(t, wc, c.const_tok)
    _wait(t, wc, rtok if rtok is not None else c.rtok[tt])
    _wait(t, wc, aux_war)
    if tt == 0:
        _wait(t, wc, c.stat_free)
        _wait(t, wc, c.pa_free)
    for ch in range(NCH):
        t.matmul(out=c.PA4[0:1, :], lhsT=c.ONES[:], rhs=c.R[:, ch, tt * 512:(tt + 1) * 512],
                 start=(ch == 0), stop=(ch == NCH - 1))
        _wait(t, wc, P.tok(f"{p}sq{tt}_{ch}"))
        ins = t.matmul(out=c.PA5[0:1, :], lhsT=c.ONES[:],
                       rhs=c.XSQ[:, (tt * NCH + ch) % 2, :],
                       start=(ch == 0), stop=(ch == NCH - 1))
        P.inc("T", ins, f"{p}sqmm{tt}_{ch}")


def ln_tensor_bcast(c, P, p, t, tt):
    wc = c.wc["tensor"]
    _wait(t, wc, P.tok(f"{p}comb2_{tt}"))
    _wait(t, wc, P.tok(f"{p}rstd{tt}"))
    t.matmul(out=c.PA4[:], lhsT=c.ONES1F[:], rhs=c.STT[0:1, 512:1024], start=True, stop=True)
    ins = t.matmul(out=c.PA5[:], lhsT=c.ONES1F[:], rhs=c.STT[0:1, 0:512], start=True, stop=True)
    P.inc("T", ins, f"{p}bc{tt}")


def ln_vector(c, P, p, v, tt, xln_war=None, rsl=None):
    wc = c.wc["vector"]
    _wait(v, wc, P.tok(f"{p}sqmm{tt}_{NCH - 1}"))
    if tt > 0:
        _wait(v, wc, P.tok(f"{p}bc{tt - 1}"))
    v.tensor_scalar(out=c.STT[0:1, 0:512], in0=c.PA4[0:1, :], scalar1=1.0 / DM,
                    scalar2=None, op0=ALU.mult)
    v.tensor_scalar(out=c.STT[0:1, 512:1024], in0=c.PA5[0:1, :], scalar1=1.0 / DM,
                    scalar2=None, op0=ALU.mult)
    v.tensor_tensor(out=c.PA4[0:1, :], in0=c.STT[0:1, 0:512], in1=c.STT[0:1, 0:512],
                    op=ALU.mult)
    v.tensor_tensor(out=c.STT[0:1, 512:1024], in0=c.STT[0:1, 512:1024], in1=c.PA4[0:1, :],
                    op=ALU.subtract)
    ins = v.tensor_scalar(out=c.STT[0:1, 512:1024], in0=c.STT[0:1, 512:1024], scalar1=EPS,
                          scalar2=None, op0=ALU.add)
    P.inc("V", ins, f"{p}comb1_{tt}")
    _wait(v, wc, P.tok(f"{p}rstd{tt}"))
    v.tensor_tensor(out=c.STT[0:1, 0:512], in0=c.STT[0:1, 0:512], in1=c.STT[0:1, 512:1024],
                    op=ALU.mult)
    ins = v.tensor_scalar(out=c.STT[0:1, 0:512], in0=c.STT[0:1, 0:512], scalar1=-1.0,
                          scalar2=None, op0=ALU.mult)
    P.inc("V", ins, f"{p}comb2_{tt}")
    _wait(v, wc, P.tok(f"{p}bc{tt}"))
    _wait(v, wc, xln_war)
    v.tensor_copy(out=c.BCA[:, 0, :], in_=c.PA4[:])
    v.tensor_copy(out=c.BCA[:, 1, :], in_=c.PA5[:])
    rs = rsl if rsl is not None else (lambda ch: c.R[:, ch, tt * 512:(tt + 1) * 512])
    for ch in range(NCH):
        v.tensor_tensor(out=c.XLN[:, ch, :], in0=rs(ch), in1=c.BCA[:, 0, :], op=ALU.mult)
        ins = v.tensor_tensor(out=c.XLN[:, ch, :], in0=c.XLN[:, ch, :], in1=c.BCA[:, 1, :],
                              op=ALU.add)
    P.inc("V", ins, f"{p}xln{tt}")



FFN_NG = 136
FFN_NLD = 88


def ffn_group_stage(g):
    if g < 32:
        return "w1", g
    if g < 64:
        return "w2", g - 32
    if g < 96:
        return "h1", g - 64
    if g < 128:
        return "h2", g - 96
    return "w3", g - 128


def ffn_load_of_group(g):
    st, m = ffn_group_stage(g)
    if st == "w1":
        return m // 4
    if st == "w2":
        return 8 + m // 4
    if st == "h1":
        return 16 + m
    if st == "h2":
        return 48 + m
    return 80 + m


def ffn_load_spec(li):
    if li < 8:
        return "w1", list(range(li * 4, li * 4 + 4)), li
    if li < 16:
        j = li - 8
        return "w2", list(range(32 + j * 4, 32 + j * 4 + 4)), j
    if li < 48:
        j = li - 16
        return "h1", [64 + j], j
    if li < 80:
        j = li - 48
        return "h2", [96 + j], j
    j = li - 80
    return "w3", [128 + j], j


def emit_ffn1(c, is_final):
    nc = c.nc
    P = Plan({"T": c.T, "S": c.S, "V": c.V, "SY": c.SY, "G": c.G})
    emit_ffn1.count = getattr(emit_ffn1, "count", 0) + 1
    p = f"f{emit_ffn1.count}_"
    for tt in range(NT):
        plan_ln(P, p, tt)
        for li in range(FFN_NLD):
            P.add("SY", f"{p}ld{tt}_{li}")
        for g in range(FFN_NG):
            P.add("T", f"{p}g{tt}_{g}")
        for g in range(128):
            P.add("S", f"{p}ev{tt}_{g}")
        for m in range(32):
            P.add("V", f"{p}h0_{tt}_{m}")
        for m in range(8):
            P.add("V", f"{p}res{tt}_{m}")
        if is_final:
            for m in range(8):
                P.add("G", f"{p}out{tt}_{m}")

    prev_hfree = c.hbuf_free
    prev_xfree = c.xln_free
    prev_wbfree = c.wb_free
    prev_pbfree = list(c.pb_free)
    prev_poolfree = c.pool_free

    def reader_tok(tt, g):
        st, m = ffn_group_stage(g)
        if st == "w3":
            return P.tok(f"{p}res{tt}_{m}")
        return P.tok(f"{p}ev{tt}_{g}")

    with nc.Block(f"ffn{emit_ffn1.count}") as blk:
        def _sy(sy):
            wc = c.wc["sync"]
            for tt in range(NT):
                for li in range(FFN_NLD):
                    st, gs, j = ffn_load_spec(li)
                    gli = tt * FFN_NLD + li
                    if gli >= 2:
                        ptt, pli = divmod(gli - 2, FFN_NLD)
                        _, pgs, _ = ffn_load_spec(pli)
                        _wait(sy, wc, reader_tok(ptt, pgs[-1]))
                    else:
                        _wait(sy, wc, prev_wbfree)
                    _wait(sy, wc, c.cast_tok["W_f" + st])
                    src = c.wsc["W_f" + st]
                    if st in ("w1", "w2"):
                        sl = src[j * 4:(j + 1) * 4]
                        dst = c.WB[:, gli % 2].rearrange("p (m ko) -> p m ko", m=4)
                    else:
                        sl = src[j:j + 1]
                        dst = c.WB[:, gli % 2].rearrange("p (m ko) -> p m ko", m=1)
                    ins = sy.dma_start(out=dst, in_=sl.rearrange("m p k o -> p m (k o)"))
                    P.inc("SY", ins, f"{p}ld{tt}_{li}")
        blk.sync(_sy)

        def _s(s):
            wc = c.wc["scalar"]
            _wait(s, wc, c.small_tok)
            for tt in range(NT):
                ln_scalar(c, P, p, s, tt,
                          prev_sq_reader=(None if tt > 0 else c.stat_free))
                for g in range(128):
                    st, m = ffn_group_stage(g)
                    _wait(s, wc, P.tok(f"{p}g{tt}_{g}"))
                    pb = c.PB[g % 4]
                    if st == "w1":
                        ins = s.activation(out=c.HA[:, m, :], in_=pb[:], func=AF.Silu,
                                           scale=c.SCL[:, 0, m:m + 1])
                    elif st == "w2":
                        ins = s.activation(out=c.HB[:, m, :], in_=pb[:], func=AF.Identity,
                                           scale=c.SCL[:, 1, m:m + 1])
                    elif st == "h1":
                        _wait(s, wc, P.tok(f"{p}h0_{tt}_{m}"))
                        ins = s.activation(out=c.HA[:, m, :], in_=pb[:], func=AF.Gelu,
                                           scale=c.SCL[:, 2, m:m + 1])
                    else:
                        ins = s.activation(out=c.HB[:, m, :], in_=pb[:], func=AF.Gelu,
                                           scale=c.SCL[:, 3, m:m + 1])
                    P.inc("S", ins, f"{p}ev{tt}_{g}")
        blk.scalar(_s)

        def _v(v):
            wc = c.wc["vector"]
            for tt in range(NT):
                xw = prev_xfree if tt == 0 else P.tok(f"{p}g{tt - 1}_{63}")
                ln_vector(c, P, p, v, tt, xln_war=xw)
                for m in range(32):
                    _wait(v, wc, P.tok(f"{p}ev{tt}_{m}"))
                    _wait(v, wc, P.tok(f"{p}ev{tt}_{32 + m}"))
                    if tt > 0:
                        _wait(v, wc, P.tok(f"{p}g{tt - 1}_{95}"))
                    else:
                        _wait(v, wc, prev_hfree)
                    ins = v.tensor_tensor(out=c.HC[:, m, :], in0=c.HA[:, m, :],
                                          in1=c.HB[:, m, :], op=ALU.mult)
                    P.inc("V", ins, f"{p}h0_{tt}_{m}")
                for m in range(8):
                    g = 128 + m
                    _wait(v, wc, P.tok(f"{p}g{tt}_{g}"))
                    _wait(v, wc, c.rfree[tt])
                    pb = c.PB[g % 4]
                    if is_final:
                        if tt == 0 and m < 2:
                            _wait(v, wc, prev_poolfree)
                        if m >= 2:
                            _wait(v, wc, P.tok(f"{p}out{tt}_{m - 2}"))
                        elif tt > 0:
                            _wait(v, wc, P.tok(f"{p}out{tt - 1}_{m + 6}"))
                        v.tensor_scalar(out=c.OSTM[:, m % 2, :], in0=pb[:],
                                        scalar1=c.SCL[:, 4, m:m + 1], scalar2=0.5,
                                        op0=ALU.mult, op1=ALU.mult)
                        ins = v.tensor_tensor(out=c.OSTM[:, m % 2, :], in0=c.OSTM[:, m % 2, :],
                                              in1=c.R[:, m, tt * 512:(tt + 1) * 512],
                                              op=ALU.add)
                    else:
                        v.tensor_scalar(out=pb[:], in0=pb[:],
                                        scalar1=c.SCL[:, 4, m:m + 1], scalar2=0.5,
                                        op0=ALU.mult, op1=ALU.mult)
                        ins = v.tensor_tensor(out=c.R[:, m, tt * 512:(tt + 1) * 512],
                                              in0=c.R[:, m, tt * 512:(tt + 1) * 512],
                                              in1=pb[:], op=ALU.add)
                    P.inc("V", ins, f"{p}res{tt}_{m}")
        blk.vector(_v)

        def _t(t):
            wc = c.wc["tensor"]
            for tt in range(NT):
                aux_war = P.tok(f"{p}xln{tt - 1}") if tt > 0 else c.stat_free
                ln_tensor(c, P, p, t, tt, aux_war=aux_war)
                ln_tensor_bcast(c, P, p, t, tt)
                for g in range(FFN_NG):
                    st, m = ffn_group_stage(g)
                    li = ffn_load_of_group(g)
                    _wait(t, wc, P.tok(f"{p}ld{tt}_{li}"))
                    pg = g - 4
                    if pg >= 0:
                        _wait(t, wc, reader_tok(tt, pg))
                    elif tt > 0:
                        _wait(t, wc, reader_tok(tt - 1, FFN_NG - 4 + g))
                    else:
                        _wait(t, wc, prev_pbfree[g % 4])
                    if st in ("w1", "w2"):
                        _wait(t, wc, P.tok(f"{p}xln{tt}"))
                        rhs = [c.XLN[:, k, :] for k in range(8)]
                    elif st == "h1":
                        _wait(t, wc, P.tok(f"{p}h0_{tt}_{31}"))
                        rhs = [c.HC[:, k, :] for k in range(32)]
                    elif st == "h2":
                        _wait(t, wc, P.tok(f"{p}ev{tt}_{64 + 31}"))
                        rhs = [c.HA[:, k, :] for k in range(32)]
                    else:
                        _wait(t, wc, P.tok(f"{p}ev{tt}_{96 + 31}"))
                        rhs = [c.HB[:, k, :] for k in range(32)]
                    W = c.WB[:, (tt * FFN_NLD + li) % 2]
                    if st in ("w1", "w2"):
                        off = (m % 4) * 1024
                    else:
                        off = 0
                    pb = c.PB[g % 4]
                    nk = len(rhs)
                    for k in range(nk):
                        ins = t.matmul(out=pb[:], lhsT=W[:, off + k * 128: off + (k + 1) * 128],
                                       rhs=rhs[k], start=(k == 0), stop=(k == nk - 1))
                    P.inc("T", ins, f"{p}g{tt}_{g}")
        blk.tensor(_t)

        if is_final:
            def _g(g):
                wc = c.wc["gpsimd"]
                yv = c.y_out.rearrange("s (ch q) t -> q ch s t", q=128)
                for tt in range(NT):
                    for m in range(8):
                        _wait(g, wc, P.tok(f"{p}res{tt}_{m}"))
                        dst = yv[:, m, tt // 4, (tt % 4) * 512:(tt % 4) * 512 + 512]
                        ins = g.dma_start(out=dst, in_=c.OSTM[:, m % 2, :])
                        P.inc("G", ins, f"{p}out{tt}_{m}")
            blk.gpsimd(_g)

    P.done()
    for tt in range(NT):
        c.rtok[tt] = P.tok(f"{p}res{tt}_{7}")
        c.rfree[tt] = P.tok(f"{p}res{tt}_{7}")
    c.xln_free = P.tok(f"{p}g{NT - 1}_{63}")
    c.hbuf_free = P.tok(f"{p}res{NT - 1}_{7}")
    c.wb_free = P.tok(f"{p}res{NT - 1}_{7}")
    c.stat_free = P.tok(f"{p}xln{NT - 1}")
    for b in range(4):
        g_last = max(g for g in range(FFN_NG) if g % 4 == b)
        c.pb_free[b] = reader_tok(NT - 1, g_last)
    c.pa_free = P.tok(f"{p}xln{NT - 1}")
    if is_final:
        c.pool_free = P.tok(f"{p}out{NT - 1}_{7}")



def emit_moe(c, is_final):
    nc = c.nc
    P = Plan({"T": c.T, "S": c.S, "V": c.V, "SY": c.SY, "G": c.G})
    emit_moe.count = getattr(emit_moe, "count", 0) + 1
    p = f"m{emit_moe.count}_"
    P.add("G", f"{p}poolA")
    P.add("G", f"{p}pool")
    for tt in range(NT):
        plan_ln(P, p, tt)
        for gr in range(4):
            P.add("T", f"{p}lg{tt}_{gr}")
        for gr in range(4):
            P.add("V", f"{p}cfc{tt}_{gr}")
        P.add("V", f"{p}cfA{tt}")
        for gr in range(4):
            P.add("S", f"{p}cex{tt}_{gr}")
        P.add("V", f"{p}cf2A{tt}")
        for gr in range(4):
            P.add("T", f"{p}ct{tt}_{gr}")
        for gr in range(4):
            P.add("V", f"{p}cte{tt}_{gr}")
        for e in range(8):
            P.add("T", f"{p}cb{tt}_{e}")
            P.add("V", f"{p}cbe{tt}_{e}")
        for mh in range(2):
            for e in range(8):
                P.add("SY", f"{p}ld{tt}_{mh}_{e}")
                P.add("V", f"{p}xe{tt}_{mh}_{e}")
                P.add("T", f"{p}em{tt}_{mh}_{e}")
            for m in range(4):
                P.add("V", f"{p}res{tt}_{mh}_{m}")
                if is_final:
                    P.add("G", f"{p}out{tt}_{mh}_{m}")

    prev_hfree = c.hbuf_free
    prev_xfree = c.xln_free
    prev_wbfree = c.wb_free
    prev_pbfree = list(c.pb_free)
    prev_pafree = c.pa_free
    prev_ptfree = c.pt_free
    prev_poolfree = c.pool_free

    with nc.Block(f"moe{emit_moe.count}") as blk:
        def _g(g):
            wc = c.wc["gpsimd"]
            _wait(g, wc, c.G.tok())
            _wait(g, wc, prev_poolfree)
            ins = g.dma_start(out=c.EBS, in_=c.win["W_expb"].rearrange("m r o -> r m o"))
            P.inc("G", ins, f"{p}poolA")
            ins = g.dma_start(out=c.SEL, in_=c.cin["C_sel"])
            P.inc("G", ins, f"{p}pool")
        blk.gpsimd(_g)

        def _sy(sy):
            wc = c.wc["sync"]
            for tt in range(NT):
                for mh in range(2):
                    for e in range(8):
                        gli = (tt * 2 + mh) * 8 + e
                        if gli >= 2:
                            ptt, r = divmod(gli - 2, 16)
                            pmh, pe = divmod(r, 8)
                            _wait(sy, wc, P.tok(f"{p}em{ptt}_{pmh}_{pe}"))
                        else:
                            _wait(sy, wc, prev_wbfree)
                        _wait(sy, wc, c.cast_tok["W_exp"])
                        src = c.wsc["W_exp"][e, :, mh * 4:(mh + 1) * 4]
                        ins = sy.dma_start(out=c.WB[:, gli % 2],
                                           in_=src.rearrange("p m k o -> p (m k o)"))
                        P.inc("SY", ins, f"{p}ld{tt}_{mh}_{e}")
        blk.sync(_sy)

        def _s(s):
            wc = c.wc["scalar"]
            for tt in range(NT):
                ln_scalar(c, P, p, s, tt,
                          prev_sq_reader=(None if tt > 0 else c.stat_free))
                for gr in range(4):
                    _wait(s, wc, P.tok(f"{p}cfA{tt}"))
                    ins = s.activation(out=c.LGS[:, gr, :], in_=c.LGS[:, gr, :], func=AF.Exp,
                                       accum_out=c.ACC4[:, gr:gr + 1])
                    P.inc("S", ins, f"{p}cex{tt}_{gr}")
        blk.scalar(_s)

        def _v(v):
            wc = c.wc["vector"]
            for tt in range(NT):
                xw = prev_xfree if tt == 0 else P.tok(f"{p}lg{tt - 1}_{3}")
                ln_vector(c, P, p, v, tt, xln_war=xw)
                for gr in range(4):
                    _wait(v, wc, P.tok(f"{p}lg{tt}_{gr}"))
                    pa = c.PA4 if gr % 2 == 0 else c.PA5
                    ins = v.tensor_copy(out=c.LG[:, gr, :], in_=pa[:, 0:8])
                    P.inc("V", ins, f"{p}cfc{tt}_{gr}")
                _wait(v, wc, P.tok(f"{p}cfc{tt}_{3}"))
                for gr in range(4):
                    ins = v.reduce_max(out=c.RD[:, gr:gr + 1], in_=c.LG[:, gr, :],
                                       axis=AX.X, negate=True)
                _selfsync(c, v, ins)
                for gr in range(4):
                    v.tensor_scalar(out=c.LGS[:, gr, :], in0=c.LG[:, gr, :],
                                    scalar1=c.RD[:, gr:gr + 1], scalar2=None, op0=ALU.add)
                for gr in range(4):
                    ins = v.tensor_scalar(out=c.LG[:, gr, :], in0=c.LG[:, gr, :],
                                          scalar1=c.RD[:, gr:gr + 1], scalar2=0.0,
                                          op0=ALU.add, op1=ALU.is_ge)
                P.inc("V", ins, f"{p}cfA{tt}")
                for gr in range(4):
                    _wait(v, wc, P.tok(f"{p}cex{tt}_{gr}"))
                ins = v.reciprocal(out=c.RD[:, 0:4], in_=c.ACC4[:, 0:4])
                _selfsync(c, v, ins)
                for gr in range(4):
                    ins = v.tensor_scalar(out=c.LG[:, gr, :], in0=c.LG[:, gr, :],
                                          scalar1=c.RD[:, gr:gr + 1], scalar2=None,
                                          op0=ALU.mult)
                _selfsync(c, v, ins)
                for gr in range(4):
                    ins = v.tensor_copy(out=c.CFB[:, gr, :], in_=c.LG[:, gr, :])
                P.inc("V", ins, f"{p}cf2A{tt}")
                for gr in range(4):
                    _wait(v, wc, P.tok(f"{p}ct{tt}_{gr}"))
                    if tt > 0:
                        _wait(v, wc, P.tok(f"{p}em{tt - 1}_{1}_{0}"))
                    else:
                        _wait(v, wc, prev_poolfree)
                    ins = v.tensor_copy(out=c.CT8[:, gr * 128:(gr + 1) * 128],
                                        in_=c.PT2[0:8, gr * 128:(gr + 1) * 128])
                    P.inc("V", ins, f"{p}cte{tt}_{gr}")
                for e in range(8):
                    _wait(v, wc, P.tok(f"{p}cb{tt}_{e}"))
                    pa = c.PA4 if e % 2 == 0 else c.PA5
                    if tt > 0 and e < 2:
                        _wait(v, wc, P.tok(f"{p}xe{tt - 1}_{1}_{7}"))
                    elif tt == 0 and e < 2:
                        _wait(v, wc, prev_hfree)
                    ins = v.tensor_copy(out=c.HB[:, e, :], in_=pa[:])
                    P.inc("V", ins, f"{p}cbe{tt}_{e}")
                for mh in range(2):
                    for e in range(8):
                        _wait(v, wc, P.tok(f"{p}cbe{tt}_{e}"))
                        gx = (tt * 2 + mh) * 8 + e
                        if gx >= 2:
                            ptt, r = divmod(gx - 2, 16)
                            pmh, pe = divmod(r, 8)
                            _wait(v, wc, P.tok(f"{p}em{ptt}_{pmh}_{pe}"))
                        else:
                            _wait(v, wc, prev_hfree)
                        xb = (gx % 2) * 8
                        for ch in range(8):
                            ins = v.tensor_tensor(out=c.HA[:, xb + ch, :],
                                                  in0=c.XLN[:, ch, :],
                                                  in1=c.HB[:, e, :], op=ALU.mult)
                        P.inc("V", ins, f"{p}xe{tt}_{mh}_{e}")
                    for m in range(4):
                        _wait(v, wc, P.tok(f"{p}em{tt}_{mh}_{7}"))
                        _wait(v, wc, c.rfree[tt])
                        cglob = mh * 4 + m
                        if is_final:
                            jj = (tt * 2 + mh) * 4 + m
                            if jj >= 2:
                                pj = jj - 2
                                ptt2, r2 = divmod(pj, 8)
                                pmh2, pm2 = divmod(r2, 4)
                                _wait(v, wc, P.tok(f"{p}out{ptt2}_{pmh2}_{pm2}"))
                            ins = v.scalar_tensor_tensor(
                                out=c.OSTM[:, m % 2, :], in0=c.PB[m][:],
                                scalar=0.5, in1=c.R[:, cglob, tt * 512:(tt + 1) * 512],
                                op0=ALU.mult, op1=ALU.add)
                        else:
                            ins = v.scalar_tensor_tensor(
                                out=c.R[:, cglob, tt * 512:(tt + 1) * 512], in0=c.PB[m][:],
                                scalar=0.5, in1=c.R[:, cglob, tt * 512:(tt + 1) * 512],
                                op0=ALU.mult, op1=ALU.add)
                        P.inc("V", ins, f"{p}res{tt}_{mh}_{m}")
        blk.vector(_v)

        def _t(t):
            wc = c.wc["tensor"]
            _wait(t, wc, c.small_tok)
            for tt in range(NT):
                aux_war = P.tok(f"{p}cbe{tt - 1}_{7}") if tt > 0 else c.stat_free
                ln_tensor(c, P, p, t, tt, aux_war=aux_war)
                ln_tensor_bcast(c, P, p, t, tt)
                for gr in range(4):
                    _wait(t, wc, P.tok(f"{p}xln{tt}"))
                    pa = c.PA4 if gr % 2 == 0 else c.PA5
                    if gr >= 2:
                        _wait(t, wc, P.tok(f"{p}cfc{tt}_{gr - 2}"))
                    for ch in range(8):
                        t.matmul(out=pa[:, 0:8],
                                 lhsT=c.XLN[:, ch, gr * 128:(gr + 1) * 128],
                                 rhs=c.RWS[:, ch, :], start=(ch == 0), stop=False)
                    ins = t.matmul(out=pa[:, 0:8], lhsT=c.ONES1[:], rhs=c.RBS[:],
                                   start=False, stop=True)
                    P.inc("T", ins, f"{p}lg{tt}_{gr}")
                for gr in range(4):
                    _wait(t, wc, P.tok(f"{p}cf2A{tt}"))
                    if gr > 0:
                        _wait(t, wc, P.tok(f"{p}cte{tt}_{gr - 1}"))
                    elif tt > 0:
                        _wait(t, wc, P.tok(f"{p}cte{tt - 1}_{3}"))
                    else:
                        _wait(t, wc, prev_ptfree)
                    ins = t.transpose(out=c.PT2[0:8, gr * 128:(gr + 1) * 128],
                                      in_=c.CFB[:, gr, :], identity=c.IDENT[:])
                    P.inc("T", ins, f"{p}ct{tt}_{gr}")
                for e in range(8):
                    _wait(t, wc, P.tok(f"{p}cte{tt}_{3}"))
                    _wait(t, wc, P.tok(f"{p}pool"))
                    pa = c.PA4 if e % 2 == 0 else c.PA5
                    if e >= 2:
                        _wait(t, wc, P.tok(f"{p}cbe{tt}_{e - 2}"))
                    else:
                        _wait(t, wc, P.tok(f"{p}cfA{tt}"))
                    ins = t.matmul(out=pa[:], lhsT=c.SEL[:, e, :], rhs=c.CT8[:],
                                   start=True, stop=True)
                    P.inc("T", ins, f"{p}cb{tt}_{e}")
                for mh in range(2):
                    for e in range(8):
                        _wait(t, wc, P.tok(f"{p}ld{tt}_{mh}_{e}"))
                        _wait(t, wc, P.tok(f"{p}xe{tt}_{mh}_{e}"))
                        gli = (tt * 2 + mh) * 8 + e
                        if e == 0:
                            if mh == 1:
                                for m in range(4):
                                    _wait(t, wc, P.tok(f"{p}res{tt}_{0}_{m}"))
                            elif tt > 0:
                                for m in range(4):
                                    _wait(t, wc, P.tok(f"{p}res{tt - 1}_{1}_{m}"))
                            else:
                                for b in range(4):
                                    _wait(t, wc, prev_pbfree[b])
                            for m in range(4):
                                t.matmul(out=c.PB[m][:], lhsT=c.EBS[:, mh * 4 + m, :],
                                         rhs=c.CT8[:], start=True, stop=False)
                        W = c.WB[:, gli % 2].rearrange("p (m k o) -> p m k o", m=4, k=8)
                        xb = (gli % 2) * 8
                        for m in range(4):
                            for k in range(8):
                                ins = t.matmul(out=c.PB[m][:], lhsT=W[:, m, k],
                                               rhs=c.HA[:, xb + k, :],
                                               start=False, stop=(e == 7 and k == 7))
                        P.inc("T", ins, f"{p}em{tt}_{mh}_{e}")
        blk.tensor(_t)

        if is_final:
            def _go(g):
                wc = c.wc["gpsimd"]
                yv = c.y_out.rearrange("s (ch q) t -> q ch s t", q=128)
                for tt in range(NT):
                    for mh in range(2):
                        for m in range(4):
                            _wait(g, wc, P.tok(f"{p}res{tt}_{mh}_{m}"))
                            cglob = mh * 4 + m
                            dst = yv[:, cglob, tt // 4, (tt % 4) * 512:(tt % 4) * 512 + 512]
                            ins = g.dma_start(out=dst, in_=c.OSTM[:, m % 2, :])
                            P.inc("G", ins, f"{p}out{tt}_{mh}_{m}")
            blk.gpsimd(_go)

    P.done()
    for tt in range(NT):
        c.rtok[tt] = P.tok(f"{p}res{tt}_{1}_{3}")
        c.rfree[tt] = P.tok(f"{p}res{tt}_{1}_{3}")
    c.xln_free = P.tok(f"{p}xe{NT - 1}_{1}_{7}")
    c.hbuf_free = P.tok(f"{p}em{NT - 1}_{1}_{7}")
    c.wb_free = P.tok(f"{p}em{NT - 1}_{1}_{7}")
    c.stat_free = P.tok(f"{p}xln{NT - 1}")
    for b in range(4):
        c.pb_free[b] = P.tok(f"{p}res{NT - 1}_{1}_{b}")
    c.pa_free = P.tok(f"{p}cbe{NT - 1}_{7}")
    c.pt_free = P.tok(f"{p}cte{NT - 1}_{3}")
    c.pool_free = P.tok(f"{p}em{NT - 1}_{1}_{0}")



def emit_store(c):
    nc = c.nc
    P = Plan({"V": c.V, "G": c.G})
    emit_store.count = getattr(emit_store, "count", 0) + 1
    p = f"st{emit_store.count}_"
    for tt in range(NT):
        for ch in range(NCH):
            P.add("V", f"{p}c{tt}_{ch}")
            P.add("G", f"{p}o{tt}_{ch}")
    with nc.Block(f"store{emit_store.count}") as blk:
        def _v(v):
            wc = c.wc["vector"]
            for tt in range(NT):
                _wait(v, wc, c.rtok[tt])
                for ch in range(NCH):
                    j = tt * NCH + ch
                    if j >= 2:
                        pt, pc = divmod(j - 2, NCH)
                        _wait(v, wc, P.tok(f"{p}o{pt}_{pc}"))
                    ins = v.tensor_copy(out=c.OSTM[:, j % 2, :],
                                        in_=c.R[:, ch, tt * 512:(tt + 1) * 512])
                    P.inc("V", ins, f"{p}c{tt}_{ch}")
        blk.vector(_v)

        def _g(g):
            wc = c.wc["gpsimd"]
            _wait(g, wc, c.G.tok())
            yv = c.y_out.rearrange("s (ch q) t -> q ch s t", q=128)
            for tt in range(NT):
                for ch in range(NCH):
                    j = tt * NCH + ch
                    _wait(g, wc, P.tok(f"{p}c{tt}_{ch}"))
                    dst = yv[:, ch, tt // 4, (tt % 4) * 512:(tt % 4) * 512 + 512]
                    ins = g.dma_start(out=dst, in_=c.OSTM[:, j % 2, :])
                    P.inc("G", ins, f"{p}o{tt}_{ch}")
        blk.gpsimd(_g)
    P.done()


def attn_kbs(layer, qi):
    if layer == 0:
        return [list(range(4 * kb, 4 * kb + 4)) for kb in range(4)]
    kcs = list(range(qi + 1))
    return [kcs[i:i + 4] for i in range(0, len(kcs), 4)]


def emit_attn(c, layer):
    nc = c.nc
    P = Plan({"T": c.T, "S": c.S, "V": c.V, "SY": c.SY, "G": c.G})
    emit_attn.count = getattr(emit_attn, "count", 0) + 1
    p = f"a{layer}_{emit_attn.count}_"
    L0 = (layer == 0)
    projs = ["q", "k", "v"]
    KR = 67 if L0 else 64
    wsc = {n: c.wsc[f"W_a{layer}{n}"] for n in "qkvo"}

    QAv = c.HC[:, 0:8, :].rearrange("p a b -> p (a b)").rearrange("p (b t) -> p b t", b=2)
    KAv = c.HC[:, 8:16, :].rearrange("p a b -> p (a b)").rearrange("p (b t) -> p b t", b=2)
    CS2v = c.HC[:, 16:24, :].rearrange("p a b -> p (a b)").rearrange("p (b t) -> p b t", b=2)
    STG = c.HC[:, 24:32, :]
    VT = c.HB[:].rearrange("p a b -> p (a b)").rearrange("p (h k d) -> p h k d", h=16, k=16)
    ATT = c.HA[:].rearrange("p a b -> p (a b)").rearrange("p (q d) -> p q d", q=16)
    OSB = c.OSTM[:].rearrange("p a b -> p (a b)").bitcast(BF16)
    PSC = c.WB[:, 0, 0:2048]
    PTS = c.WB[:, 1, 0:2048]
    PTB = [c.PT2[:, 0:512], c.PB[2][:].bitcast(BF16)[:, 0:512]]
    PTD = [c.PT2[:, 0:1024], c.PB[2][:].bitcast(BF16)[:, 0:1024]] if False else None

    ld_counter = [0]
    ld_slot = {}
    ld_war = {}

    def plan_load(name, reader_name):
        i = ld_counter[0]
        ld_slot[name] = i % 2
        ld_war[name] = None if i < 2 else plan_load.hist[i - 2][1]
        plan_load.hist.append((name, reader_name))
        ld_counter[0] += 1
        P.add("SY", name)
    plan_load.hist = []

    kbctr = [0]
    kb_bank = {}
    lora_prev = []
    psc_war = {}
    acc_war = {}
    pts_war = {}
    ptc_war = {}
    _psc_prev = {}
    _acc_prev = {}
    _pts_prev = {}
    _ptc_prev = {}

    if L0:
        P.add("G", p + "rez_ld")
        P.add("T", p + "rez_mm")
        P.add("V", p + "rez_ev")
        P.add("V", p + "ones0")
        P.add("V", p + "ones1")
    lwo_waits = {}
    lw_waits = {}
    for s in range(2):
        if not L0:
            P.add("SY", p + f"cs2a_{s}")
            P.add("SY", p + f"cs2_{s}")
        for tl in range(4):
            gt = s * 4 + tl
            plan_ln(P, p, gt)
            for proj in projs:
                if L0:
                    P.add("G", p + f"lwA{gt}_{proj}")
                    P.add("G", p + f"lwB{gt}_{proj}")
                    P.add("G", p + f"lw{gt}_{proj}")
                    lw_waits[p + f"lw{gt}_{proj}"] = list(lora_prev)
                    lora_prev = [p + f"ev{gt}_{proj}_{7}", p + f"g{gt}_{proj}_{7}"]
                for hf in range(2):
                    plan_load(p + f"wl{gt}_{proj}_{hf}", p + f"g{gt}_{proj}_{3 + 4 * hf}")
                if L0:
                    P.add("T", p + f"u{gt}_{proj}")
                    P.add("S", p + f"ue{gt}_{proj}")
                for m in range(8):
                    P.add("T", p + f"g{gt}_{proj}_{m}")
                    P.add("S", p + f"ev{gt}_{proj}_{m}")
                    if ATTN_STAGE < 1:
                        continue
                    if (not L0) and proj != "v" and "r" in ATTN_FEAT:
                        P.add("T", p + f"sw{gt}_{proj}_{m}")
                        P.add("V", p + f"rp{gt}_{proj}_{m}")
                    if proj != "v":
                        if "n" in ATTN_FEAT:
                            P.add("S", p + f"nsq{gt}_{proj}_{m}")
                            P.add("T", p + f"nmm{gt}_{proj}_{m}")
                            P.add("V", p + f"nrd{gt}_{proj}_{m}")
                    elif "t" in ATTN_FEAT:
                        P.add("T", p + f"vt{gt}_{m}")
                        P.add("V", p + f"vte{gt}_{m}")
                if proj != "v" and ATTN_STAGE >= 1 and "s" in ATTN_FEAT:
                    for ch in range(NCH - 1):
                        P.add("G", p + f"sp{gt}_{proj}c{ch}")
                    P.add("G", p + f"sp{gt}_{proj}")
            if L0 and ATTN_STAGE >= 1 and "s" in ATTN_FEAT:
                for ch in range(NCH - 1):
                    P.add("G", p + f"spx{gt}c{ch}")
                P.add("G", p + f"spx{gt}")
        if ATTN_STAGE >= 1 and "b" in ATTN_FEAT:
            P.add("V", p + f"nb{s}")
            P.add("T", p + f"nbm{s}")
            P.add("V", p + f"nbe{s}")
        if ATTN_STAGE < 2:
            continue
        for h in range(NH):
            P.add("SY", p + f"hq{s}_{h}")
            P.add("SY", p + f"hk{s}_{h}")
            if L0:
                P.add("SY", p + f"ha{s}_{h}")
            for qi in range(16):
                kbs = attn_kbs(layer, qi)
                for kb, kcs in enumerate(kbs):
                    nm = p + f"ex{s}_{h}_{qi}_{kb}"
                    sb = kbctr[0] % 2
                    tb = kbctr[0] % 2
                    kb_bank[(s, h, qi, kb)] = sb
                    kbctr[0] += 1
                    P.add("T", p + f"qk{s}_{h}_{qi}_{kb}")
                    if (not L0) and kb == len(kbs) - 1:
                        P.add("V", p + f"dg{s}_{h}_{qi}")
                    P.add("S", nm)
                    psc_war[nm] = _psc_prev.get(kb)
                    acc_war[nm] = _acc_prev.get(kb)
                    _psc_prev[kb] = p + f"pt{s}_{h}_{qi}_{kb}"
                    _acc_prev[kb] = p + f"o{s}_{h}_{qi}"
                    ptn = p + f"pt{s}_{h}_{qi}_{kb}"
                    P.add("T", ptn)
                    ptc_war[ptn] = _ptc_prev.get(tb)
                    _ptc_prev[tb] = p + f"pte{s}_{h}_{qi}_{kb}"
                    pte = p + f"pte{s}_{h}_{qi}_{kb}"
                    P.add("V", pte)
                    pts_war[pte] = _pts_prev.get(kb)
                    _pts_prev[kb] = p + f"pv{s}_{h}_{qi}_{kb}"
                    P.add("T", p + f"pv{s}_{h}_{qi}_{kb}")
                P.add("V", p + f"o{s}_{h}_{qi}")
        if ATTN_STAGE < 3:
            continue
        for hf in range(2):
            plan_load(p + f"wo{s}_{hf}", p + f"go{s}_{15}_{3 + 4 * hf}")
        if L0 and ATTN_STAGE >= 3:
            P.add("G", p + f"lwoA{s}")
            P.add("G", p + f"lwoB{s}")
            P.add("G", p + f"lwo{s}")
            lwo_waits[p + f"lwo{s}"] = list(lora_prev)
            lora_prev = [p + f"uoe{s}_{15}", p + f"go{s}_{15}_{7}"]
        for qi in range(16):
            if L0:
                P.add("SY", p + f"xr{s}_{qi}")
            P.add("T", p + f"at{s}_{qi}")
            P.add("V", p + f"ate{s}_{qi}")
            if L0:
                P.add("T", p + f"uo{s}_{qi}")
                P.add("S", p + f"uoe{s}_{qi}")
            for m in range(8):
                P.add("T", p + f"go{s}_{qi}_{m}")
                P.add("V", p + f"ro{s}_{qi}_{m}")
    at_war = {}
    for s in range(2):
        for qi in range(16):
            nm = p + f"at{s}_{qi}"
            at_war[nm] = [_ptc_prev.get(0), _ptc_prev.get(1)] if qi == 0 and s == 0 else None

    prev_hfree = c.hbuf_free
    prev_xfree = c.xln_free
    prev_wbfree = c.wb_free
    prev_pbfree = list(c.pb_free)
    prev_pafree = c.pa_free
    prev_ptfree = c.pt_free
    prev_pofree = c.po_free
    prev_poolfree = c.pool_free
    prev_statfree = c.stat_free
    prev_lorafree = getattr(c, "lora_free", None)
    rfree0 = list(c.rfree)

    pbseq = {0: [], 1: []}
    for s in range(2):
        gidx = 0
        for tl in range(4):
            gt = s * 4 + tl
            for proj in projs:
                for m in range(8):
                    gi = (gt * 24 + projs.index(proj) * 8 + m)
                    pbseq[gi % 2].append((p + f"g{gt}_{proj}_{m}", p + f"ev{gt}_{proj}_{m}"))
        if ATTN_STAGE >= 2:
            for h in range(NH):
                for qi in range(16):
                    for kb in range(len(attn_kbs(layer, qi))):
                        pbseq[kb_bank[(s, h, qi, kb)]].append(
                            (p + f"qk{s}_{h}_{qi}_{kb}", p + f"ex{s}_{h}_{qi}_{kb}"))
        if ATTN_STAGE >= 3:
            for qi in range(16):
                for m in range(8):
                    pbseq[(qi * 8 + m) % 2].append((p + f"go{s}_{qi}_{m}",
                                                    p + f"ro{s}_{qi}_{m}"))
    pb_war = {}
    for b in (0, 1):
        prev = None
        for w, r in pbseq[b]:
            pb_war[w] = prev
            prev = r
    pb_last_reader = {b: pbseq[b][-1][1] for b in (0, 1)}

    def tokm(name):
        return None if name is None else P.tok(name)

    def pbwar_tok(writer, bank):
        r = pb_war.get(writer)
        if r is None:
            return prev_pbfree[bank]
        return P.tok(r)

    def pbwar_tok_pb2(s):
        return prev_pbfree[2]

    def proj_bank(gt, proj, m):
        return (gt * 24 + projs.index(proj) * 8 + m) % 2

    def proj_bank(gt, proj, m):
        return (gt * 24 + projs.index(proj) * 8 + m) % 2

    with nc.Block(f"attn{layer}_{emit_attn.count}") as blk:
        def _g(g):
            wc = c.wc["gpsimd"]
            _wait(g, wc, c.G.tok())
            if L0:
                _wait(g, wc, prev_poolfree)
                _wait(g, wc, prev_lorafree)
                ins = g.dma_start(out=c.REZI[:], in_=c.win["W_rez"])
                P.inc("G", ins, p + "rez_ld")
            for s in range(2):
                for tl in range(4):
                    gt = s * 4 + tl
                    for proj in projs:
                        if L0:
                            for wn in lw_waits[p + f"lw{gt}_{proj}"]:
                                _wait(g, wc, P.tok(wn))
                            if gt == 0 and proj == "q":
                                _wait(g, wc, prev_lorafree)
                                _wait(g, wc, prev_poolfree)
                            ins = g.dma_start(out=c.LORA[:], in_=c.win[f"W_a0A{proj}"])
                            P.inc("G", ins, p + f"lwA{gt}_{proj}")
                            ins = g.dma_start(out=c.LORB, in_=c.win[f"W_a0B{proj}"])
                            P.inc("G", ins, p + f"lwB{gt}_{proj}")
                            ins = g.dma_start(out=c.LBIA[:], in_=c.win[f"W_a0b{proj}"])
                            P.inc("G", ins, p + f"lw{gt}_{proj}")
                        if proj != "v" and ATTN_STAGE >= 1 and "s" in ATTN_FEAT:
                            for m in range(8):
                                if (not L0) and "r" in ATTN_FEAT:
                                    _wait(g, wc, P.tok(p + f"rp{gt}_{proj}_{m}"))
                                else:
                                    _wait(g, wc, P.tok(p + f"ev{gt}_{proj}_{m}"))
                            dst = (c.q_sp if proj == "q" else c.k_sp)[s]
                            for ch in range(NCH):
                                ins = g.dma_start(
                                    out=dst[ch, :, tl * 512:(tl + 1) * 512],
                                    in_=STG[:, ch, :])
                                if ch < NCH - 1:
                                    P.inc("G", ins, p + f"sp{gt}_{proj}c{ch}")
                            P.inc("G", ins, p + f"sp{gt}_{proj}")
                    if L0 and ATTN_STAGE >= 1 and "s" in ATTN_FEAT:
                        _wait(g, wc, P.tok(p + f"xln{gt}"))
                        for ch in range(NCH):
                            ins = g.dma_start(
                                out=c.xln_spd[s][ch, :, tl * 512:(tl + 1) * 512],
                                in_=c.XLN[:, ch, :])
                            if ch < NCH - 1:
                                P.inc("G", ins, p + f"spx{gt}c{ch}")
                        P.inc("G", ins, p + f"spx{gt}")
                if L0 and ATTN_STAGE >= 3:
                    for wn in lwo_waits[p + f"lwo{s}"]:
                        _wait(g, wc, P.tok(wn))
                    ins = g.dma_start(out=c.LORA[:], in_=c.win["W_a0Ao"])
                    P.inc("G", ins, p + f"lwoA{s}")
                    ins = g.dma_start(out=c.LORB, in_=c.win["W_a0Bo"])
                    P.inc("G", ins, p + f"lwoB{s}")
                    ins = g.dma_start(out=c.LBIA[:], in_=c.win["W_a0bo"])
                    P.inc("G", ins, p + f"lwo{s}")
        blk.gpsimd(_g)

        def _sy(sy):
            wc = c.wc["sync"]
            for s in range(2):
                if not L0:
                    if s > 0 and ATTN_STAGE >= 1 and "r" in ATTN_FEAT:
                        _wait(sy, wc, P.tok(p + f"rp{(s - 1) * 4 + 3}_k_{7}"))
                    else:
                        _wait(sy, wc, prev_hfree)
                    ins = sy.dma_start(out=CS2v[:, 0, :], in_=c.cin["C_cos"])
                    P.inc("SY", ins, p + f"cs2a_{s}")
                    ins = sy.dma_start(out=CS2v[:, 1, :], in_=c.cin["C_sin"])
                    P.inc("SY", ins, p + f"cs2_{s}")
                for tl in range(4):
                    gt = s * 4 + tl
                    for proj in projs:
                        for hf in range(2):
                            lname = p + f"wl{gt}_{proj}_{hf}"
                            war = ld_war[lname]
                            _wait(sy, wc, P.tok(war) if war else prev_wbfree)
                            _wait(sy, wc, c.cast_tok[f"W_a{layer}{proj}"])
                            sl = wsc[proj][hf * 4:(hf + 1) * 4]
                            dst = c.WB[:, ld_slot[lname]].rearrange("p (m ko) -> p m ko", m=4)
                            ins = sy.dma_start(out=dst,
                                               in_=sl.rearrange("m p k o -> p m (k o)"))
                            P.inc("SY", ins, lname)
                for h in (range(NH) if ATTN_STAGE >= 2 else []):
                    if h >= 2:
                        pq = len(attn_kbs(layer, 15))
                        _wait(sy, wc, P.tok(p + f"qk{s}_{h - 2}_{15}_{pq - 1}"))
                    elif s > 0:
                        pq = len(attn_kbs(layer, 15))
                        _wait(sy, wc, P.tok(p + f"qk{s - 1}_{14 + h}_{15}_{pq - 1}"))
                    else:
                        _wait(sy, wc, prev_hfree)
                    _wait(sy, wc, P.tok(p + f"sp{s * 4 + 3}_q"))
                    ins = sy.dma_start(out=QAv[0:64, h % 2, :],
                                       in_=c.q_sp[s][h // 2, 64 * (h % 2):64 * (h % 2) + 64, :])
                    P.inc("SY", ins, p + f"hq{s}_{h}")
                    _wait(sy, wc, P.tok(p + f"sp{s * 4 + 3}_k"))
                    ins = sy.dma_start(out=KAv[0:64, h % 2, :],
                                       in_=c.k_sp[s][h // 2, 64 * (h % 2):64 * (h % 2) + 64, :])
                    P.inc("SY", ins, p + f"hk{s}_{h}")
                    if L0:
                        ins = sy.dma_start(out=KAv[64:67, h % 2, :], in_=c.cin["C_alibi"][h])
                        P.inc("SY", ins, p + f"ha{s}_{h}")
                for hf in (range(2) if ATTN_STAGE >= 3 else []):
                    lname = p + f"wo{s}_{hf}"
                    war = ld_war[lname]
                    _wait(sy, wc, P.tok(war) if war else prev_wbfree)
                    if ATTN_STAGE >= 2:
                        lkb = len(attn_kbs(layer, 15)) - 1
                        _wait(sy, wc, P.tok(p + f"pv{s}_{15}_{15}_{lkb}"))
                    _wait(sy, wc, c.cast_tok[f"W_a{layer}o"])
                    sl = wsc["o"][hf * 4:(hf + 1) * 4]
                    dst = c.WB[:, ld_slot[lname]].rearrange("p (m ko) -> p m ko", m=4)
                    ins = sy.dma_start(out=dst, in_=sl.rearrange("m p k o -> p m (k o)"))
                    P.inc("SY", ins, lname)
                if L0 and ATTN_STAGE >= 3:
                    for qi in range(16):
                        if qi > 0:
                            _wait(sy, wc, P.tok(p + f"ro{s}_{qi - 1}_{7}"))
                        else:
                            _wait(sy, wc, P.tok(p + f"spx{s * 4 + 3}"))
                            _wait(sy, wc, P.tok(p + f"g{s * 4 + 3}_v_{7}"))
                            if L0:
                                _wait(sy, wc, P.tok(p + f"u{s * 4 + 3}_v"))
                        ins = sy.dma_start(
                            out=c.XLN[:, :, 0:128],
                            in_=c.xln_spd[s][:, :, qi * 128:(qi + 1) * 128].rearrange(
                                "m q t -> q m t"))
                        P.inc("SY", ins, p + f"xr{s}_{qi}")
        blk.sync(_sy)

        def _s(sc):
            wc = c.wc["scalar"]
            for s in range(2):
                for tl in range(4):
                    gt = s * 4 + tl
                    ln_scalar(c, P, p, sc, gt,
                              prev_sq_reader=(None if gt > 0 else prev_statfree))
                    for proj in projs:
                        if L0:
                            _wait(sc, wc, P.tok(p + f"u{gt}_{proj}"))
                            ins = sc.activation(out=c.USB[:, :], in_=c.PA4[0:8, :],
                                                func=AF.Identity, scale=0.125)
                            P.inc("S", ins, p + f"ue{gt}_{proj}")
                        for m in range(8):
                            _wait(sc, wc, P.tok(p + f"g{gt}_{proj}_{m}"))
                            if ATTN_STAGE >= 1 and "s" in ATTN_FEAT:
                                if proj == "k":
                                    _wait(sc, wc, P.tok(p + f"sp{gt}_q"))
                                elif proj == "v":
                                    _wait(sc, wc, P.tok(p + f"sp{gt}_k"))
                            pb = c.PB[proj_bank(gt, proj, m)]
                            if L0:
                                _wait(sc, wc, P.tok(p + f"lw{gt}_{proj}"))
                                ins = sc.activation(out=STG[:, m, :], in_=pb[:],
                                                    func=AF.Identity,
                                                    bias=c.LBIA[:, m:m + 1])
                            else:
                                ins = sc.activation(out=STG[:, m, :], in_=pb[:],
                                                    func=AF.Identity)
                            P.inc("S", ins, p + f"ev{gt}_{proj}_{m}")
                            if proj != "v" and ATTN_STAGE >= 1 and "n" in ATTN_FEAT:
                                if (not L0) and "r" in ATTN_FEAT:
                                    _wait(sc, wc, P.tok(p + f"rp{gt}_{proj}_{m}"))
                                if m > 0:
                                    _wait(sc, wc, P.tok(p + f"nmm{gt}_{proj}_{m - 1}"))
                                ins = sc.activation(out=OSB[:, 1024:1536],
                                                    in_=STG[:, m, :], func=AF.Square)
                                P.inc("S", ins, p + f"nsq{gt}_{proj}_{m}")
                for h in (range(NH) if ATTN_STAGE >= 2 else []):
                    for qi in range(16):
                        kbs = attn_kbs(layer, qi)
                        for kb, kcs in enumerate(kbs):
                            nm = p + f"ex{s}_{h}_{qi}_{kb}"
                            Lkb = len(kcs) * 128
                            _wait(sc, wc, P.tok(p + f"nbe{s}"))
                            if (not L0) and (qi in kcs):
                                _wait(sc, wc, P.tok(p + f"dg{s}_{h}_{qi}"))
                            else:
                                _wait(sc, wc, P.tok(p + f"qk{s}_{h}_{qi}_{kb}"))
                            if psc_war[nm] is None:
                                _wait(sc, wc, prev_wbfree)
                            _wait(sc, wc, P.tok(p + f"g{s * 4 + 3}_v_{7}"))
                            _wait(sc, wc, tokm(psc_war[nm]))
                            _wait(sc, wc, tokm(acc_war[nm]))
                            ins = sc.activation(out=PSC[:, kb * 512:kb * 512 + Lkb],
                                                in_=c.PB[kb_bank[(s, h, qi, kb)]][:, 0:Lkb],
                                                func=AF.Exp,
                                                scale=0.125, bias=c.ABIA[:, h:h + 1],
                                                accum_out=c.ACC4[:, kb:kb + 1])
                            P.inc("S", ins, nm)
                if L0 and ATTN_STAGE >= 3:
                    for qi in range(16):
                        _wait(sc, wc, P.tok(p + f"uo{s}_{qi}"))
                        ins = sc.activation(out=c.USB[:, 0:128], in_=c.PA4[0:8, 0:128],
                                            func=AF.Identity, scale=0.125)
                        P.inc("S", ins, p + f"uoe{s}_{qi}")
        blk.scalar(_s)

        def _v(v):
            wc = c.wc["vector"]
            if L0:
                _wait(v, wc, P.tok(p + "rez_mm"))
                ins = v.tensor_copy(out=c.REZ[:], in_=c.PA4[:, 0:1])
                P.inc("V", ins, p + "rez_ev")
                _wait(v, wc, prev_hfree)
                for b in range(2):
                    ins = v.memset(QAv[64:67, b, :], 1.0)
                    P.inc("V", ins, p + f"ones{b}")
            for s in range(2):
                for tl in range(4):
                    gt = s * 4 + tl
                    if gt == 0:
                        xw = [prev_xfree]
                    else:
                        xw = [P.tok(p + f"g{gt - 1}_v_{7}")]
                        if ATTN_STAGE >= 1 and "s" in ATTN_FEAT:
                            xw.append(P.tok(p + f"spx{gt - 1}") if L0
                                      else P.tok(p + f"sp{gt - 1}_k"))
                        if L0:
                            xw.append(P.tok(p + f"u{gt - 1}_v"))
                    for w in xw:
                        _wait(v, wc, w)
                    ln_vector(c, P, p, v, gt, xln_war=None)
                    for proj in (projs if ATTN_STAGE >= 1 else []):
                        for m in range(8):
                            if (not L0) and proj != "v" and "r" in ATTN_FEAT:
                                _wait(v, wc, P.tok(p + f"sw{gt}_{proj}_{m}"))
                                _wait(v, wc, P.tok(p + f"ev{gt}_{proj}_{m}"))
                                tloc = tl
                                cosl = CS2v[:, 0, tloc * 512:(tloc + 1) * 512]
                                sinl = CS2v[:, 1, tloc * 512:(tloc + 1) * 512]
                                v.tensor_tensor(out=OSB[:, 0:512], in0=STG[:, m, :],
                                                in1=cosl, op=ALU.mult)
                                v.tensor_tensor(out=OSB[:, 512:1024], in0=c.PA5[:],
                                                in1=sinl, op=ALU.mult)
                                ins = v.tensor_tensor(out=STG[:, m, :], in0=OSB[:, 0:512],
                                                      in1=OSB[:, 512:1024], op=ALU.add)
                                P.inc("V", ins, p + f"rp{gt}_{proj}_{m}")
                            if proj != "v":
                                if "n" not in ATTN_FEAT:
                                    continue
                                _wait(v, wc, P.tok(p + f"nmm{gt}_{proj}_{m}"))
                                col = projs.index(proj) * 32 + m * 4 + tl
                                ins = v.reduce_max(out=c.NRM2[0:2, col:col + 1],
                                                   in_=c.PO3[0:2, :], axis=AX.X)
                                P.inc("V", ins, p + f"nrd{gt}_{proj}_{m}")
                            else:
                                if "t" not in ATTN_FEAT:
                                    continue
                                _wait(v, wc, P.tok(p + f"vt{gt}_{m}"))
                                dstv = VT[:, 2 * m:2 * m + 2, tl * 4:(tl + 1) * 4, :]
                                ins = v.tensor_copy(
                                    out=dstv,
                                    in_=c.PT2[:, (m % 2) * 512:(m % 2) * 512 + 512]
                                    .rearrange("p (k h d) -> p h k d", k=4, h=2))
                                P.inc("V", ins, p + f"vte{gt}_{m}")
                if ATTN_STAGE < 1 or "b" not in ATTN_FEAT:
                    continue
                for proj in ["q", "k"]:
                    for m in range(8):
                        _wait(v, wc, P.tok(p + f"nrd{s * 4 + 3}_{proj}_{m}"))
                ins = v.engine_nop()
                _selfsync(c, v, ins)
                for j in range(16):
                    pi, m = divmod(j, 8)
                    ins = v.reduce_max(out=c.QM[0:2, j:j + 1],
                                       in_=c.NRM2[0:2, pi * 32 + m * 4:pi * 32 + m * 4 + 4],
                                       axis=AX.X)
                _selfsync(c, v, ins)
                v.tensor_tensor(out=c.BHE[0:2, 0:8], in0=c.QM[0:2, 0:8],
                                in1=c.QM[0:2, 8:16], op=ALU.add)
                ins = v.tensor_scalar(out=c.BHE[0:2, 0:8], in0=c.BHE[0:2, 0:8],
                                      scalar1=-1.0 / 16.0, scalar2=None, op0=ALU.mult)
                _selfsync(c, v, ins)
                ins = v.tensor_copy(out=c.QM[0:2, 0:16].rearrange("p (a b) -> p a b", b=2),
                                    in_=c.BHE[0:2, 0:8].unsqueeze(2).broadcast_to([2, 8, 2]))
                _selfsync(c, v, ins)
                if L0:
                    ins = v.tensor_tensor(out=c.QM[0:2, 0:16], in0=c.QM[0:2, 0:16],
                                          in1=c.M2[0:2, :], op=ALU.mult)
                    _selfsync(c, v, ins)
                    ins = v.tensor_tensor(out=c.QM[0:2, 0:16], in0=c.QM[0:2, 0:16],
                                          in1=c.AB2[0:2, :], op=ALU.add)
                else:
                    ins = v.tensor_tensor(out=c.QM[0:2, 0:16], in0=c.QM[0:2, 0:16],
                                          in1=c.M2[0:2, :], op=ALU.mult)
                P.inc("V", ins, p + f"nb{s}")
                _wait(v, wc, P.tok(p + f"nbm{s}"))
                ins = v.tensor_copy(out=c.ABIA[:], in_=c.PA4[:, 0:16])
                P.inc("V", ins, p + f"nbe{s}")
                for h in (range(NH) if ATTN_STAGE >= 2 else []):
                    for qi in range(16):
                        kbs = attn_kbs(layer, qi)
                        for kb, kcs in enumerate(kbs):
                            if (not L0) and kb == len(kbs) - 1:
                                dkb = qi // 4
                                db = kb_bank[(s, h, qi, dkb)]
                                dcol = (qi - 4 * dkb) * 128
                                _wait(v, wc, P.tok(p + f"qk{s}_{h}_{qi}_{dkb}"))
                                ins = v.tensor_tensor(out=c.PB[db][:, dcol:dcol + 128],
                                                      in0=c.PB[db][:, dcol:dcol + 128],
                                                      in1=c.MASKC[:], op=ALU.add)
                                P.inc("V", ins, p + f"dg{s}_{h}_{qi}")
                            pte = p + f"pte{s}_{h}_{qi}_{kb}"
                            Lkb = len(kcs) * 128
                            _wait(v, wc, P.tok(p + f"pt{s}_{h}_{qi}_{kb}"))
                            _wait(v, wc, tokm(pts_war[pte]))
                            ins = v.tensor_copy(
                                out=PTS[:, kb * 512:kb * 512 + Lkb],
                                in_=PTB[kb_bank[(s, h, qi, kb)]][:, 0:Lkb])
                            P.inc("V", ins, pte)
                        _wait(v, wc, P.tok(p + f"pv{s}_{h}_{qi}_{len(kbs) - 1}"))
                        if h == 0 and qi == 0 and s == 0:
                            _wait(v, wc, prev_hfree)
                        ins = v.reduce_sum(out=c.RD[:, 0:1], in_=c.ACC4[:, 0:len(kbs)],
                                           axis=AX.X)
                        _selfsync(c, v, ins)
                        ins = v.reciprocal(out=c.RD[:, 1:2], in_=c.RD[:, 0:1])
                        _selfsync(c, v, ins)
                        ins = v.tensor_scalar(out=ATT[:, qi, h * 64:(h + 1) * 64],
                                              in0=c.PO3[:, 0:64], scalar1=c.RD[:, 1:2],
                                              scalar2=None, op0=ALU.mult)
                        P.inc("V", ins, p + f"o{s}_{h}_{qi}")
                for qi in (range(16) if ATTN_STAGE >= 3 else []):
                    _wait(v, wc, P.tok(p + f"at{s}_{qi}"))
                    if qi >= 1:
                        _wait(v, wc, P.tok(p + f"go{s}_{qi - 1}_{7}"))
                    elif s == 0:
                        _wait(v, wc, prev_poolfree)
                    ins = v.tensor_copy(out=c.AF_[:], in_=c.PT2[:].rearrange(
                        "p (a b) -> p a b", a=8))
                    P.inc("V", ins, p + f"ate{s}_{qi}")
                    goff = s * SEQ + qi * 128
                    for m in range(8):
                        _wait(v, wc, P.tok(p + f"go{s}_{qi}_{m}"))
                        _wait(v, wc, rfree0[goff // 512])
                        pb = c.PB[(qi * 8 + m) % 2]
                        if L0:
                            _wait(v, wc, P.tok(p + f"xr{s}_{qi}"))
                            v.tensor_tensor(out=c.R[:, m, goff:goff + 128],
                                            in0=c.R[:, m, goff:goff + 128],
                                            in1=c.XLN[:, m, 0:128], op=ALU.add)
                            ins = v.scalar_tensor_tensor(
                                out=c.R[:, m, goff:goff + 128], in0=pb[:, 0:128],
                                scalar=c.REZ[:, 0:1], in1=c.R[:, m, goff:goff + 128],
                                op0=ALU.mult, op1=ALU.add)
                        else:
                            ins = v.tensor_tensor(out=c.R[:, m, goff:goff + 128],
                                                  in0=c.R[:, m, goff:goff + 128],
                                                  in1=pb[:, 0:128], op=ALU.add)
                        P.inc("V", ins, p + f"ro{s}_{qi}_{m}")
        blk.vector(_v)

        def _t(t):
            wc = c.wc["tensor"]
            if L0:
                _wait(t, wc, P.tok(p + "rez_ld"))
                _wait(t, wc, prev_pafree)
                ins = t.matmul(out=c.PA4[:, 0:1], lhsT=c.ONES1F[:], rhs=c.REZI[:],
                               start=True, stop=True)
                P.inc("T", ins, p + "rez_mm")
            for s in range(2):
                for tl in range(4):
                    gt = s * 4 + tl
                    aux = [P.tok(p + "rez_ev")] if (L0 and gt == 0) else []
                    if gt > 0:
                        if L0:
                            aux.append(P.tok(p + f"ue{gt - 1}_v"))
                        else:
                            aux.append(P.tok(p + f"xln{gt - 1}"))
                        if (not L0) and ATTN_STAGE >= 1 and "r" in ATTN_FEAT:
                            aux.append(P.tok(p + f"rp{gt - 1}_k_{7}"))
                    elif s == 0:
                        aux.append(prev_pafree)
                        aux.append(prev_statfree)
                    else:
                        pass
                    if s > 0 and tl == 0:
                        if L0 and ATTN_STAGE >= 3:
                            aux.append(P.tok(p + f"uoe{s - 1}_{15}"))
                        if ATTN_STAGE >= 1 and "b" in ATTN_FEAT:
                            aux.append(P.tok(p + f"nbe{s - 1}"))
                    for a in aux:
                        _wait(t, wc, a)
                    ln_tensor(c, P, p, t, gt, aux_war=None)
                    ln_tensor_bcast(c, P, p, t, gt)
                    for proj in projs:
                        if L0:
                            _wait(t, wc, P.tok(p + f"lw{gt}_{proj}"))
                            _wait(t, wc, P.tok(p + f"xln{gt}"))
                            if projs.index(proj) > 0:
                                _wait(t, wc, P.tok(p + f"ue{gt}_{projs[projs.index(proj) - 1]}"))
                            for k in range(8):
                                ins = t.matmul(out=c.PA4[0:8, :], lhsT=c.LORA[:, k, :],
                                               rhs=c.XLN[:, k, :], start=(k == 0),
                                               stop=(k == 7))
                            P.inc("T", ins, p + f"u{gt}_{proj}")
                        for m in range(8):
                            gname = p + f"g{gt}_{proj}_{m}"
                            hf = m // 4
                            lname = p + f"wl{gt}_{proj}_{hf}"
                            _wait(t, wc, P.tok(lname))
                            _wait(t, wc, P.tok(p + f"xln{gt}"))
                            _wait(t, wc, pbwar_tok(gname, proj_bank(gt, proj, m)))
                            if L0:
                                _wait(t, wc, P.tok(p + f"ue{gt}_{proj}"))
                            lhs = wl_lhsT_mk(c, ld_slot[lname], m)
                            pb = c.PB[proj_bank(gt, proj, m)]
                            for k in range(8):
                                ins = t.matmul(out=pb[:], lhsT=lhs[k], rhs=c.XLN[:, k, :],
                                               start=(k == 0), stop=(False if L0 else k == 7))
                            if L0:
                                ins = t.matmul(out=pb[:], lhsT=c.LORB[:, m, :],
                                               rhs=c.USB[:, :], start=False, stop=True)
                            P.inc("T", ins, gname)
                            if ATTN_STAGE < 1:
                                continue
                            if (not L0) and proj != "v" and "r" in ATTN_FEAT:
                                _wait(t, wc, P.tok(p + f"ev{gt}_{proj}_{m}"))
                                _wait(t, wc, P.tok(p + f"cs2_{s}"))
                                if m > 0:
                                    _wait(t, wc, P.tok(p + f"rp{gt}_{proj}_{m - 1}"))
                                elif not (gt == 0 and proj == "q"):
                                    pp = "q" if proj == "k" else "k"
                                    pgt = gt if proj == "k" else gt - 1
                                    _wait(t, wc, P.tok(p + f"rp{pgt}_{pp}_{7}"))
                                ins = t.matmul(out=c.PA5[:], lhsT=c.SWT[:],
                                               rhs=STG[:, m, :], start=True, stop=True)
                                P.inc("T", ins, p + f"sw{gt}_{proj}_{m}")
                            if proj != "v":
                                if "n" not in ATTN_FEAT:
                                    continue
                                _wait(t, wc, P.tok(p + f"nsq{gt}_{proj}_{m}"))
                                if m > 0:
                                    _wait(t, wc, P.tok(p + f"nrd{gt}_{proj}_{m - 1}"))
                                else:
                                    prevnrd = None
                                    if proj == "k":
                                        prevnrd = p + f"nrd{gt}_q_{7}"
                                    elif gt > 0:
                                        prevnrd = p + f"nrd{gt - 1}_k_{7}"
                                    elif s > 0:
                                        prevnrd = None
                                    if prevnrd:
                                        _wait(t, wc, P.tok(prevnrd))
                                    elif s > 0:
                                        _wait(t, wc, P.tok(p + f"nb{s - 1}"))
                                    else:
                                        _wait(t, wc, prev_pofree)
                                ins = t.matmul(out=c.PO3[0:2, :], lhsT=c.H2[:],
                                               rhs=OSB[:, 1024:1536], start=True, stop=True)
                                P.inc("T", ins, p + f"nmm{gt}_{proj}_{m}")
                            else:
                                if "t" not in ATTN_FEAT:
                                    continue
                                _wait(t, wc, P.tok(p + f"ev{gt}_v_{m}"))
                                if m >= 1:
                                    _wait(t, wc, P.tok(p + f"vte{gt}_{m - 1}"))
                                elif gt > 0:
                                    _wait(t, wc, P.tok(p + f"vte{gt - 1}_{7}"))
                                elif s == 0:
                                    _wait(t, wc, prev_ptfree)
                                for kc in range(4):
                                    ins = t.transpose(
                                        out=c.PT2[:, (m % 2) * 512 + kc * 128:
                                                  (m % 2) * 512 + (kc + 1) * 128],
                                        in_=STG[:, m, kc * 128:(kc + 1) * 128],
                                        identity=c.IDENT[:])
                                P.inc("T", ins, p + f"vt{gt}_{m}")
                if ATTN_STAGE < 1 or "b" not in ATTN_FEAT:
                    continue
                _wait(t, wc, P.tok(p + f"nb{s}"))
                if L0:
                    _wait(t, wc, P.tok(p + f"ue{s * 4 + 3}_v"))
                ins = t.matmul(out=c.PA4[:, 0:16], lhsT=c.ONES2F[:], rhs=c.QM[0:2, 0:16],
                               start=True, stop=True)
                P.inc("T", ins, p + f"nbm{s}")
                for h in (range(NH) if ATTN_STAGE >= 2 else []):
                    for qi in range(16):
                        kbs = attn_kbs(layer, qi)
                        for kb, kcs in enumerate(kbs):
                            Lkb = len(kcs) * 128
                            qkn = p + f"qk{s}_{h}_{qi}_{kb}"
                            _wait(t, wc, P.tok(p + f"hq{s}_{h}"))
                            _wait(t, wc, P.tok(p + f"hk{s}_{h}"))
                            if L0:
                                _wait(t, wc, P.tok(p + f"ha{s}_{h}"))
                                _wait(t, wc, P.tok(p + "ones0"))
                                _wait(t, wc, P.tok(p + "ones1"))
                            _wait(t, wc, pbwar_tok(qkn, kb_bank[(s, h, qi, kb)]))
                            ins = t.matmul(out=c.PB[kb_bank[(s, h, qi, kb)]][:, 0:Lkb],
                                           lhsT=QAv[0:KR, h % 2, qi * 128:(qi + 1) * 128],
                                           rhs=KAv[0:KR, h % 2,
                                                   kcs[0] * 128:kcs[0] * 128 + Lkb],
                                           start=True, stop=True)
                            P.inc("T", ins, qkn)
                            ptn = p + f"pt{s}_{h}_{qi}_{kb}"
                            _wait(t, wc, P.tok(p + f"ex{s}_{h}_{qi}_{kb}"))
                            w_ = ptc_war[ptn]
                            _wait(t, wc, P.tok(p + f"vte{s * 4 + 3}_{7}"))
                            if w_ is None:
                                _wait(t, wc, prev_ptfree)
                                _wait(t, wc, pbwar_tok_pb2(s))
                            else:
                                _wait(t, wc, P.tok(w_))
                            dstb = PTB[kb_bank[(s, h, qi, kb)]]
                            for j in range(len(kcs)):
                                ins = t.transpose(
                                    out=dstb[:, j * 128:(j + 1) * 128],
                                    in_=PSC[:, kb * 512 + j * 128:kb * 512 + j * 128 + 128],
                                    identity=c.IDENT[:])
                            P.inc("T", ins, ptn)
                            _wait(t, wc, P.tok(p + f"pte{s}_{h}_{qi}_{kb}"))
                            if kb == 0:
                                _wait(t, wc, P.tok(p + f"vte{s * 4 + 3}_{7}"))
                                _wait(t, wc, P.tok(p + f"nrd{s * 4 + 3}_k_{7}"))
                                if qi > 0:
                                    _wait(t, wc, P.tok(p + f"o{s}_{h}_{qi - 1}"))
                                elif h > 0:
                                    _wait(t, wc, P.tok(p + f"o{s}_{h - 1}_{15}"))
                                elif s > 0:
                                    _wait(t, wc, P.tok(p + f"o{s - 1}_{15}_{15}"))
                                else:
                                    _wait(t, wc, prev_pofree)
                            for j, kc in enumerate(kcs):
                                ins = t.matmul(out=c.PO3[:, 0:64],
                                               lhsT=PTS[:, (kb * 4 + j) * 128:
                                                        (kb * 4 + j) * 128 + 128],
                                               rhs=VT[:, h, kc, :],
                                               start=(kb == 0 and j == 0),
                                               stop=(kb == len(kbs) - 1 and
                                                     j == len(kcs) - 1))
                            P.inc("T", ins, p + f"pv{s}_{h}_{qi}_{kb}")
                for qi in (range(16) if ATTN_STAGE >= 3 else []):
                    atn = p + f"at{s}_{qi}"
                    _wait(t, wc, P.tok(p + f"o{s}_{15}_{qi}"))
                    if qi > 0:
                        _wait(t, wc, P.tok(p + f"ate{s}_{qi - 1}"))
                    else:
                        lkb = len(attn_kbs(layer, 15)) - 1
                        _wait(t, wc, P.tok(p + f"pte{s}_{15}_{15}_{lkb}"))
                    for m in range(8):
                        ins = t.transpose(out=c.PT2[:, m * 128:(m + 1) * 128],
                                          in_=ATT[:, qi, m * 128:(m + 1) * 128],
                                          identity=c.IDENT[:])
                    P.inc("T", ins, atn)
                    if L0:
                        _wait(t, wc, P.tok(p + f"lwo{s}"))
                        _wait(t, wc, P.tok(p + f"ate{s}_{qi}"))
                        if qi > 0:
                            _wait(t, wc, P.tok(p + f"uoe{s}_{qi - 1}"))
                        else:
                            _wait(t, wc, P.tok(p + f"nbe{s}"))
                        for k in range(8):
                            ins = t.matmul(out=c.PA4[0:8, 0:128], lhsT=c.LORA[:, k, :],
                                           rhs=c.AF_[:, k, :], start=(k == 0), stop=(k == 7))
                        P.inc("T", ins, p + f"uo{s}_{qi}")
                    for m in range(8):
                        gname = p + f"go{s}_{qi}_{m}"
                        hf = m // 4
                        lname = p + f"wo{s}_{hf}"
                        _wait(t, wc, P.tok(lname))
                        _wait(t, wc, P.tok(p + f"ate{s}_{qi}"))
                        _wait(t, wc, pbwar_tok(gname, (qi * 8 + m) % 2))
                        if L0:
                            _wait(t, wc, P.tok(p + f"uoe{s}_{qi}"))
                        lhs = wl_lhsT_mk(c, ld_slot[lname], m)
                        pb = c.PB[(qi * 8 + m) % 2]
                        for k in range(8):
                            ins = t.matmul(out=pb[:, 0:128], lhsT=lhs[k],
                                           rhs=c.AF_[:, k, :], start=(k == 0),
                                           stop=(False if L0 else k == 7))
                        if L0:
                            ins = t.matmul(out=pb[:, 0:128], lhsT=c.LORB[:, m, :],
                                           rhs=c.USB[:, 0:128], start=False, stop=True)
                        P.inc("T", ins, gname)
        blk.tensor(_t)

    P.done()
    if ATTN_STAGE >= 3:
        last = P.tok(p + f"ro{1}_{15}_{7}")
    elif ATTN_STAGE == 2:
        last = P.tok(p + f"o{1}_{15}_{15}")
    elif ATTN_STAGE == 1:
        last = P.tok(p + f"nbe{1}") if "b" in ATTN_FEAT else P.tok(p + f"ev{7}_v_{7}")
    else:
        last = P.tok(p + f"ev{7}_v_{7}")
    for tt in range(NT):
        c.rtok[tt] = last
        c.rfree[tt] = last
    c.xln_free = last
    c.hbuf_free = last
    c.wb_free = last
    c.stat_free = P.tok(p + f"xln{7}")
    for b in range(3):
        c.pb_free[b] = last
    c.pa_free = last
    c.pt_free = last
    c.po_free = last
    c.pool_free = last
    c.lora_free = last if L0 else getattr(c, "lora_free", None)


def wl_lhsT_mk(c, slot, m):
    off = (m % 4) * 1024
    W = c.WB[:, slot]
    return [W[:, off + k * 128: off + (k + 1) * 128] for k in range(8)]




def run_group(group, x_cur, wd):
    nc = build_program(group)
    need = weights_needed(group)
    in_maps = []
    for ci in range(NCORES):
        m = {"x": x_cur[ci]}
        for n in WSHAPES:
            if n in need:
                m[n] = wd[n]
        for n in CSHAPES:
            m[n] = wd[n]
        in_maps.append(m)
    res = run_bass_kernel_spmd(nc, in_maps, core_ids=list(range(NCORES)))
    return [np.asarray(res.results[ci]["y"]) for ci in range(NCORES)]


def kernel(**inputs):
    wd = host_prepare(inputs)
    xs = shard_x(np.asarray(inputs["L0_attn_x"], np.float32))
    groups = [SUBLAYERS]
    x_cur = xs
    for group in groups:
        x_cur = run_group(group, x_cur, wd)
    return unshard_y(x_cur)


# version 04
# speedup vs baseline: 1.1128x; 1.0345x over previous
import numpy as np
import ml_dtypes
import concourse.bass as bass
import concourse.mybir as mybir
from concourse.bass_utils import run_bass_kernel_spmd

F32 = mybir.dt.float32
BF16 = mybir.dt.bfloat16
U8 = mybir.dt.uint8
AF = mybir.ActivationFunctionType
ALU = mybir.AluOpType
AX = mybir.AxisListType
BF = ml_dtypes.bfloat16

NCORES = 8
SEQ = 2048
DM = 1024
NC_TOK = 2 * SEQ
NT = NC_TOK // 512
NCH = DM // 128
NH = 16
HD = 64
EPS = 1e-5

SUBLAYERS = ["moe0a", "attn0", "moe0b", "ffn1a", "attn1", "ffn1b"]
ATTN_STAGE = 3
ATTN_FEAT = "rntsb"



def tile_lhsT(wT):
    Kt, Mt = wT.shape
    a = wT.reshape(Kt // 128, 128, Mt // 128, 128)
    return np.ascontiguousarray(a.transpose(2, 1, 0, 3))


def host_prepare(g):
    d = {}
    ew = np.asarray(g["L0_ffn_expert_w"], np.float32)
    t = np.stack([tile_lhsT(ew[e]) for e in range(8)])
    d["W_exp"] = np.ascontiguousarray(t.transpose(0, 2, 1, 3, 4))
    d["W_expb"] = np.ascontiguousarray(
        np.asarray(g["L0_ffn_expert_b"], np.float32).reshape(8, 8, 128).transpose(1, 0, 2))
    d["W_rw"] = np.ascontiguousarray(
        np.asarray(g["L0_ffn_router_w"], np.float32).reshape(8, 128, 8).transpose(1, 0, 2))
    d["W_rb"] = np.asarray(g["L0_ffn_router_b"], np.float32).reshape(1, 8)
    for n in ["q", "k", "v", "o"]:
        W = np.asarray(g[f"L0_attn_W{n}"], np.float32)
        d[f"W_a0{n}"] = tile_lhsT(np.ascontiguousarray(W.T))
        A = np.asarray(g[f"L0_attn_A{n}"], np.float32)
        d[f"W_a0A{n}"] = np.ascontiguousarray(A.T.reshape(8, 128, 8).transpose(1, 0, 2))
        Bm = np.asarray(g[f"L0_attn_B{n}"], np.float32)
        d[f"W_a0B{n}"] = np.ascontiguousarray(Bm.T.reshape(8, 8, 128))
        b = np.asarray(g[f"L0_attn_b{n}"], np.float32)
        d[f"W_a0b{n}"] = np.ascontiguousarray(b.reshape(8, 128).T)
        W1 = np.asarray(g[f"L1_attn_w{n}"], np.float32)
        d[f"W_a1{n}"] = tile_lhsT(W1)
    d["W_rez"] = np.asarray(g["L0_attn_rezero"], np.float32).reshape(1, 1)
    for n in ["w1", "w2", "h1", "h2", "w3"]:
        W = np.asarray(g[f"L1_ffn_{n}_w"], np.float32)
        d[f"W_f{n}"] = tile_lhsT(np.ascontiguousarray(W.T))
        s = np.asarray(g[f"L1_ffn_{n}_s"], np.float32)
        d[f"W_f{n}s"] = np.ascontiguousarray(s.reshape(-1, 128).T)
    d["C_ones"] = np.ones((128, 1), BF)
    d["C_ones1"] = np.ones((1, 128), BF)
    d["C_ones1f"] = np.ones((1, 128), np.float32)
    d["C_ident"] = np.eye(128, dtype=BF)
    inv = 1.0 / (10000.0 ** (np.arange(0, HD, 2, dtype=np.float64) / HD))
    ang = np.arange(SEQ, dtype=np.float64)[:, None] * inv[None, :]
    c64 = np.repeat(np.cos(ang).T, 2, axis=0)
    s64 = np.repeat(np.sin(ang).T, 2, axis=0)
    d["C_cos"] = np.concatenate([c64, c64], 0).astype(BF)
    d["C_sin"] = np.concatenate([s64, s64], 0).astype(BF)
    S = np.zeros((HD, HD), np.float32)
    for i in range(0, HD, 2):
        S[i, i + 1] = -1.0
        S[i + 1, i] = 1.0
    S128 = np.zeros((128, 128), np.float32)
    S128[:64, :64] = S
    S128[64:, 64:] = S
    d["C_swapT"] = np.ascontiguousarray(S128.T).astype(BF)
    slopes = 0.5 ** np.arange(NH)
    al = np.zeros((NH, 3, SEQ), np.float64)
    kpos = np.arange(SEQ, dtype=np.float64)
    for h in range(NH):
        v = np.asarray(8.0 * slopes[h] * kpos, np.float32).astype(np.float64)
        hi = np.asarray(v, np.float32).astype(BF).astype(np.float64)
        mid = np.asarray(v - hi, np.float32).astype(BF).astype(np.float64)
        al[h, 0], al[h, 1], al[h, 2] = hi, mid, v - hi - mid
    d["C_alibi"] = al.astype(BF)
    m = np.zeros((128, 128), np.float32)
    m[np.triu_indices(128, 1)] = -30000.0
    d["C_mask"] = m.astype(BF)
    d["C_identf"] = np.eye(128, dtype=np.float32)
    ones2 = np.ones((2, 128), np.float32)
    d["C_ones2f"] = ones2
    h2 = np.zeros((128, 2), np.float32)
    h2[0:64, 0] = 1.0
    h2[64:128, 1] = 1.0
    d["C_h2"] = h2.astype(BF)
    ab2 = np.zeros((2, 16), np.float32)
    for h in range(16):
        ab2[h % 2, h] = -(0.5 ** h) * 2047.0 * 8.0 * 0.125
    d["C_ab2"] = ab2
    m2 = np.zeros((2, 16), np.float32)
    for h in range(16):
        m2[h % 2, h] = 1.0
    d["C_m2"] = m2
    sel = np.zeros((8, 8, 128), np.float32)
    for e in range(8):
        sel[e, e, :] = 1.0
    d["C_sel"] = np.ascontiguousarray(sel.transpose(1, 0, 2)).astype(BF)
    return d


def shard_x(x):
    return [np.ascontiguousarray(np.asarray(x[2 * c:2 * c + 2], np.float32).transpose(0, 2, 1))
            for c in range(NCORES)]


def unshard_y(parts):
    out = np.empty((16, SEQ, DM), np.float32)
    for c, p in enumerate(parts):
        out[2 * c:2 * c + 2] = np.asarray(p).transpose(0, 2, 1)
    return out



class Sem:
    def __init__(self, h, step=1):
        self.h = h
        self.n = 0
        self.step = step

    def inc(self, instr):
        instr.then_inc(self.h, self.step)
        self.n += self.step
        return (self.h, self.n)

    def tok(self):
        return (self.h, self.n)


class Plan:
    def __init__(self, engs):
        self.sems = dict(engs)
        self.base = {e: s.n for e, s in self.sems.items()}
        self.order = {e: [] for e in self.sems}
        self.idx = {}
        self.emitted = {e: 0 for e in self.sems}

    def add(self, eng, name):
        assert name not in self.idx, name
        self.idx[name] = (eng, len(self.order[eng]))
        self.order[eng].append(name)

    def tok(self, name):
        eng, i = self.idx[name]
        s = self.sems[eng]
        return (s.h, self.base[eng] + (i + 1) * s.step)

    def inc(self, eng, instr, name):
        s = self.sems[eng]
        i = self.emitted[eng]
        assert self.order[eng][i] == name, \
            f"plan order mismatch on {eng}: expected {self.order[eng][i]}, got {name}"
        self.emitted[eng] += 1
        t = s.inc(instr)
        assert t == self.tok(name)
        return t

    def done(self):
        for e in self.sems:
            assert self.emitted[e] == len(self.order[e]), \
                f"engine {e}: emitted {self.emitted[e]} of {len(self.order[e])}"


class Ctx:
    pass


def _selfsync(c, eng, instr):
    t = c.X.inc(instr)
    eng.wait_ge(t[0], t[1])


def _wait(eng, cache, tok):
    if tok is None:
        return
    h, v = tok
    if v <= 0:
        return
    if cache.get(h, 0) >= v:
        return
    eng.wait_ge(h, v)
    cache[h] = v


def weights_needed(sublayers):
    need = set()
    for sl in sublayers:
        if sl.startswith("moe"):
            need |= {"W_exp", "W_expb", "W_rw", "W_rb"}
        if sl == "attn0":
            for n in "qkvo":
                need |= {f"W_a0{n}", f"W_a0A{n}", f"W_a0B{n}", f"W_a0b{n}"}
            need.add("W_rez")
        if sl == "attn1":
            need |= {f"W_a1{n}" for n in "qkvo"}
        if sl.startswith("ffn1"):
            need |= {"W_fw1", "W_fw2", "W_fh1", "W_fh2", "W_fw3",
                     "W_fw1s", "W_fw2s", "W_fh1s", "W_fh2s", "W_fw3s"}
    return need


WSHAPES = {
    "W_exp": [8, 128, 8, 8, 128], "W_expb": [8, 8, 128], "W_rw": [128, 8, 8],
    "W_rb": [1, 8], "W_rez": [1, 1],
    "W_fw1": [32, 128, 8, 128], "W_fw2": [32, 128, 8, 128],
    "W_fh1": [32, 128, 32, 128], "W_fh2": [32, 128, 32, 128], "W_fw3": [8, 128, 32, 128],
    "W_fw1s": [128, 32], "W_fw2s": [128, 32], "W_fh1s": [128, 32],
    "W_fh2s": [128, 32], "W_fw3s": [128, 8],
}
for _n in "qkvo":
    WSHAPES[f"W_a0{_n}"] = [8, 128, 8, 128]
    WSHAPES[f"W_a0A{_n}"] = [128, 8, 8]
    WSHAPES[f"W_a0B{_n}"] = [8, 8, 128]
    WSHAPES[f"W_a0b{_n}"] = [128, 8]
    WSHAPES[f"W_a1{_n}"] = [8, 128, 8, 128]
CSHAPES = {"C_ones": ([128, 1], BF16), "C_ones1": ([1, 128], BF16),
           "C_ident": ([128, 128], BF16), "C_cos": ([128, SEQ], BF16),
           "C_sin": ([128, SEQ], BF16), "C_swapT": ([128, 128], BF16),
           "C_alibi": ([NH, 3, SEQ], BF16), "C_mask": ([128, 128], BF16),
           "C_sel": ([8, 8, 128], BF16), "C_ones1f": ([1, 128], F32),
           "C_identf": ([128, 128], F32), "C_ones2f": ([2, 128], F32),
           "C_h2": ([128, 2], BF16), "C_ab2": ([2, 16], F32), "C_m2": ([2, 16], F32)}
CAST_W = ["W_exp", "W_fw1", "W_fw2", "W_fh1", "W_fh2", "W_fw3",
          "W_a0q", "W_a0k", "W_a0v", "W_a0o", "W_a1q", "W_a1k", "W_a1v", "W_a1o"]


def build_program(sublayers):
    nc = bass.Bass("TRN2", target_bir_lowering=False, dynamic_dma_scratch_size=16384)
    c = Ctx()
    c.nc = nc
    need = weights_needed(sublayers)
    c.need = need

    c.x_in = nc.dram_tensor("x", [2, DM, SEQ], F32, kind="ExternalInput").ap()
    c.y_out = nc.dram_tensor("y", [2, DM, SEQ], F32, kind="ExternalOutput").ap()
    c.win = {n: nc.dram_tensor(n, s, F32, kind="ExternalInput").ap()
             for n, s in WSHAPES.items() if n in need}
    c.cin = {n: nc.dram_tensor(n, s, dt, kind="ExternalInput").ap()
             for n, (s, dt) in CSHAPES.items()}
    c.wsc = {n: nc.dram_tensor(n + "_b", WSHAPES[n], BF16).ap()
             for n in CAST_W if n in need}
    c.q_sp = nc.dram_tensor("q_sp", [2, NCH, 128, SEQ], BF16).ap()
    c.k_sp = nc.dram_tensor("k_sp", [2, NCH, 128, SEQ], BF16).ap()
    c.v_sp = nc.dram_tensor("v_sp", [2, NCH, 128, SEQ], BF16).ap()
    c.xln_spd = nc.dram_tensor("xln_spd", [2, NCH, 128, SEQ], BF16).ap()

    import contextlib
    with contextlib.ExitStack() as st:
        e = st.enter_context
        c.R = e(nc.sbuf_tensor("R", [128, NCH, NC_TOK], BF16))
        c.XLN = e(nc.sbuf_tensor("XLN", [128, NCH, 512], BF16))
        c.HA = e(nc.sbuf_tensor("HA", [128, 32, 512], BF16))
        c.HB = e(nc.sbuf_tensor("HB", [128, 32, 512], BF16))
        c.HC = e(nc.sbuf_tensor("HC", [128, 32, 512], BF16))
        c.WB = e(nc.sbuf_tensor("WB", [128, 2, 4096], BF16))
        c.XSQ = e(nc.sbuf_tensor("XSQ", [128, 2, 512], BF16))
        c.STT = e(nc.sbuf_tensor("STT", [1, 1024], F32))
        c.BCA = e(nc.sbuf_tensor("BCA", [128, 2, 512], BF16))
        c.POOL = e(nc.sbuf_tensor("POOL", [128, 5120], U8))
        c.LG = e(nc.sbuf_tensor("LG", [128, 4, 8], F32))
        c.LGS = e(nc.sbuf_tensor("LGS", [128, 4, 8], F32))
        c.CFB = e(nc.sbuf_tensor("CFB", [128, 4, 8], BF16))
        c.RWS = e(nc.sbuf_tensor("RWS", [128, 8, 8], BF16))
        c.RBS = e(nc.sbuf_tensor("RBS", [1, 8], BF16))
        c.ONES = e(nc.sbuf_tensor("ONES", [128, 1], BF16))
        c.ONES1 = e(nc.sbuf_tensor("ONES1", [1, 128], BF16))
        c.ONES1F = e(nc.sbuf_tensor("ONES1F", [1, 128], F32))
        c.IDENT = e(nc.sbuf_tensor("IDENT", [128, 128], BF16))
        c.MASKC = e(nc.sbuf_tensor("MASKC", [128, 128], BF16))
        c.SWT = e(nc.sbuf_tensor("SWT", [128, 128], BF16))
        c.SCL = e(nc.sbuf_tensor("SCL", [128, 5, 32], F32))
        c.ACC4 = e(nc.sbuf_tensor("ACC4", [128, 8], F32))
        c.RD = e(nc.sbuf_tensor("RD", [128, 4], F32))
        c.LORA = e(nc.sbuf_tensor("LORA", [128, 8, 8], BF16))
        c.LBIA = e(nc.sbuf_tensor("LBIA", [128, 8], F32))
        c.REZ = e(nc.sbuf_tensor("REZ", [128, 1], F32))
        c.OSTM = e(nc.sbuf_tensor("OSTM", [128, 2, 512], F32))
        c.ABIA = e(nc.sbuf_tensor("ABIA", [128, NH], F32))
        c.NRM2 = e(nc.sbuf_tensor("NRM2", [2, 64], F32))
        c.QM = e(nc.sbuf_tensor("QM", [2, 16], F32))
        c.BHE = e(nc.sbuf_tensor("BHE", [2, 16], F32))
        c.IDENTF = e(nc.sbuf_tensor("IDENTF", [128, 128], F32))
        c.ONES2F = e(nc.sbuf_tensor("ONES2F", [2, 128], F32))
        c.H2 = e(nc.sbuf_tensor("H2", [128, 2], BF16))
        c.AB2 = e(nc.sbuf_tensor("AB2", [2, 16], F32))
        c.M2 = e(nc.sbuf_tensor("M2", [2, 16], F32))
        c.REZI = e(nc.sbuf_tensor("REZI", [1, 1], F32))
        pv = c.POOL[:]
        c.CT8 = pv.bitcast(BF16)[0:8, 0:512]
        c.EBS = pv.bitcast(BF16)[0:8, 512:1536].rearrange("r (m o) -> r m o", m=8)
        c.SEL = pv.bitcast(BF16)[0:8, 1536:2560].rearrange("r (s o) -> r s o", s=8)
        c.LORB = pv.bitcast(BF16)[0:8, 0:1024].rearrange("r (m o) -> r m o", m=8)
        c.USB = pv.bitcast(BF16)[0:8, 1024:1536]
        c.AF_ = pv.bitcast(BF16)[:, 1536:2560].rearrange("p (a b) -> p a b", a=8)
        c.PB = [e(nc.psum_tensor(f"PBm{i}", [128, 512], F32)) for i in range(4)]
        c.PA4 = e(nc.psum_tensor("PA4", [128, 512], F32))
        c.PA5 = e(nc.psum_tensor("PA5", [128, 512], F32))
        c.PO3 = e(nc.psum_tensor("PO3", [128, 512], F32))
        c.PT2 = e(nc.psum_tensor("PT2", [128, 1024], BF16))
        c.X = Sem(e(nc.semaphore("s_x")))
        c.SY = Sem(e(nc.semaphore("s_sy")), 16)
        c.G = Sem(e(nc.semaphore("s_g")), 16)
        c.T = Sem(e(nc.semaphore("s_t")))
        c.V = Sem(e(nc.semaphore("s_v")))
        c.S = Sem(e(nc.semaphore("s_s")))
        c.wc = {k: {} for k in ["sync", "gpsimd", "scalar", "vector", "tensor"]}
        emit_all(c, sublayers)
    return nc


def emit_all(c, sublayers):
    nc = c.nc
    c.cast_tok = {}
    with nc.Block("init") as blk:
        def _g(g):
            for sq in range(2):
                for ch in range(NCH):
                    c.rload = c.G.inc(g.dma_start(
                        out=c.R[:, ch, sq * SEQ:(sq + 1) * SEQ],
                        in_=c.x_in[sq, ch * 128:(ch + 1) * 128, :]))
            if "W_rw" in c.win:
                c.G.inc(g.dma_start(out=c.RWS[:], in_=c.win["W_rw"]))
                c.G.inc(g.dma_start(out=c.RBS[:], in_=c.win["W_rb"]))
            if "W_fw1s" in c.win:
                for i, n in enumerate(["W_fw1s", "W_fw2s", "W_fh1s", "W_fh2s"]):
                    c.G.inc(g.dma_start(out=c.SCL[:, i, :], in_=c.win[n]))
                c.G.inc(g.dma_start(out=c.SCL[:, 4, 0:8], in_=c.win["W_fw3s"]))
            c.small_tok = c.G.tok()
            order = [n for n in CAST_W if n in c.wsc]
            for n in order:
                src, dst = c.win[n], c.wsc[n]
                per = int(np.prod(src.shape[1:])) * 4
                step = max(1, (8 << 20) // per)
                tk = None
                for i0 in range(0, src.shape[0], step):
                    i1 = min(src.shape[0], i0 + step)
                    tk = c.G.inc(g.dma_start(out=dst[i0:i1], in_=src[i0:i1]))
                c.cast_tok[n] = tk
        blk.gpsimd(_g)

        def _sy(sy):
            for dst, src in [(c.ONES, c.cin["C_ones"]), (c.ONES1, c.cin["C_ones1"]),
                             (c.ONES1F, c.cin["C_ones1f"]),
                             (c.IDENT, c.cin["C_ident"]), (c.MASKC, c.cin["C_mask"]),
                             (c.SWT, c.cin["C_swapT"]), (c.IDENTF, c.cin["C_identf"]),
                             (c.ONES2F, c.cin["C_ones2f"]), (c.H2, c.cin["C_h2"]),
                             (c.AB2, c.cin["C_ab2"]), (c.M2, c.cin["C_m2"])]:
                c.SY.inc(sy.dma_start(out=dst[:], in_=src))
            c.const_tok = c.SY.tok()
        blk.sync(_sy)

    gfin = c.G.tok()
    c.rload = gfin
    c.small_tok = gfin
    for k in list(c.cast_tok):
        c.cast_tok[k] = gfin
    c.rtok = [c.rload] * NT
    c.rfree = [None] * NT
    c.xln_free = None
    c.hbuf_free = None
    c.wb_free = None
    c.stat_free = None
    c.pool_free = None
    c.pb_free = [None] * 4
    c.pa_free = None
    c.pt_free = None
    c.po_free = None
    final = len(sublayers) - 1

    for i, sl in enumerate(sublayers):
        if sl.startswith("moe"):
            emit_moe(c, is_final=(i == final))
        elif sl.startswith("ffn1"):
            emit_ffn1(c, is_final=(i == final))
        elif sl == "attn0":
            emit_attn(c, 0)
        elif sl == "attn1":
            emit_attn(c, 1)
    if sublayers[-1].startswith("attn"):
        emit_store(c)

    with nc.Block("fin") as blk:
        def _g(g):
            _wait(g, c.wc["gpsimd"], c.G.tok())
        blk.gpsimd(_g)

        def _sy(sy):
            _wait(sy, c.wc["sync"], c.SY.tok())
        blk.sync(_sy)



def plan_ln(P, p, tt):
    for ch in range(NCH):
        P.add("S", f"{p}sq{tt}_{ch}")
    P.add("S", f"{p}rstd{tt}")
    for ch in range(NCH):
        P.add("T", f"{p}sqmm{tt}_{ch}")
    P.add("T", f"{p}bc{tt}")
    P.add("V", f"{p}comb1_{tt}")
    P.add("V", f"{p}comb2_{tt}")
    P.add("V", f"{p}xln{tt}")


def ln_scalar(c, P, p, s, tt, prev_sq_reader=None, rtok=None):
    wc = c.wc["scalar"]
    _wait(s, wc, rtok if rtok is not None else c.rtok[tt])
    if tt == 0 and c.stat_free is not None:
        _wait(s, wc, c.stat_free)
    for ch in range(NCH):
        j = tt * NCH + ch
        if j >= 2:
            t2, c2 = divmod(j - 2, NCH)
            _wait(s, wc, P.tok(f"{p}sqmm{t2}_{c2}"))
        elif prev_sq_reader is not None:
            _wait(s, wc, prev_sq_reader)
        ins = s.activation(out=c.XSQ[:, j % 2, :], in_=c.R[:, ch, tt * 512:(tt + 1) * 512],
                           func=AF.Square)
        P.inc("S", ins, f"{p}sq{tt}_{ch}")
    _wait(s, wc, P.tok(f"{p}comb1_{tt}"))
    if tt > 0:
        _wait(s, wc, P.tok(f"{p}bc{tt - 1}"))
    s.activation(out=c.STT[0:1, 512:1024], in_=c.STT[0:1, 512:1024], func=AF.Ln)
    ins = s.activation(out=c.STT[0:1, 512:1024], in_=c.STT[0:1, 512:1024], func=AF.Exp,
                       scale=-0.5)
    P.inc("S", ins, f"{p}rstd{tt}")


def ln_tensor(c, P, p, t, tt, aux_war=None, rtok=None):
    wc = c.wc["tensor"]
    _wait(t, wc, c.const_tok)
    _wait(t, wc, rtok if rtok is not None else c.rtok[tt])
    _wait(t, wc, aux_war)
    if tt == 0:
        _wait(t, wc, c.stat_free)
        _wait(t, wc, c.pa_free)
    for ch in range(NCH):
        t.matmul(out=c.PA4[0:1, :], lhsT=c.ONES[:], rhs=c.R[:, ch, tt * 512:(tt + 1) * 512],
                 start=(ch == 0), stop=(ch == NCH - 1))
        _wait(t, wc, P.tok(f"{p}sq{tt}_{ch}"))
        ins = t.matmul(out=c.PA5[0:1, :], lhsT=c.ONES[:],
                       rhs=c.XSQ[:, (tt * NCH + ch) % 2, :],
                       start=(ch == 0), stop=(ch == NCH - 1))
        P.inc("T", ins, f"{p}sqmm{tt}_{ch}")


def ln_tensor_bcast(c, P, p, t, tt):
    wc = c.wc["tensor"]
    _wait(t, wc, P.tok(f"{p}comb2_{tt}"))
    _wait(t, wc, P.tok(f"{p}rstd{tt}"))
    t.matmul(out=c.PA4[:], lhsT=c.ONES1F[:], rhs=c.STT[0:1, 512:1024], start=True, stop=True)
    ins = t.matmul(out=c.PA5[:], lhsT=c.ONES1F[:], rhs=c.STT[0:1, 0:512], start=True, stop=True)
    P.inc("T", ins, f"{p}bc{tt}")


def ln_vector(c, P, p, v, tt, xln_war=None, rsl=None):
    wc = c.wc["vector"]
    _wait(v, wc, P.tok(f"{p}sqmm{tt}_{NCH - 1}"))
    if tt > 0:
        _wait(v, wc, P.tok(f"{p}bc{tt - 1}"))
    v.tensor_scalar(out=c.STT[0:1, 0:512], in0=c.PA4[0:1, :], scalar1=1.0 / DM,
                    scalar2=None, op0=ALU.mult)
    v.tensor_scalar(out=c.STT[0:1, 512:1024], in0=c.PA5[0:1, :], scalar1=1.0 / DM,
                    scalar2=None, op0=ALU.mult)
    v.tensor_tensor(out=c.PA4[0:1, :], in0=c.STT[0:1, 0:512], in1=c.STT[0:1, 0:512],
                    op=ALU.mult)
    v.tensor_tensor(out=c.STT[0:1, 512:1024], in0=c.STT[0:1, 512:1024], in1=c.PA4[0:1, :],
                    op=ALU.subtract)
    ins = v.tensor_scalar(out=c.STT[0:1, 512:1024], in0=c.STT[0:1, 512:1024], scalar1=EPS,
                          scalar2=None, op0=ALU.add)
    P.inc("V", ins, f"{p}comb1_{tt}")
    _wait(v, wc, P.tok(f"{p}rstd{tt}"))
    v.tensor_tensor(out=c.STT[0:1, 0:512], in0=c.STT[0:1, 0:512], in1=c.STT[0:1, 512:1024],
                    op=ALU.mult)
    ins = v.tensor_scalar(out=c.STT[0:1, 0:512], in0=c.STT[0:1, 0:512], scalar1=-1.0,
                          scalar2=None, op0=ALU.mult)
    P.inc("V", ins, f"{p}comb2_{tt}")
    _wait(v, wc, P.tok(f"{p}bc{tt}"))
    _wait(v, wc, xln_war)
    v.tensor_copy(out=c.BCA[:, 0, :], in_=c.PA4[:])
    v.tensor_copy(out=c.BCA[:, 1, :], in_=c.PA5[:])
    rs = rsl if rsl is not None else (lambda ch: c.R[:, ch, tt * 512:(tt + 1) * 512])
    for ch in range(NCH):
        v.tensor_tensor(out=c.XLN[:, ch, :], in0=rs(ch), in1=c.BCA[:, 0, :], op=ALU.mult)
        ins = v.tensor_tensor(out=c.XLN[:, ch, :], in0=c.XLN[:, ch, :], in1=c.BCA[:, 1, :],
                              op=ALU.add)
    P.inc("V", ins, f"{p}xln{tt}")



FFN_NG = 136
FFN_NLD = 88


def ffn_group_stage(g):
    if g < 32:
        return "w1", g
    if g < 64:
        return "w2", g - 32
    if g < 96:
        return "h1", g - 64
    if g < 128:
        return "h2", g - 96
    return "w3", g - 128


def ffn_load_of_group(g):
    st, m = ffn_group_stage(g)
    if st == "w1":
        return m // 4
    if st == "w2":
        return 8 + m // 4
    if st == "h1":
        return 16 + m
    if st == "h2":
        return 48 + m
    return 80 + m


def ffn_load_spec(li):
    if li < 8:
        return "w1", list(range(li * 4, li * 4 + 4)), li
    if li < 16:
        j = li - 8
        return "w2", list(range(32 + j * 4, 32 + j * 4 + 4)), j
    if li < 48:
        j = li - 16
        return "h1", [64 + j], j
    if li < 80:
        j = li - 48
        return "h2", [96 + j], j
    j = li - 80
    return "w3", [128 + j], j


def emit_ffn1(c, is_final):
    nc = c.nc
    P = Plan({"T": c.T, "S": c.S, "V": c.V, "SY": c.SY, "G": c.G})
    emit_ffn1.count = getattr(emit_ffn1, "count", 0) + 1
    p = f"f{emit_ffn1.count}_"
    for tt in range(NT):
        plan_ln(P, p, tt)
        for li in range(FFN_NLD):
            P.add("SY", f"{p}ld{tt}_{li}")
        for g in range(FFN_NG):
            P.add("T", f"{p}g{tt}_{g}")
        for g in range(128):
            P.add("S", f"{p}ev{tt}_{g}")
        for m in range(32):
            P.add("V", f"{p}h0_{tt}_{m}")
        for m in range(8):
            P.add("V", f"{p}res{tt}_{m}")
        if is_final:
            for m in range(8):
                P.add("G", f"{p}out{tt}_{m}")

    prev_hfree = c.hbuf_free
    prev_xfree = c.xln_free
    prev_wbfree = c.wb_free
    prev_pbfree = list(c.pb_free)
    prev_poolfree = c.pool_free

    def reader_tok(tt, g):
        st, m = ffn_group_stage(g)
        if st == "w3":
            return P.tok(f"{p}res{tt}_{m}")
        return P.tok(f"{p}ev{tt}_{g}")

    with nc.Block(f"ffn{emit_ffn1.count}") as blk:
        def _sy(sy):
            wc = c.wc["sync"]
            for tt in range(NT):
                for li in range(FFN_NLD):
                    st, gs, j = ffn_load_spec(li)
                    gli = tt * FFN_NLD + li
                    if gli >= 2:
                        ptt, pli = divmod(gli - 2, FFN_NLD)
                        _, pgs, _ = ffn_load_spec(pli)
                        _wait(sy, wc, reader_tok(ptt, pgs[-1]))
                    else:
                        _wait(sy, wc, prev_wbfree)
                    _wait(sy, wc, c.cast_tok["W_f" + st])
                    src = c.wsc["W_f" + st]
                    if st in ("w1", "w2"):
                        sl = src[j * 4:(j + 1) * 4]
                        dst = c.WB[:, gli % 2].rearrange("p (m ko) -> p m ko", m=4)
                    else:
                        sl = src[j:j + 1]
                        dst = c.WB[:, gli % 2].rearrange("p (m ko) -> p m ko", m=1)
                    ins = sy.dma_start(out=dst, in_=sl.rearrange("m p k o -> p m (k o)"))
                    P.inc("SY", ins, f"{p}ld{tt}_{li}")
        blk.sync(_sy)

        def _s(s):
            wc = c.wc["scalar"]
            _wait(s, wc, c.small_tok)
            for tt in range(NT):
                ln_scalar(c, P, p, s, tt,
                          prev_sq_reader=(None if tt > 0 else c.stat_free))
                for g in range(128):
                    st, m = ffn_group_stage(g)
                    _wait(s, wc, P.tok(f"{p}g{tt}_{g}"))
                    pb = c.PB[g % 4]
                    if st == "w1":
                        ins = s.activation(out=c.HA[:, m, :], in_=pb[:], func=AF.Silu,
                                           scale=c.SCL[:, 0, m:m + 1])
                    elif st == "w2":
                        ins = s.activation(out=c.HB[:, m, :], in_=pb[:], func=AF.Identity,
                                           scale=c.SCL[:, 1, m:m + 1])
                    elif st == "h1":
                        _wait(s, wc, P.tok(f"{p}h0_{tt}_{m}"))
                        ins = s.activation(out=c.HA[:, m, :], in_=pb[:], func=AF.Gelu,
                                           scale=c.SCL[:, 2, m:m + 1])
                    else:
                        ins = s.activation(out=c.HB[:, m, :], in_=pb[:], func=AF.Gelu,
                                           scale=c.SCL[:, 3, m:m + 1])
                    P.inc("S", ins, f"{p}ev{tt}_{g}")
        blk.scalar(_s)

        def _v(v):
            wc = c.wc["vector"]
            for tt in range(NT):
                xw = prev_xfree if tt == 0 else P.tok(f"{p}g{tt - 1}_{63}")
                ln_vector(c, P, p, v, tt, xln_war=xw)
                for m in range(32):
                    _wait(v, wc, P.tok(f"{p}ev{tt}_{m}"))
                    _wait(v, wc, P.tok(f"{p}ev{tt}_{32 + m}"))
                    if tt > 0:
                        _wait(v, wc, P.tok(f"{p}g{tt - 1}_{95}"))
                    else:
                        _wait(v, wc, prev_hfree)
                    ins = v.tensor_tensor(out=c.HC[:, m, :], in0=c.HA[:, m, :],
                                          in1=c.HB[:, m, :], op=ALU.mult)
                    P.inc("V", ins, f"{p}h0_{tt}_{m}")
                for m in range(8):
                    g = 128 + m
                    _wait(v, wc, P.tok(f"{p}g{tt}_{g}"))
                    _wait(v, wc, c.rfree[tt])
                    pb = c.PB[g % 4]
                    if is_final:
                        if tt == 0 and m < 2:
                            _wait(v, wc, prev_poolfree)
                        if m >= 2:
                            _wait(v, wc, P.tok(f"{p}out{tt}_{m - 2}"))
                        elif tt > 0:
                            _wait(v, wc, P.tok(f"{p}out{tt - 1}_{m + 6}"))
                        v.tensor_scalar(out=c.OSTM[:, m % 2, :], in0=pb[:],
                                        scalar1=c.SCL[:, 4, m:m + 1], scalar2=0.5,
                                        op0=ALU.mult, op1=ALU.mult)
                        ins = v.tensor_tensor(out=c.OSTM[:, m % 2, :], in0=c.OSTM[:, m % 2, :],
                                              in1=c.R[:, m, tt * 512:(tt + 1) * 512],
                                              op=ALU.add)
                    else:
                        v.tensor_scalar(out=pb[:], in0=pb[:],
                                        scalar1=c.SCL[:, 4, m:m + 1], scalar2=0.5,
                                        op0=ALU.mult, op1=ALU.mult)
                        ins = v.tensor_tensor(out=c.R[:, m, tt * 512:(tt + 1) * 512],
                                              in0=c.R[:, m, tt * 512:(tt + 1) * 512],
                                              in1=pb[:], op=ALU.add)
                    P.inc("V", ins, f"{p}res{tt}_{m}")
        blk.vector(_v)

        def _t(t):
            wc = c.wc["tensor"]
            for tt in range(NT):
                aux_war = P.tok(f"{p}xln{tt - 1}") if tt > 0 else c.stat_free
                ln_tensor(c, P, p, t, tt, aux_war=aux_war)
                ln_tensor_bcast(c, P, p, t, tt)
                for g in range(FFN_NG):
                    st, m = ffn_group_stage(g)
                    li = ffn_load_of_group(g)
                    _wait(t, wc, P.tok(f"{p}ld{tt}_{li}"))
                    pg = g - 4
                    if pg >= 0:
                        _wait(t, wc, reader_tok(tt, pg))
                    elif tt > 0:
                        _wait(t, wc, reader_tok(tt - 1, FFN_NG - 4 + g))
                    else:
                        _wait(t, wc, prev_pbfree[g % 4])
                    if st in ("w1", "w2"):
                        _wait(t, wc, P.tok(f"{p}xln{tt}"))
                        rhs = [c.XLN[:, k, :] for k in range(8)]
                    elif st == "h1":
                        _wait(t, wc, P.tok(f"{p}h0_{tt}_{31}"))
                        rhs = [c.HC[:, k, :] for k in range(32)]
                    elif st == "h2":
                        _wait(t, wc, P.tok(f"{p}ev{tt}_{64 + 31}"))
                        rhs = [c.HA[:, k, :] for k in range(32)]
                    else:
                        _wait(t, wc, P.tok(f"{p}ev{tt}_{96 + 31}"))
                        rhs = [c.HB[:, k, :] for k in range(32)]
                    W = c.WB[:, (tt * FFN_NLD + li) % 2]
                    if st in ("w1", "w2"):
                        off = (m % 4) * 1024
                    else:
                        off = 0
                    pb = c.PB[g % 4]
                    nk = len(rhs)
                    for k in range(nk):
                        ins = t.matmul(out=pb[:], lhsT=W[:, off + k * 128: off + (k + 1) * 128],
                                       rhs=rhs[k], start=(k == 0), stop=(k == nk - 1))
                    P.inc("T", ins, f"{p}g{tt}_{g}")
        blk.tensor(_t)

        if is_final:
            def _g(g):
                wc = c.wc["gpsimd"]
                yv = c.y_out.rearrange("s (ch q) t -> q ch s t", q=128)
                for tt in range(NT):
                    for m in range(8):
                        _wait(g, wc, P.tok(f"{p}res{tt}_{m}"))
                        dst = yv[:, m, tt // 4, (tt % 4) * 512:(tt % 4) * 512 + 512]
                        ins = g.dma_start(out=dst, in_=c.OSTM[:, m % 2, :])
                        P.inc("G", ins, f"{p}out{tt}_{m}")
            blk.gpsimd(_g)

    P.done()
    for tt in range(NT):
        c.rtok[tt] = P.tok(f"{p}res{tt}_{7}")
        c.rfree[tt] = P.tok(f"{p}res{tt}_{7}")
    c.xln_free = P.tok(f"{p}g{NT - 1}_{63}")
    c.hbuf_free = P.tok(f"{p}res{NT - 1}_{7}")
    c.wb_free = P.tok(f"{p}res{NT - 1}_{7}")
    c.stat_free = P.tok(f"{p}xln{NT - 1}")
    for b in range(4):
        g_last = max(g for g in range(FFN_NG) if g % 4 == b)
        c.pb_free[b] = reader_tok(NT - 1, g_last)
    c.pa_free = P.tok(f"{p}xln{NT - 1}")
    if is_final:
        c.pool_free = P.tok(f"{p}out{NT - 1}_{7}")



def emit_moe(c, is_final):
    nc = c.nc
    P = Plan({"T": c.T, "S": c.S, "V": c.V, "SY": c.SY, "G": c.G})
    emit_moe.count = getattr(emit_moe, "count", 0) + 1
    p = f"m{emit_moe.count}_"
    P.add("G", f"{p}poolA")
    P.add("G", f"{p}pool")
    for tt in range(NT):
        plan_ln(P, p, tt)
        for gr in range(4):
            P.add("T", f"{p}lg{tt}_{gr}")
        for gr in range(4):
            P.add("V", f"{p}cfc{tt}_{gr}")
        P.add("V", f"{p}cfA{tt}")
        for gr in range(4):
            P.add("S", f"{p}cex{tt}_{gr}")
        P.add("V", f"{p}cf2A{tt}")
        for gr in range(4):
            P.add("T", f"{p}ct{tt}_{gr}")
        for gr in range(4):
            P.add("V", f"{p}cte{tt}_{gr}")
        for e in range(8):
            P.add("T", f"{p}cb{tt}_{e}")
            P.add("V", f"{p}cbe{tt}_{e}")
        for mh in range(2):
            for e in range(8):
                P.add("SY", f"{p}ld{tt}_{mh}_{e}")
                P.add("V", f"{p}xe{tt}_{mh}_{e}")
                P.add("T", f"{p}em{tt}_{mh}_{e}")
            for m in range(4):
                P.add("V", f"{p}res{tt}_{mh}_{m}")
                if is_final:
                    P.add("G", f"{p}out{tt}_{mh}_{m}")

    prev_hfree = c.hbuf_free
    prev_xfree = c.xln_free
    prev_wbfree = c.wb_free
    prev_pbfree = list(c.pb_free)
    prev_pafree = c.pa_free
    prev_ptfree = c.pt_free
    prev_poolfree = c.pool_free

    with nc.Block(f"moe{emit_moe.count}") as blk:
        def _g(g):
            wc = c.wc["gpsimd"]
            _wait(g, wc, c.G.tok())
            _wait(g, wc, prev_poolfree)
            ins = g.dma_start(out=c.EBS, in_=c.win["W_expb"].rearrange("m r o -> r m o"))
            P.inc("G", ins, f"{p}poolA")
            ins = g.dma_start(out=c.SEL, in_=c.cin["C_sel"])
            P.inc("G", ins, f"{p}pool")
        blk.gpsimd(_g)

        def _sy(sy):
            wc = c.wc["sync"]
            for tt in range(NT):
                for mh in range(2):
                    for e in range(8):
                        gli = (tt * 2 + mh) * 8 + e
                        if gli >= 2:
                            ptt, r = divmod(gli - 2, 16)
                            pmh, pe = divmod(r, 8)
                            _wait(sy, wc, P.tok(f"{p}em{ptt}_{pmh}_{pe}"))
                        else:
                            _wait(sy, wc, prev_wbfree)
                        _wait(sy, wc, c.cast_tok["W_exp"])
                        src = c.wsc["W_exp"][e, :, mh * 4:(mh + 1) * 4]
                        ins = sy.dma_start(out=c.WB[:, gli % 2],
                                           in_=src.rearrange("p m k o -> p (m k o)"))
                        P.inc("SY", ins, f"{p}ld{tt}_{mh}_{e}")
        blk.sync(_sy)

        def _s(s):
            wc = c.wc["scalar"]
            for tt in range(NT):
                ln_scalar(c, P, p, s, tt,
                          prev_sq_reader=(None if tt > 0 else c.stat_free))
                for gr in range(4):
                    _wait(s, wc, P.tok(f"{p}cfA{tt}"))
                    ins = s.activation(out=c.LGS[:, gr, :], in_=c.LGS[:, gr, :], func=AF.Exp,
                                       accum_out=c.ACC4[:, gr:gr + 1])
                    P.inc("S", ins, f"{p}cex{tt}_{gr}")
        blk.scalar(_s)

        def _v(v):
            wc = c.wc["vector"]
            for tt in range(NT):
                xw = prev_xfree if tt == 0 else P.tok(f"{p}lg{tt - 1}_{3}")
                ln_vector(c, P, p, v, tt, xln_war=xw)
                for gr in range(4):
                    _wait(v, wc, P.tok(f"{p}lg{tt}_{gr}"))
                    pa = c.PA4 if gr % 2 == 0 else c.PA5
                    ins = v.tensor_copy(out=c.LG[:, gr, :], in_=pa[:, 0:8])
                    P.inc("V", ins, f"{p}cfc{tt}_{gr}")
                _wait(v, wc, P.tok(f"{p}cfc{tt}_{3}"))
                for gr in range(4):
                    ins = v.reduce_max(out=c.RD[:, gr:gr + 1], in_=c.LG[:, gr, :],
                                       axis=AX.X, negate=True)
                _selfsync(c, v, ins)
                for gr in range(4):
                    v.tensor_scalar(out=c.LGS[:, gr, :], in0=c.LG[:, gr, :],
                                    scalar1=c.RD[:, gr:gr + 1], scalar2=None, op0=ALU.add)
                for gr in range(4):
                    ins = v.tensor_scalar(out=c.LG[:, gr, :], in0=c.LG[:, gr, :],
                                          scalar1=c.RD[:, gr:gr + 1], scalar2=0.0,
                                          op0=ALU.add, op1=ALU.is_ge)
                P.inc("V", ins, f"{p}cfA{tt}")
                for gr in range(4):
                    _wait(v, wc, P.tok(f"{p}cex{tt}_{gr}"))
                ins = v.reciprocal(out=c.RD[:, 0:4], in_=c.ACC4[:, 0:4])
                _selfsync(c, v, ins)
                for gr in range(4):
                    ins = v.tensor_scalar(out=c.LG[:, gr, :], in0=c.LG[:, gr, :],
                                          scalar1=c.RD[:, gr:gr + 1], scalar2=None,
                                          op0=ALU.mult)
                _selfsync(c, v, ins)
                for gr in range(4):
                    ins = v.tensor_copy(out=c.CFB[:, gr, :], in_=c.LG[:, gr, :])
                P.inc("V", ins, f"{p}cf2A{tt}")
                for gr in range(4):
                    _wait(v, wc, P.tok(f"{p}ct{tt}_{gr}"))
                    if tt > 0:
                        _wait(v, wc, P.tok(f"{p}em{tt - 1}_{1}_{0}"))
                    else:
                        _wait(v, wc, prev_poolfree)
                    ins = v.tensor_copy(out=c.CT8[:, gr * 128:(gr + 1) * 128],
                                        in_=c.PT2[0:8, gr * 128:(gr + 1) * 128])
                    P.inc("V", ins, f"{p}cte{tt}_{gr}")
                for e in range(8):
                    _wait(v, wc, P.tok(f"{p}cb{tt}_{e}"))
                    pa = c.PA4 if e % 2 == 0 else c.PA5
                    if tt > 0 and e < 2:
                        _wait(v, wc, P.tok(f"{p}xe{tt - 1}_{1}_{7}"))
                    elif tt == 0 and e < 2:
                        _wait(v, wc, prev_hfree)
                    ins = v.tensor_copy(out=c.HB[:, e, :], in_=pa[:])
                    P.inc("V", ins, f"{p}cbe{tt}_{e}")
                for mh in range(2):
                    for e in range(8):
                        _wait(v, wc, P.tok(f"{p}cbe{tt}_{e}"))
                        gx = (tt * 2 + mh) * 8 + e
                        if gx >= 2:
                            ptt, r = divmod(gx - 2, 16)
                            pmh, pe = divmod(r, 8)
                            _wait(v, wc, P.tok(f"{p}em{ptt}_{pmh}_{pe}"))
                        else:
                            _wait(v, wc, prev_hfree)
                        xb = (gx % 2) * 8
                        for ch in range(8):
                            ins = v.tensor_tensor(out=c.HA[:, xb + ch, :],
                                                  in0=c.XLN[:, ch, :],
                                                  in1=c.HB[:, e, :], op=ALU.mult)
                        P.inc("V", ins, f"{p}xe{tt}_{mh}_{e}")
                    for m in range(4):
                        _wait(v, wc, P.tok(f"{p}em{tt}_{mh}_{7}"))
                        _wait(v, wc, c.rfree[tt])
                        cglob = mh * 4 + m
                        if is_final:
                            jj = (tt * 2 + mh) * 4 + m
                            if jj >= 2:
                                pj = jj - 2
                                ptt2, r2 = divmod(pj, 8)
                                pmh2, pm2 = divmod(r2, 4)
                                _wait(v, wc, P.tok(f"{p}out{ptt2}_{pmh2}_{pm2}"))
                            ins = v.scalar_tensor_tensor(
                                out=c.OSTM[:, m % 2, :], in0=c.PB[m][:],
                                scalar=0.5, in1=c.R[:, cglob, tt * 512:(tt + 1) * 512],
                                op0=ALU.mult, op1=ALU.add)
                        else:
                            ins = v.scalar_tensor_tensor(
                                out=c.R[:, cglob, tt * 512:(tt + 1) * 512], in0=c.PB[m][:],
                                scalar=0.5, in1=c.R[:, cglob, tt * 512:(tt + 1) * 512],
                                op0=ALU.mult, op1=ALU.add)
                        P.inc("V", ins, f"{p}res{tt}_{mh}_{m}")
        blk.vector(_v)

        def _t(t):
            wc = c.wc["tensor"]
            _wait(t, wc, c.small_tok)
            for tt in range(NT):
                aux_war = P.tok(f"{p}cbe{tt - 1}_{7}") if tt > 0 else c.stat_free
                ln_tensor(c, P, p, t, tt, aux_war=aux_war)
                ln_tensor_bcast(c, P, p, t, tt)
                for gr in range(4):
                    _wait(t, wc, P.tok(f"{p}xln{tt}"))
                    pa = c.PA4 if gr % 2 == 0 else c.PA5
                    if gr >= 2:
                        _wait(t, wc, P.tok(f"{p}cfc{tt}_{gr - 2}"))
                    for ch in range(8):
                        t.matmul(out=pa[:, 0:8],
                                 lhsT=c.XLN[:, ch, gr * 128:(gr + 1) * 128],
                                 rhs=c.RWS[:, ch, :], start=(ch == 0), stop=False)
                    ins = t.matmul(out=pa[:, 0:8], lhsT=c.ONES1[:], rhs=c.RBS[:],
                                   start=False, stop=True)
                    P.inc("T", ins, f"{p}lg{tt}_{gr}")
                for gr in range(4):
                    _wait(t, wc, P.tok(f"{p}cf2A{tt}"))
                    if gr > 0:
                        _wait(t, wc, P.tok(f"{p}cte{tt}_{gr - 1}"))
                    elif tt > 0:
                        _wait(t, wc, P.tok(f"{p}cte{tt - 1}_{3}"))
                    else:
                        _wait(t, wc, prev_ptfree)
                    ins = t.transpose(out=c.PT2[0:8, gr * 128:(gr + 1) * 128],
                                      in_=c.CFB[:, gr, :], identity=c.IDENT[:])
                    P.inc("T", ins, f"{p}ct{tt}_{gr}")
                for e in range(8):
                    _wait(t, wc, P.tok(f"{p}cte{tt}_{3}"))
                    _wait(t, wc, P.tok(f"{p}pool"))
                    pa = c.PA4 if e % 2 == 0 else c.PA5
                    if e >= 2:
                        _wait(t, wc, P.tok(f"{p}cbe{tt}_{e - 2}"))
                    else:
                        _wait(t, wc, P.tok(f"{p}cfA{tt}"))
                    ins = t.matmul(out=pa[:], lhsT=c.SEL[:, e, :], rhs=c.CT8[:],
                                   start=True, stop=True)
                    P.inc("T", ins, f"{p}cb{tt}_{e}")
                for mh in range(2):
                    for e in range(8):
                        _wait(t, wc, P.tok(f"{p}ld{tt}_{mh}_{e}"))
                        _wait(t, wc, P.tok(f"{p}xe{tt}_{mh}_{e}"))
                        gli = (tt * 2 + mh) * 8 + e
                        if e == 0:
                            if mh == 1:
                                for m in range(4):
                                    _wait(t, wc, P.tok(f"{p}res{tt}_{0}_{m}"))
                            elif tt > 0:
                                for m in range(4):
                                    _wait(t, wc, P.tok(f"{p}res{tt - 1}_{1}_{m}"))
                            else:
                                for b in range(4):
                                    _wait(t, wc, prev_pbfree[b])
                            for m in range(4):
                                t.matmul(out=c.PB[m][:], lhsT=c.EBS[:, mh * 4 + m, :],
                                         rhs=c.CT8[:], start=True, stop=False)
                        W = c.WB[:, gli % 2].rearrange("p (m k o) -> p m k o", m=4, k=8)
                        xb = (gli % 2) * 8
                        for m in range(4):
                            for k in range(8):
                                ins = t.matmul(out=c.PB[m][:], lhsT=W[:, m, k],
                                               rhs=c.HA[:, xb + k, :],
                                               start=False, stop=(e == 7 and k == 7))
                        P.inc("T", ins, f"{p}em{tt}_{mh}_{e}")
        blk.tensor(_t)

        if is_final:
            def _go(g):
                wc = c.wc["gpsimd"]
                yv = c.y_out.rearrange("s (ch q) t -> q ch s t", q=128)
                for tt in range(NT):
                    for mh in range(2):
                        for m in range(4):
                            _wait(g, wc, P.tok(f"{p}res{tt}_{mh}_{m}"))
                            cglob = mh * 4 + m
                            dst = yv[:, cglob, tt // 4, (tt % 4) * 512:(tt % 4) * 512 + 512]
                            ins = g.dma_start(out=dst, in_=c.OSTM[:, m % 2, :])
                            P.inc("G", ins, f"{p}out{tt}_{mh}_{m}")
            blk.gpsimd(_go)

    P.done()
    for tt in range(NT):
        c.rtok[tt] = P.tok(f"{p}res{tt}_{1}_{3}")
        c.rfree[tt] = P.tok(f"{p}res{tt}_{1}_{3}")
    c.xln_free = P.tok(f"{p}xe{NT - 1}_{1}_{7}")
    c.hbuf_free = P.tok(f"{p}em{NT - 1}_{1}_{7}")
    c.wb_free = P.tok(f"{p}em{NT - 1}_{1}_{7}")
    c.stat_free = P.tok(f"{p}xln{NT - 1}")
    for b in range(4):
        c.pb_free[b] = P.tok(f"{p}res{NT - 1}_{1}_{b}")
    c.pa_free = P.tok(f"{p}cbe{NT - 1}_{7}")
    c.pt_free = P.tok(f"{p}cte{NT - 1}_{3}")
    c.pool_free = P.tok(f"{p}em{NT - 1}_{1}_{0}")



def emit_store(c):
    nc = c.nc
    P = Plan({"V": c.V, "G": c.G})
    emit_store.count = getattr(emit_store, "count", 0) + 1
    p = f"st{emit_store.count}_"
    for tt in range(NT):
        for ch in range(NCH):
            P.add("V", f"{p}c{tt}_{ch}")
            P.add("G", f"{p}o{tt}_{ch}")
    with nc.Block(f"store{emit_store.count}") as blk:
        def _v(v):
            wc = c.wc["vector"]
            for tt in range(NT):
                _wait(v, wc, c.rtok[tt])
                for ch in range(NCH):
                    j = tt * NCH + ch
                    if j >= 2:
                        pt, pc = divmod(j - 2, NCH)
                        _wait(v, wc, P.tok(f"{p}o{pt}_{pc}"))
                    ins = v.tensor_copy(out=c.OSTM[:, j % 2, :],
                                        in_=c.R[:, ch, tt * 512:(tt + 1) * 512])
                    P.inc("V", ins, f"{p}c{tt}_{ch}")
        blk.vector(_v)

        def _g(g):
            wc = c.wc["gpsimd"]
            _wait(g, wc, c.G.tok())
            yv = c.y_out.rearrange("s (ch q) t -> q ch s t", q=128)
            for tt in range(NT):
                for ch in range(NCH):
                    j = tt * NCH + ch
                    _wait(g, wc, P.tok(f"{p}c{tt}_{ch}"))
                    dst = yv[:, ch, tt // 4, (tt % 4) * 512:(tt % 4) * 512 + 512]
                    ins = g.dma_start(out=dst, in_=c.OSTM[:, j % 2, :])
                    P.inc("G", ins, f"{p}o{tt}_{ch}")
        blk.gpsimd(_g)
    P.done()


def attn_kbs(layer, qi):
    if layer == 0:
        return [list(range(4 * kb, 4 * kb + 4)) for kb in range(4)]
    kcs = list(range(qi + 1))
    return [kcs[i:i + 4] for i in range(0, len(kcs), 4)]


def emit_attn(c, layer):
    nc = c.nc
    P = Plan({"T": c.T, "S": c.S, "V": c.V, "SY": c.SY, "G": c.G})
    emit_attn.count = getattr(emit_attn, "count", 0) + 1
    p = f"a{layer}_{emit_attn.count}_"
    L0 = (layer == 0)
    projs = ["q", "k", "v"]
    KR = 67 if L0 else 64
    wsc = {n: c.wsc[f"W_a{layer}{n}"] for n in "qkvo"}

    QAv = c.HC[:, 0:8, :].rearrange("p a b -> p (a b)").rearrange("p (b t) -> p b t", b=2)
    KAv = c.HC[:, 8:16, :].rearrange("p a b -> p (a b)").rearrange("p (b t) -> p b t", b=2)
    CS2v = c.HC[:, 16:24, :].rearrange("p a b -> p (a b)").rearrange("p (b t) -> p b t", b=2)
    STG = c.HC[:, 24:32, :]
    VT = c.HB[:].rearrange("p a b -> p (a b)").rearrange("p (h k d) -> p h k d", h=16, k=16)
    ATT = c.HA[:].rearrange("p a b -> p (a b)").rearrange("p (q d) -> p q d", q=16)
    OSB = c.OSTM[:].rearrange("p a b -> p (a b)").bitcast(BF16)
    PSCP = [c.WB[:, 0, 0:2048], c.WB[:, 0, 2048:4096]]
    PTSP = [c.WB[:, 1, 0:2048], c.WB[:, 1, 2048:4096]]
    PTB = [c.PT2[:, 0:512], c.PB[2][:].bitcast(BF16)[:, 0:512]]
    PVB = [c.PO3, c.PA5]
    PTD = [c.PT2[:, 0:1024], c.PB[2][:].bitcast(BF16)[:, 0:1024]] if False else None

    ld_counter = [0]
    ld_slot = {}
    ld_war = {}

    def plan_load(name, reader_name):
        i = ld_counter[0]
        ld_slot[name] = i % 2
        ld_war[name] = None if i < 2 else plan_load.hist[i - 2][1]
        plan_load.hist.append((name, reader_name))
        ld_counter[0] += 1
        P.add("SY", name)
    plan_load.hist = []

    kbctr = [0]
    kb_bank = {}
    pvb_war = {}
    _pvb_prev = {}
    lora_prev = []
    psc_war = {}
    acc_war = {}
    pts_war = {}
    ptc_war = {}
    _psc_prev = {}
    _acc_prev = {}
    _pts_prev = {}
    _ptc_prev = {}

    if L0:
        P.add("G", p + "rez_ld")
        P.add("T", p + "rez_mm")
        P.add("V", p + "rez_ev")
        P.add("V", p + "ones0")
        P.add("V", p + "ones1")
    lwo_waits = {}
    lw_waits = {}
    for s in range(2):
        if not L0:
            P.add("SY", p + f"cs2a_{s}")
            P.add("SY", p + f"cs2_{s}")
        for tl in range(4):
            gt = s * 4 + tl
            plan_ln(P, p, gt)
            for proj in projs:
                if L0:
                    P.add("G", p + f"lwA{gt}_{proj}")
                    P.add("G", p + f"lwB{gt}_{proj}")
                    P.add("G", p + f"lw{gt}_{proj}")
                    lw_waits[p + f"lw{gt}_{proj}"] = list(lora_prev)
                    lora_prev = [p + f"ev{gt}_{proj}_{7}", p + f"g{gt}_{proj}_{7}"]
                for hf in range(2):
                    plan_load(p + f"wl{gt}_{proj}_{hf}", p + f"g{gt}_{proj}_{3 + 4 * hf}")
                if L0:
                    P.add("T", p + f"u{gt}_{proj}")
                    P.add("S", p + f"ue{gt}_{proj}")
                for m in range(8):
                    P.add("T", p + f"g{gt}_{proj}_{m}")
                    P.add("S", p + f"ev{gt}_{proj}_{m}")
                    if ATTN_STAGE < 1:
                        continue
                    if (not L0) and proj != "v" and "r" in ATTN_FEAT:
                        P.add("T", p + f"sw{gt}_{proj}_{m}")
                        P.add("V", p + f"rp{gt}_{proj}_{m}")
                    if proj != "v":
                        if "n" in ATTN_FEAT:
                            P.add("S", p + f"nsq{gt}_{proj}_{m}")
                            P.add("T", p + f"nmm{gt}_{proj}_{m}")
                            P.add("V", p + f"nrd{gt}_{proj}_{m}")
                    elif "t" in ATTN_FEAT:
                        P.add("T", p + f"vt{gt}_{m}")
                        P.add("V", p + f"vte{gt}_{m}")
                if proj != "v" and ATTN_STAGE >= 1 and "s" in ATTN_FEAT:
                    for ch in range(NCH - 1):
                        P.add("G", p + f"sp{gt}_{proj}c{ch}")
                    P.add("G", p + f"sp{gt}_{proj}")
            if L0 and ATTN_STAGE >= 1 and "s" in ATTN_FEAT:
                for ch in range(NCH - 1):
                    P.add("G", p + f"spx{gt}c{ch}")
                P.add("G", p + f"spx{gt}")
        if ATTN_STAGE >= 1 and "b" in ATTN_FEAT:
            P.add("V", p + f"nb{s}")
            P.add("T", p + f"nbm{s}")
            P.add("V", p + f"nbe{s}")
        if ATTN_STAGE < 2:
            continue
        for h in range(NH):
            P.add("SY", p + f"hq{s}_{h}")
            P.add("SY", p + f"hk{s}_{h}")
            if L0:
                P.add("SY", p + f"ha{s}_{h}")
            for qi in range(16):
                kbs = attn_kbs(layer, qi)
                par = qi % 2
                pvb_war[(s, h, qi)] = _pvb_prev.get(par)
                _pvb_prev[par] = p + f"o{s}_{h}_{qi}"
                for kb, kcs in enumerate(kbs):
                    nm = p + f"ex{s}_{h}_{qi}_{kb}"
                    sb = kbctr[0] % 2
                    tb = kbctr[0] % 2
                    kb_bank[(s, h, qi, kb)] = sb
                    kbctr[0] += 1
                    P.add("T", p + f"qk{s}_{h}_{qi}_{kb}")
                    if (not L0) and kb == len(kbs) - 1:
                        P.add("V", p + f"dg{s}_{h}_{qi}")
                    P.add("S", nm)
                    psc_war[nm] = _psc_prev.get((kb, par))
                    acc_war[nm] = _acc_prev.get((kb, par))
                    _psc_prev[(kb, par)] = p + f"pt{s}_{h}_{qi}_{kb}"
                    _acc_prev[(kb, par)] = p + f"o{s}_{h}_{qi}"
                    ptn = p + f"pt{s}_{h}_{qi}_{kb}"
                    P.add("T", ptn)
                    ptc_war[ptn] = _ptc_prev.get(tb)
                    _ptc_prev[tb] = p + f"pte{s}_{h}_{qi}_{kb}"
                    pte = p + f"pte{s}_{h}_{qi}_{kb}"
                    P.add("V", pte)
                    pts_war[pte] = _pts_prev.get((kb, par))
                    _pts_prev[(kb, par)] = p + f"pv{s}_{h}_{qi}_{kb}"
                    P.add("T", p + f"pv{s}_{h}_{qi}_{kb}")
                P.add("V", p + f"o{s}_{h}_{qi}")
        if ATTN_STAGE < 3:
            continue
        for hf in range(2):
            plan_load(p + f"wo{s}_{hf}", p + f"go{s}_{15}_{3 + 4 * hf}")
        if L0 and ATTN_STAGE >= 3:
            P.add("G", p + f"lwoA{s}")
            P.add("G", p + f"lwoB{s}")
            P.add("G", p + f"lwo{s}")
            lwo_waits[p + f"lwo{s}"] = list(lora_prev)
            lora_prev = [p + f"uoe{s}_{15}", p + f"go{s}_{15}_{7}"]
        for qi in range(16):
            if L0:
                P.add("SY", p + f"xr{s}_{qi}")
            P.add("T", p + f"at{s}_{qi}")
            P.add("V", p + f"ate{s}_{qi}")
            if L0:
                P.add("T", p + f"uo{s}_{qi}")
                P.add("S", p + f"uoe{s}_{qi}")
            for m in range(8):
                P.add("T", p + f"go{s}_{qi}_{m}")
                P.add("V", p + f"ro{s}_{qi}_{m}")
    at_war = {}
    for s in range(2):
        for qi in range(16):
            nm = p + f"at{s}_{qi}"
            at_war[nm] = [_ptc_prev.get(0), _ptc_prev.get(1)] if qi == 0 and s == 0 else None

    prev_hfree = c.hbuf_free
    prev_xfree = c.xln_free
    prev_wbfree = c.wb_free
    prev_pbfree = list(c.pb_free)
    prev_pafree = c.pa_free
    prev_ptfree = c.pt_free
    prev_pofree = c.po_free
    prev_poolfree = c.pool_free
    prev_statfree = c.stat_free
    prev_lorafree = getattr(c, "lora_free", None)
    rfree0 = list(c.rfree)

    pbseq = {0: [], 1: []}
    for s in range(2):
        gidx = 0
        for tl in range(4):
            gt = s * 4 + tl
            for proj in projs:
                for m in range(8):
                    gi = (gt * 24 + projs.index(proj) * 8 + m)
                    pbseq[gi % 2].append((p + f"g{gt}_{proj}_{m}", p + f"ev{gt}_{proj}_{m}"))
        if ATTN_STAGE >= 2:
            for h in range(NH):
                for qi in range(16):
                    for kb in range(len(attn_kbs(layer, qi))):
                        pbseq[kb_bank[(s, h, qi, kb)]].append(
                            (p + f"qk{s}_{h}_{qi}_{kb}", p + f"ex{s}_{h}_{qi}_{kb}"))
        if ATTN_STAGE >= 3:
            for qi in range(16):
                for m in range(8):
                    pbseq[(qi * 8 + m) % 2].append((p + f"go{s}_{qi}_{m}",
                                                    p + f"ro{s}_{qi}_{m}"))
    pb_war = {}
    for b in (0, 1):
        prev = None
        for w, r in pbseq[b]:
            pb_war[w] = prev
            prev = r
    pb_last_reader = {b: pbseq[b][-1][1] for b in (0, 1)}

    def tokm(name):
        return None if name is None else P.tok(name)

    def pbwar_tok(writer, bank):
        r = pb_war.get(writer)
        if r is None:
            return prev_pbfree[bank]
        return P.tok(r)

    def pbwar_tok_pb2(s):
        return prev_pbfree[2]

    def proj_bank(gt, proj, m):
        return (gt * 24 + projs.index(proj) * 8 + m) % 2

    def proj_bank(gt, proj, m):
        return (gt * 24 + projs.index(proj) * 8 + m) % 2

    with nc.Block(f"attn{layer}_{emit_attn.count}") as blk:
        def _g(g):
            wc = c.wc["gpsimd"]
            _wait(g, wc, c.G.tok())
            if L0:
                _wait(g, wc, prev_poolfree)
                _wait(g, wc, prev_lorafree)
                ins = g.dma_start(out=c.REZI[:], in_=c.win["W_rez"])
                P.inc("G", ins, p + "rez_ld")
            for s in range(2):
                for tl in range(4):
                    gt = s * 4 + tl
                    for proj in projs:
                        if L0:
                            for wn in lw_waits[p + f"lw{gt}_{proj}"]:
                                _wait(g, wc, P.tok(wn))
                            if gt == 0 and proj == "q":
                                _wait(g, wc, prev_lorafree)
                                _wait(g, wc, prev_poolfree)
                            ins = g.dma_start(out=c.LORA[:], in_=c.win[f"W_a0A{proj}"])
                            P.inc("G", ins, p + f"lwA{gt}_{proj}")
                            ins = g.dma_start(out=c.LORB, in_=c.win[f"W_a0B{proj}"])
                            P.inc("G", ins, p + f"lwB{gt}_{proj}")
                            ins = g.dma_start(out=c.LBIA[:], in_=c.win[f"W_a0b{proj}"])
                            P.inc("G", ins, p + f"lw{gt}_{proj}")
                        if proj != "v" and ATTN_STAGE >= 1 and "s" in ATTN_FEAT:
                            for m in range(8):
                                if (not L0) and "r" in ATTN_FEAT:
                                    _wait(g, wc, P.tok(p + f"rp{gt}_{proj}_{m}"))
                                else:
                                    _wait(g, wc, P.tok(p + f"ev{gt}_{proj}_{m}"))
                            dst = (c.q_sp if proj == "q" else c.k_sp)[s]
                            for ch in range(NCH):
                                ins = g.dma_start(
                                    out=dst[ch, :, tl * 512:(tl + 1) * 512],
                                    in_=STG[:, ch, :])
                                if ch < NCH - 1:
                                    P.inc("G", ins, p + f"sp{gt}_{proj}c{ch}")
                            P.inc("G", ins, p + f"sp{gt}_{proj}")
                    if L0 and ATTN_STAGE >= 1 and "s" in ATTN_FEAT:
                        _wait(g, wc, P.tok(p + f"xln{gt}"))
                        for ch in range(NCH):
                            ins = g.dma_start(
                                out=c.xln_spd[s][ch, :, tl * 512:(tl + 1) * 512],
                                in_=c.XLN[:, ch, :])
                            if ch < NCH - 1:
                                P.inc("G", ins, p + f"spx{gt}c{ch}")
                        P.inc("G", ins, p + f"spx{gt}")
                if L0 and ATTN_STAGE >= 3:
                    for wn in lwo_waits[p + f"lwo{s}"]:
                        _wait(g, wc, P.tok(wn))
                    ins = g.dma_start(out=c.LORA[:], in_=c.win["W_a0Ao"])
                    P.inc("G", ins, p + f"lwoA{s}")
                    ins = g.dma_start(out=c.LORB, in_=c.win["W_a0Bo"])
                    P.inc("G", ins, p + f"lwoB{s}")
                    ins = g.dma_start(out=c.LBIA[:], in_=c.win["W_a0bo"])
                    P.inc("G", ins, p + f"lwo{s}")
        blk.gpsimd(_g)

        def _sy(sy):
            wc = c.wc["sync"]
            for s in range(2):
                if not L0:
                    if s > 0 and ATTN_STAGE >= 1 and "r" in ATTN_FEAT:
                        _wait(sy, wc, P.tok(p + f"rp{(s - 1) * 4 + 3}_k_{7}"))
                    else:
                        _wait(sy, wc, prev_hfree)
                    ins = sy.dma_start(out=CS2v[:, 0, :], in_=c.cin["C_cos"])
                    P.inc("SY", ins, p + f"cs2a_{s}")
                    ins = sy.dma_start(out=CS2v[:, 1, :], in_=c.cin["C_sin"])
                    P.inc("SY", ins, p + f"cs2_{s}")
                for tl in range(4):
                    gt = s * 4 + tl
                    for proj in projs:
                        for hf in range(2):
                            lname = p + f"wl{gt}_{proj}_{hf}"
                            war = ld_war[lname]
                            _wait(sy, wc, P.tok(war) if war else prev_wbfree)
                            _wait(sy, wc, c.cast_tok[f"W_a{layer}{proj}"])
                            sl = wsc[proj][hf * 4:(hf + 1) * 4]
                            dst = c.WB[:, ld_slot[lname]].rearrange("p (m ko) -> p m ko", m=4)
                            ins = sy.dma_start(out=dst,
                                               in_=sl.rearrange("m p k o -> p m (k o)"))
                            P.inc("SY", ins, lname)
                for h in (range(NH) if ATTN_STAGE >= 2 else []):
                    if h >= 2:
                        pq = len(attn_kbs(layer, 15))
                        _wait(sy, wc, P.tok(p + f"qk{s}_{h - 2}_{15}_{pq - 1}"))
                    elif s > 0:
                        pq = len(attn_kbs(layer, 15))
                        _wait(sy, wc, P.tok(p + f"qk{s - 1}_{14 + h}_{15}_{pq - 1}"))
                    else:
                        _wait(sy, wc, prev_hfree)
                    _wait(sy, wc, P.tok(p + f"sp{s * 4 + 3}_q"))
                    ins = sy.dma_start(out=QAv[0:64, h % 2, :],
                                       in_=c.q_sp[s][h // 2, 64 * (h % 2):64 * (h % 2) + 64, :])
                    P.inc("SY", ins, p + f"hq{s}_{h}")
                    _wait(sy, wc, P.tok(p + f"sp{s * 4 + 3}_k"))
                    ins = sy.dma_start(out=KAv[0:64, h % 2, :],
                                       in_=c.k_sp[s][h // 2, 64 * (h % 2):64 * (h % 2) + 64, :])
                    P.inc("SY", ins, p + f"hk{s}_{h}")
                    if L0:
                        ins = sy.dma_start(out=KAv[64:67, h % 2, :], in_=c.cin["C_alibi"][h])
                        P.inc("SY", ins, p + f"ha{s}_{h}")
                for hf in (range(2) if ATTN_STAGE >= 3 else []):
                    lname = p + f"wo{s}_{hf}"
                    war = ld_war[lname]
                    _wait(sy, wc, P.tok(war) if war else prev_wbfree)
                    if ATTN_STAGE >= 2:
                        lkb = len(attn_kbs(layer, 15)) - 1
                        _wait(sy, wc, P.tok(p + f"pv{s}_{15}_{15}_{lkb}"))
                    _wait(sy, wc, c.cast_tok[f"W_a{layer}o"])
                    sl = wsc["o"][hf * 4:(hf + 1) * 4]
                    dst = c.WB[:, ld_slot[lname]].rearrange("p (m ko) -> p m ko", m=4)
                    ins = sy.dma_start(out=dst, in_=sl.rearrange("m p k o -> p m (k o)"))
                    P.inc("SY", ins, lname)
                if L0 and ATTN_STAGE >= 3:
                    for qi in range(16):
                        if qi > 0:
                            _wait(sy, wc, P.tok(p + f"ro{s}_{qi - 1}_{7}"))
                        else:
                            _wait(sy, wc, P.tok(p + f"spx{s * 4 + 3}"))
                            _wait(sy, wc, P.tok(p + f"g{s * 4 + 3}_v_{7}"))
                            if L0:
                                _wait(sy, wc, P.tok(p + f"u{s * 4 + 3}_v"))
                        ins = sy.dma_start(
                            out=c.XLN[:, :, 0:128],
                            in_=c.xln_spd[s][:, :, qi * 128:(qi + 1) * 128].rearrange(
                                "m q t -> q m t"))
                        P.inc("SY", ins, p + f"xr{s}_{qi}")
        blk.sync(_sy)

        def _s(sc):
            wc = c.wc["scalar"]
            for s in range(2):
                for tl in range(4):
                    gt = s * 4 + tl
                    ln_scalar(c, P, p, sc, gt,
                              prev_sq_reader=(None if gt > 0 else prev_statfree))
                    for proj in projs:
                        if L0:
                            _wait(sc, wc, P.tok(p + f"u{gt}_{proj}"))
                            ins = sc.activation(out=c.USB[:, :], in_=c.PA4[0:8, :],
                                                func=AF.Identity, scale=0.125)
                            P.inc("S", ins, p + f"ue{gt}_{proj}")
                        for m in range(8):
                            _wait(sc, wc, P.tok(p + f"g{gt}_{proj}_{m}"))
                            if ATTN_STAGE >= 1 and "s" in ATTN_FEAT:
                                if proj == "k":
                                    _wait(sc, wc, P.tok(p + f"sp{gt}_q"))
                                elif proj == "v":
                                    _wait(sc, wc, P.tok(p + f"sp{gt}_k"))
                            pb = c.PB[proj_bank(gt, proj, m)]
                            if L0:
                                _wait(sc, wc, P.tok(p + f"lw{gt}_{proj}"))
                                ins = sc.activation(out=STG[:, m, :], in_=pb[:],
                                                    func=AF.Identity,
                                                    bias=c.LBIA[:, m:m + 1])
                            else:
                                ins = sc.activation(out=STG[:, m, :], in_=pb[:],
                                                    func=AF.Identity)
                            P.inc("S", ins, p + f"ev{gt}_{proj}_{m}")
                            if proj != "v" and ATTN_STAGE >= 1 and "n" in ATTN_FEAT:
                                if (not L0) and "r" in ATTN_FEAT:
                                    _wait(sc, wc, P.tok(p + f"rp{gt}_{proj}_{m}"))
                                if m > 0:
                                    _wait(sc, wc, P.tok(p + f"nmm{gt}_{proj}_{m - 1}"))
                                ins = sc.activation(out=OSB[:, 1024:1536],
                                                    in_=STG[:, m, :], func=AF.Square)
                                P.inc("S", ins, p + f"nsq{gt}_{proj}_{m}")
                for h in (range(NH) if ATTN_STAGE >= 2 else []):
                    for qi in range(16):
                        kbs = attn_kbs(layer, qi)
                        for kb, kcs in enumerate(kbs):
                            nm = p + f"ex{s}_{h}_{qi}_{kb}"
                            Lkb = len(kcs) * 128
                            _wait(sc, wc, P.tok(p + f"nbe{s}"))
                            if (not L0) and (qi in kcs):
                                _wait(sc, wc, P.tok(p + f"dg{s}_{h}_{qi}"))
                            else:
                                _wait(sc, wc, P.tok(p + f"qk{s}_{h}_{qi}_{kb}"))
                            if psc_war[nm] is None:
                                _wait(sc, wc, prev_wbfree)
                            _wait(sc, wc, P.tok(p + f"g{s * 4 + 3}_v_{7}"))
                            _wait(sc, wc, tokm(psc_war[nm]))
                            _wait(sc, wc, tokm(acc_war[nm]))
                            par = qi % 2
                            ins = sc.activation(out=PSCP[par][:, kb * 512:kb * 512 + Lkb],
                                                in_=c.PB[kb_bank[(s, h, qi, kb)]][:, 0:Lkb],
                                                func=AF.Exp,
                                                scale=0.125, bias=c.ABIA[:, h:h + 1],
                                                accum_out=c.ACC4[:, 4 * par + kb:
                                                                 4 * par + kb + 1])
                            P.inc("S", ins, nm)
                if L0 and ATTN_STAGE >= 3:
                    for qi in range(16):
                        _wait(sc, wc, P.tok(p + f"uo{s}_{qi}"))
                        ins = sc.activation(out=c.USB[:, 0:128], in_=c.PA4[0:8, 0:128],
                                            func=AF.Identity, scale=0.125)
                        P.inc("S", ins, p + f"uoe{s}_{qi}")
        blk.scalar(_s)

        def _v(v):
            wc = c.wc["vector"]
            if L0:
                _wait(v, wc, P.tok(p + "rez_mm"))
                ins = v.tensor_copy(out=c.REZ[:], in_=c.PA4[:, 0:1])
                P.inc("V", ins, p + "rez_ev")
                _wait(v, wc, prev_hfree)
                for b in range(2):
                    ins = v.memset(QAv[64:67, b, :], 1.0)
                    P.inc("V", ins, p + f"ones{b}")
            for s in range(2):
                for tl in range(4):
                    gt = s * 4 + tl
                    if gt == 0:
                        xw = [prev_xfree]
                    else:
                        xw = [P.tok(p + f"g{gt - 1}_v_{7}")]
                        if ATTN_STAGE >= 1 and "s" in ATTN_FEAT:
                            xw.append(P.tok(p + f"spx{gt - 1}") if L0
                                      else P.tok(p + f"sp{gt - 1}_k"))
                        if L0:
                            xw.append(P.tok(p + f"u{gt - 1}_v"))
                    for w in xw:
                        _wait(v, wc, w)
                    ln_vector(c, P, p, v, gt, xln_war=None)
                    for proj in (projs if ATTN_STAGE >= 1 else []):
                        for m in range(8):
                            if (not L0) and proj != "v" and "r" in ATTN_FEAT:
                                _wait(v, wc, P.tok(p + f"sw{gt}_{proj}_{m}"))
                                _wait(v, wc, P.tok(p + f"ev{gt}_{proj}_{m}"))
                                tloc = tl
                                cosl = CS2v[:, 0, tloc * 512:(tloc + 1) * 512]
                                sinl = CS2v[:, 1, tloc * 512:(tloc + 1) * 512]
                                v.tensor_tensor(out=OSB[:, 0:512], in0=STG[:, m, :],
                                                in1=cosl, op=ALU.mult)
                                v.tensor_tensor(out=OSB[:, 512:1024], in0=c.PA5[:],
                                                in1=sinl, op=ALU.mult)
                                ins = v.tensor_tensor(out=STG[:, m, :], in0=OSB[:, 0:512],
                                                      in1=OSB[:, 512:1024], op=ALU.add)
                                P.inc("V", ins, p + f"rp{gt}_{proj}_{m}")
                            if proj != "v":
                                if "n" not in ATTN_FEAT:
                                    continue
                                _wait(v, wc, P.tok(p + f"nmm{gt}_{proj}_{m}"))
                                col = projs.index(proj) * 32 + m * 4 + tl
                                ins = v.reduce_max(out=c.NRM2[0:2, col:col + 1],
                                                   in_=c.PO3[0:2, :], axis=AX.X)
                                P.inc("V", ins, p + f"nrd{gt}_{proj}_{m}")
                            else:
                                if "t" not in ATTN_FEAT:
                                    continue
                                _wait(v, wc, P.tok(p + f"vt{gt}_{m}"))
                                dstv = VT[:, 2 * m:2 * m + 2, tl * 4:(tl + 1) * 4, :]
                                ins = v.tensor_copy(
                                    out=dstv,
                                    in_=c.PT2[:, (m % 2) * 512:(m % 2) * 512 + 512]
                                    .rearrange("p (k h d) -> p h k d", k=4, h=2))
                                P.inc("V", ins, p + f"vte{gt}_{m}")
                if ATTN_STAGE < 1 or "b" not in ATTN_FEAT:
                    continue
                for proj in ["q", "k"]:
                    for m in range(8):
                        _wait(v, wc, P.tok(p + f"nrd{s * 4 + 3}_{proj}_{m}"))
                ins = v.engine_nop()
                _selfsync(c, v, ins)
                for j in range(16):
                    pi, m = divmod(j, 8)
                    ins = v.reduce_max(out=c.QM[0:2, j:j + 1],
                                       in_=c.NRM2[0:2, pi * 32 + m * 4:pi * 32 + m * 4 + 4],
                                       axis=AX.X)
                _selfsync(c, v, ins)
                v.tensor_tensor(out=c.BHE[0:2, 0:8], in0=c.QM[0:2, 0:8],
                                in1=c.QM[0:2, 8:16], op=ALU.add)
                ins = v.tensor_scalar(out=c.BHE[0:2, 0:8], in0=c.BHE[0:2, 0:8],
                                      scalar1=-1.0 / 16.0, scalar2=None, op0=ALU.mult)
                _selfsync(c, v, ins)
                ins = v.tensor_copy(out=c.QM[0:2, 0:16].rearrange("p (a b) -> p a b", b=2),
                                    in_=c.BHE[0:2, 0:8].unsqueeze(2).broadcast_to([2, 8, 2]))
                _selfsync(c, v, ins)
                if L0:
                    ins = v.tensor_tensor(out=c.QM[0:2, 0:16], in0=c.QM[0:2, 0:16],
                                          in1=c.M2[0:2, :], op=ALU.mult)
                    _selfsync(c, v, ins)
                    ins = v.tensor_tensor(out=c.QM[0:2, 0:16], in0=c.QM[0:2, 0:16],
                                          in1=c.AB2[0:2, :], op=ALU.add)
                else:
                    ins = v.tensor_tensor(out=c.QM[0:2, 0:16], in0=c.QM[0:2, 0:16],
                                          in1=c.M2[0:2, :], op=ALU.mult)
                P.inc("V", ins, p + f"nb{s}")
                _wait(v, wc, P.tok(p + f"nbm{s}"))
                ins = v.tensor_copy(out=c.ABIA[:], in_=c.PA4[:, 0:16])
                P.inc("V", ins, p + f"nbe{s}")
                for h in (range(NH) if ATTN_STAGE >= 2 else []):
                    for qi in range(16):
                        kbs = attn_kbs(layer, qi)
                        for kb, kcs in enumerate(kbs):
                            if (not L0) and kb == len(kbs) - 1:
                                dkb = qi // 4
                                db = kb_bank[(s, h, qi, dkb)]
                                dcol = (qi - 4 * dkb) * 128
                                _wait(v, wc, P.tok(p + f"qk{s}_{h}_{qi}_{dkb}"))
                                ins = v.tensor_tensor(out=c.PB[db][:, dcol:dcol + 128],
                                                      in0=c.PB[db][:, dcol:dcol + 128],
                                                      in1=c.MASKC[:], op=ALU.add)
                                P.inc("V", ins, p + f"dg{s}_{h}_{qi}")
                            pte = p + f"pte{s}_{h}_{qi}_{kb}"
                            Lkb = len(kcs) * 128
                            _wait(v, wc, P.tok(p + f"pt{s}_{h}_{qi}_{kb}"))
                            _wait(v, wc, tokm(pts_war[pte]))
                            ins = v.tensor_copy(
                                out=PTSP[qi % 2][:, kb * 512:kb * 512 + Lkb],
                                in_=PTB[kb_bank[(s, h, qi, kb)]][:, 0:Lkb])
                            P.inc("V", ins, pte)
                        _wait(v, wc, P.tok(p + f"pv{s}_{h}_{qi}_{len(kbs) - 1}"))
                        if h == 0 and qi == 0 and s == 0:
                            _wait(v, wc, prev_hfree)
                        par = qi % 2
                        ins = v.reduce_sum(out=c.RD[:, 2 * par:2 * par + 1],
                                           in_=c.ACC4[:, 4 * par:4 * par + len(kbs)],
                                           axis=AX.X)
                        _selfsync(c, v, ins)
                        ins = v.reciprocal(out=c.RD[:, 2 * par + 1:2 * par + 2],
                                           in_=c.RD[:, 2 * par:2 * par + 1])
                        _selfsync(c, v, ins)
                        ins = v.tensor_scalar(out=ATT[:, qi, h * 64:(h + 1) * 64],
                                              in0=PVB[par][:, 0:64],
                                              scalar1=c.RD[:, 2 * par + 1:2 * par + 2],
                                              scalar2=None, op0=ALU.mult)
                        P.inc("V", ins, p + f"o{s}_{h}_{qi}")
                for qi in (range(16) if ATTN_STAGE >= 3 else []):
                    _wait(v, wc, P.tok(p + f"at{s}_{qi}"))
                    if qi >= 1:
                        _wait(v, wc, P.tok(p + f"go{s}_{qi - 1}_{7}"))
                    elif s == 0:
                        _wait(v, wc, prev_poolfree)
                    ins = v.tensor_copy(out=c.AF_[:], in_=c.PT2[:].rearrange(
                        "p (a b) -> p a b", a=8))
                    P.inc("V", ins, p + f"ate{s}_{qi}")
                    goff = s * SEQ + qi * 128
                    for m in range(8):
                        _wait(v, wc, P.tok(p + f"go{s}_{qi}_{m}"))
                        _wait(v, wc, rfree0[goff // 512])
                        pb = c.PB[(qi * 8 + m) % 2]
                        if L0:
                            _wait(v, wc, P.tok(p + f"xr{s}_{qi}"))
                            v.tensor_tensor(out=c.R[:, m, goff:goff + 128],
                                            in0=c.R[:, m, goff:goff + 128],
                                            in1=c.XLN[:, m, 0:128], op=ALU.add)
                            ins = v.scalar_tensor_tensor(
                                out=c.R[:, m, goff:goff + 128], in0=pb[:, 0:128],
                                scalar=c.REZ[:, 0:1], in1=c.R[:, m, goff:goff + 128],
                                op0=ALU.mult, op1=ALU.add)
                        else:
                            ins = v.tensor_tensor(out=c.R[:, m, goff:goff + 128],
                                                  in0=c.R[:, m, goff:goff + 128],
                                                  in1=pb[:, 0:128], op=ALU.add)
                        P.inc("V", ins, p + f"ro{s}_{qi}_{m}")
        blk.vector(_v)

        def _t(t):
            wc = c.wc["tensor"]
            if L0:
                _wait(t, wc, P.tok(p + "rez_ld"))
                _wait(t, wc, prev_pafree)
                ins = t.matmul(out=c.PA4[:, 0:1], lhsT=c.ONES1F[:], rhs=c.REZI[:],
                               start=True, stop=True)
                P.inc("T", ins, p + "rez_mm")
            for s in range(2):
                for tl in range(4):
                    gt = s * 4 + tl
                    aux = [P.tok(p + "rez_ev")] if (L0 and gt == 0) else []
                    if gt > 0:
                        if L0:
                            aux.append(P.tok(p + f"ue{gt - 1}_v"))
                        else:
                            aux.append(P.tok(p + f"xln{gt - 1}"))
                        if (not L0) and ATTN_STAGE >= 1 and "r" in ATTN_FEAT:
                            aux.append(P.tok(p + f"rp{gt - 1}_k_{7}"))
                    elif s == 0:
                        aux.append(prev_pafree)
                        aux.append(prev_statfree)
                    else:
                        pass
                    if s > 0 and tl == 0:
                        if L0 and ATTN_STAGE >= 3:
                            aux.append(P.tok(p + f"uoe{s - 1}_{15}"))
                        if ATTN_STAGE >= 1 and "b" in ATTN_FEAT:
                            aux.append(P.tok(p + f"nbe{s - 1}"))
                        if ATTN_STAGE >= 2:
                            aux.append(P.tok(p + f"o{s - 1}_{15}_{15}"))
                    for a in aux:
                        _wait(t, wc, a)
                    ln_tensor(c, P, p, t, gt, aux_war=None)
                    ln_tensor_bcast(c, P, p, t, gt)
                    for proj in projs:
                        if L0:
                            _wait(t, wc, P.tok(p + f"lw{gt}_{proj}"))
                            _wait(t, wc, P.tok(p + f"xln{gt}"))
                            if projs.index(proj) > 0:
                                _wait(t, wc, P.tok(p + f"ue{gt}_{projs[projs.index(proj) - 1]}"))
                            for k in range(8):
                                ins = t.matmul(out=c.PA4[0:8, :], lhsT=c.LORA[:, k, :],
                                               rhs=c.XLN[:, k, :], start=(k == 0),
                                               stop=(k == 7))
                            P.inc("T", ins, p + f"u{gt}_{proj}")
                        for m in range(8):
                            gname = p + f"g{gt}_{proj}_{m}"
                            hf = m // 4
                            lname = p + f"wl{gt}_{proj}_{hf}"
                            _wait(t, wc, P.tok(lname))
                            _wait(t, wc, P.tok(p + f"xln{gt}"))
                            _wait(t, wc, pbwar_tok(gname, proj_bank(gt, proj, m)))
                            if L0:
                                _wait(t, wc, P.tok(p + f"ue{gt}_{proj}"))
                            lhs = wl_lhsT_mk(c, ld_slot[lname], m)
                            pb = c.PB[proj_bank(gt, proj, m)]
                            for k in range(8):
                                ins = t.matmul(out=pb[:], lhsT=lhs[k], rhs=c.XLN[:, k, :],
                                               start=(k == 0), stop=(False if L0 else k == 7))
                            if L0:
                                ins = t.matmul(out=pb[:], lhsT=c.LORB[:, m, :],
                                               rhs=c.USB[:, :], start=False, stop=True)
                            P.inc("T", ins, gname)
                            if ATTN_STAGE < 1:
                                continue
                            if (not L0) and proj != "v" and "r" in ATTN_FEAT:
                                _wait(t, wc, P.tok(p + f"ev{gt}_{proj}_{m}"))
                                _wait(t, wc, P.tok(p + f"cs2_{s}"))
                                if m > 0:
                                    _wait(t, wc, P.tok(p + f"rp{gt}_{proj}_{m - 1}"))
                                elif not (gt == 0 and proj == "q"):
                                    pp = "q" if proj == "k" else "k"
                                    pgt = gt if proj == "k" else gt - 1
                                    _wait(t, wc, P.tok(p + f"rp{pgt}_{pp}_{7}"))
                                if s > 0 and ATTN_STAGE >= 2 and gt == s * 4 and                                         proj == "q" and m == 0:
                                    _wait(t, wc, P.tok(p + f"o{s - 1}_{15}_{15}"))
                                ins = t.matmul(out=c.PA5[:], lhsT=c.SWT[:],
                                               rhs=STG[:, m, :], start=True, stop=True)
                                P.inc("T", ins, p + f"sw{gt}_{proj}_{m}")
                            if proj != "v":
                                if "n" not in ATTN_FEAT:
                                    continue
                                _wait(t, wc, P.tok(p + f"nsq{gt}_{proj}_{m}"))
                                if m > 0:
                                    _wait(t, wc, P.tok(p + f"nrd{gt}_{proj}_{m - 1}"))
                                else:
                                    prevnrd = None
                                    if proj == "k":
                                        prevnrd = p + f"nrd{gt}_q_{7}"
                                    elif gt > 0:
                                        prevnrd = p + f"nrd{gt - 1}_k_{7}"
                                    elif s > 0:
                                        prevnrd = None
                                    if prevnrd:
                                        _wait(t, wc, P.tok(prevnrd))
                                    elif s > 0:
                                        _wait(t, wc, P.tok(p + f"nb{s - 1}"))
                                    else:
                                        _wait(t, wc, prev_pofree)
                                ins = t.matmul(out=c.PO3[0:2, :], lhsT=c.H2[:],
                                               rhs=OSB[:, 1024:1536], start=True, stop=True)
                                P.inc("T", ins, p + f"nmm{gt}_{proj}_{m}")
                            else:
                                if "t" not in ATTN_FEAT:
                                    continue
                                _wait(t, wc, P.tok(p + f"ev{gt}_v_{m}"))
                                if m >= 1:
                                    _wait(t, wc, P.tok(p + f"vte{gt}_{m - 1}"))
                                elif gt > 0:
                                    _wait(t, wc, P.tok(p + f"vte{gt - 1}_{7}"))
                                elif s == 0:
                                    _wait(t, wc, prev_ptfree)
                                for kc in range(4):
                                    ins = t.transpose(
                                        out=c.PT2[:, (m % 2) * 512 + kc * 128:
                                                  (m % 2) * 512 + (kc + 1) * 128],
                                        in_=STG[:, m, kc * 128:(kc + 1) * 128],
                                        identity=c.IDENT[:])
                                P.inc("T", ins, p + f"vt{gt}_{m}")
                if ATTN_STAGE < 1 or "b" not in ATTN_FEAT:
                    continue
                _wait(t, wc, P.tok(p + f"nb{s}"))
                if L0:
                    _wait(t, wc, P.tok(p + f"ue{s * 4 + 3}_v"))
                ins = t.matmul(out=c.PA4[:, 0:16], lhsT=c.ONES2F[:], rhs=c.QM[0:2, 0:16],
                               start=True, stop=True)
                P.inc("T", ins, p + f"nbm{s}")
                for h in (range(NH) if ATTN_STAGE >= 2 else []):
                    for qi in range(16):
                        kbs = attn_kbs(layer, qi)
                        for kb, kcs in enumerate(kbs):
                            Lkb = len(kcs) * 128
                            qkn = p + f"qk{s}_{h}_{qi}_{kb}"
                            _wait(t, wc, P.tok(p + f"hq{s}_{h}"))
                            _wait(t, wc, P.tok(p + f"hk{s}_{h}"))
                            if L0:
                                _wait(t, wc, P.tok(p + f"ha{s}_{h}"))
                                _wait(t, wc, P.tok(p + "ones0"))
                                _wait(t, wc, P.tok(p + "ones1"))
                            _wait(t, wc, pbwar_tok(qkn, kb_bank[(s, h, qi, kb)]))
                            ins = t.matmul(out=c.PB[kb_bank[(s, h, qi, kb)]][:, 0:Lkb],
                                           lhsT=QAv[0:KR, h % 2, qi * 128:(qi + 1) * 128],
                                           rhs=KAv[0:KR, h % 2,
                                                   kcs[0] * 128:kcs[0] * 128 + Lkb],
                                           start=True, stop=True)
                            P.inc("T", ins, qkn)
                            ptn = p + f"pt{s}_{h}_{qi}_{kb}"
                            _wait(t, wc, P.tok(p + f"ex{s}_{h}_{qi}_{kb}"))
                            w_ = ptc_war[ptn]
                            _wait(t, wc, P.tok(p + f"vte{s * 4 + 3}_{7}"))
                            if w_ is None:
                                _wait(t, wc, prev_ptfree)
                                _wait(t, wc, pbwar_tok_pb2(s))
                            else:
                                _wait(t, wc, P.tok(w_))
                            dstb = PTB[kb_bank[(s, h, qi, kb)]]
                            for j in range(len(kcs)):
                                ins = t.transpose(
                                    out=dstb[:, j * 128:(j + 1) * 128],
                                    in_=PSCP[qi % 2][:, kb * 512 + j * 128:
                                                     kb * 512 + j * 128 + 128],
                                    identity=c.IDENT[:])
                            P.inc("T", ins, ptn)
                            _wait(t, wc, P.tok(p + f"pte{s}_{h}_{qi}_{kb}"))
                            par = qi % 2
                            if kb == 0:
                                _wait(t, wc, P.tok(p + f"vte{s * 4 + 3}_{7}"))
                                _wait(t, wc, P.tok(p + f"nrd{s * 4 + 3}_k_{7}"))
                                if (not L0) and par == 1:
                                    _wait(t, wc, P.tok(p + f"rp{s * 4 + 3}_k_{7}"))
                                w_ = pvb_war.get((s, h, qi))
                                if w_ is not None:
                                    _wait(t, wc, P.tok(w_))
                                else:
                                    _wait(t, wc, prev_pofree)
                                    _wait(t, wc, prev_pafree)
                            for j, kc in enumerate(kcs):
                                ins = t.matmul(out=PVB[par][:, 0:64],
                                               lhsT=PTSP[par][:, (kb * 4 + j) * 128:
                                                              (kb * 4 + j) * 128 + 128],
                                               rhs=VT[:, h, kc, :],
                                               start=(kb == 0 and j == 0),
                                               stop=(kb == len(kbs) - 1 and
                                                     j == len(kcs) - 1))
                            P.inc("T", ins, p + f"pv{s}_{h}_{qi}_{kb}")
                for qi in (range(16) if ATTN_STAGE >= 3 else []):
                    atn = p + f"at{s}_{qi}"
                    _wait(t, wc, P.tok(p + f"o{s}_{15}_{qi}"))
                    if qi > 0:
                        _wait(t, wc, P.tok(p + f"ate{s}_{qi - 1}"))
                    else:
                        lkb = len(attn_kbs(layer, 15)) - 1
                        _wait(t, wc, P.tok(p + f"pte{s}_{15}_{15}_{lkb}"))
                    for m in range(8):
                        ins = t.transpose(out=c.PT2[:, m * 128:(m + 1) * 128],
                                          in_=ATT[:, qi, m * 128:(m + 1) * 128],
                                          identity=c.IDENT[:])
                    P.inc("T", ins, atn)
                    if L0:
                        _wait(t, wc, P.tok(p + f"lwo{s}"))
                        _wait(t, wc, P.tok(p + f"ate{s}_{qi}"))
                        if qi > 0:
                            _wait(t, wc, P.tok(p + f"uoe{s}_{qi - 1}"))
                        else:
                            _wait(t, wc, P.tok(p + f"nbe{s}"))
                        for k in range(8):
                            ins = t.matmul(out=c.PA4[0:8, 0:128], lhsT=c.LORA[:, k, :],
                                           rhs=c.AF_[:, k, :], start=(k == 0), stop=(k == 7))
                        P.inc("T", ins, p + f"uo{s}_{qi}")
                    for m in range(8):
                        gname = p + f"go{s}_{qi}_{m}"
                        hf = m // 4
                        lname = p + f"wo{s}_{hf}"
                        _wait(t, wc, P.tok(lname))
                        _wait(t, wc, P.tok(p + f"ate{s}_{qi}"))
                        _wait(t, wc, pbwar_tok(gname, (qi * 8 + m) % 2))
                        if L0:
                            _wait(t, wc, P.tok(p + f"uoe{s}_{qi}"))
                        lhs = wl_lhsT_mk(c, ld_slot[lname], m)
                        pb = c.PB[(qi * 8 + m) % 2]
                        for k in range(8):
                            ins = t.matmul(out=pb[:, 0:128], lhsT=lhs[k],
                                           rhs=c.AF_[:, k, :], start=(k == 0),
                                           stop=(False if L0 else k == 7))
                        if L0:
                            ins = t.matmul(out=pb[:, 0:128], lhsT=c.LORB[:, m, :],
                                           rhs=c.USB[:, 0:128], start=False, stop=True)
                        P.inc("T", ins, gname)
        blk.tensor(_t)

    P.done()
    if ATTN_STAGE >= 3:
        last = P.tok(p + f"ro{1}_{15}_{7}")
    elif ATTN_STAGE == 2:
        last = P.tok(p + f"o{1}_{15}_{15}")
    elif ATTN_STAGE == 1:
        last = P.tok(p + f"nbe{1}") if "b" in ATTN_FEAT else P.tok(p + f"ev{7}_v_{7}")
    else:
        last = P.tok(p + f"ev{7}_v_{7}")
    for tt in range(NT):
        c.rtok[tt] = last
        c.rfree[tt] = last
    c.xln_free = last
    c.hbuf_free = last
    c.wb_free = last
    c.stat_free = P.tok(p + f"xln{7}")
    for b in range(3):
        c.pb_free[b] = last
    c.pa_free = last
    c.pt_free = last
    c.po_free = last
    c.pool_free = last
    c.lora_free = last if L0 else getattr(c, "lora_free", None)


def wl_lhsT_mk(c, slot, m):
    off = (m % 4) * 1024
    W = c.WB[:, slot]
    return [W[:, off + k * 128: off + (k + 1) * 128] for k in range(8)]




def run_group(group, x_cur, wd):
    nc = build_program(group)
    need = weights_needed(group)
    in_maps = []
    for ci in range(NCORES):
        m = {"x": x_cur[ci]}
        for n in WSHAPES:
            if n in need:
                m[n] = wd[n]
        for n in CSHAPES:
            m[n] = wd[n]
        in_maps.append(m)
    res = run_bass_kernel_spmd(nc, in_maps, core_ids=list(range(NCORES)))
    return [np.asarray(res.results[ci]["y"]) for ci in range(NCORES)]


def kernel(**inputs):
    wd = host_prepare(inputs)
    xs = shard_x(np.asarray(inputs["L0_attn_x"], np.float32))
    groups = [SUBLAYERS]
    x_cur = xs
    for group in groups:
        x_cur = run_group(group, x_cur, wd)
    return unshard_y(x_cur)


# version 06
# speedup vs baseline: 1.2629x; 1.1349x over previous
import numpy as np
import ml_dtypes
import concourse.bass as bass
import concourse.mybir as mybir
from concourse.bass_utils import run_bass_kernel_spmd

F32 = mybir.dt.float32
BF16 = mybir.dt.bfloat16
U8 = mybir.dt.uint8
AF = mybir.ActivationFunctionType
ALU = mybir.AluOpType
AX = mybir.AxisListType
BF = ml_dtypes.bfloat16

NCORES = 8
SEQ = 2048
DM = 1024
NC_TOK = 2 * SEQ
NT = NC_TOK // 512
NCH = DM // 128
NH = 16
HD = 64
EPS = 1e-5

SUBLAYERS = ["moe0a", "attn0", "moe0b", "ffn1a", "attn1", "ffn1b"]
ATTN_STAGE = 3
ATTN_FEAT = "rntsb"



def tile_lhsT(wT):
    Kt, Mt = wT.shape
    a = wT.reshape(Kt // 128, 128, Mt // 128, 128)
    return np.ascontiguousarray(a.transpose(2, 1, 0, 3))


def host_prepare(g):
    d = {}
    ew = np.asarray(g["L0_ffn_expert_w"], np.float32)
    t = np.stack([tile_lhsT(ew[e]) for e in range(8)])
    d["W_exp"] = np.ascontiguousarray(t.transpose(0, 2, 1, 3, 4))
    d["W_expb"] = np.ascontiguousarray(
        np.asarray(g["L0_ffn_expert_b"], np.float32).reshape(8, 8, 128).transpose(1, 0, 2))
    d["W_rw"] = np.ascontiguousarray(
        np.asarray(g["L0_ffn_router_w"], np.float32).reshape(8, 128, 8).transpose(1, 0, 2))
    d["W_rb"] = np.asarray(g["L0_ffn_router_b"], np.float32).reshape(1, 8)
    for n in ["q", "k", "v", "o"]:
        W = np.asarray(g[f"L0_attn_W{n}"], np.float32)
        d[f"W_a0{n}"] = tile_lhsT(np.ascontiguousarray(W.T))
        A = np.asarray(g[f"L0_attn_A{n}"], np.float32)
        d[f"W_a0A{n}"] = np.ascontiguousarray(A.T.reshape(8, 128, 8).transpose(1, 0, 2))
        Bm = np.asarray(g[f"L0_attn_B{n}"], np.float32)
        d[f"W_a0B{n}"] = np.ascontiguousarray(Bm.T.reshape(8, 8, 128))
        b = np.asarray(g[f"L0_attn_b{n}"], np.float32)
        d[f"W_a0b{n}"] = np.ascontiguousarray(b.reshape(8, 128).T)
        W1 = np.asarray(g[f"L1_attn_w{n}"], np.float32)
        d[f"W_a1{n}"] = tile_lhsT(W1)
    d["W_rez"] = np.asarray(g["L0_attn_rezero"], np.float32).reshape(1, 1)
    for n in ["w1", "w2", "h1", "h2", "w3"]:
        W = np.asarray(g[f"L1_ffn_{n}_w"], np.float32)
        d[f"W_f{n}"] = tile_lhsT(np.ascontiguousarray(W.T))
        s = np.asarray(g[f"L1_ffn_{n}_s"], np.float32)
        d[f"W_f{n}s"] = np.ascontiguousarray(s.reshape(-1, 128).T)
    d["C_ones"] = np.ones((128, 1), BF)
    d["C_ones1"] = np.ones((1, 128), BF)
    d["C_ones1f"] = np.ones((1, 128), np.float32)
    d["C_ident"] = np.eye(128, dtype=BF)
    inv = 1.0 / (10000.0 ** (np.arange(0, HD, 2, dtype=np.float64) / HD))
    ang = np.arange(SEQ, dtype=np.float64)[:, None] * inv[None, :]
    c64 = np.repeat(np.cos(ang).T, 2, axis=0)
    s64 = np.repeat(np.sin(ang).T, 2, axis=0)
    d["C_cos"] = np.concatenate([c64, c64], 0).astype(BF)
    d["C_sin"] = np.concatenate([s64, s64], 0).astype(BF)
    S = np.zeros((HD, HD), np.float32)
    for i in range(0, HD, 2):
        S[i, i + 1] = -1.0
        S[i + 1, i] = 1.0
    S128 = np.zeros((128, 128), np.float32)
    S128[:64, :64] = S
    S128[64:, 64:] = S
    d["C_swapT"] = np.ascontiguousarray(S128.T).astype(BF)
    slopes = 0.5 ** np.arange(NH)
    al = np.zeros((NH, 3, SEQ), np.float64)
    kpos = np.arange(SEQ, dtype=np.float64)
    for h in range(NH):
        v = np.asarray(8.0 * slopes[h] * kpos, np.float32).astype(np.float64)
        hi = np.asarray(v, np.float32).astype(BF).astype(np.float64)
        mid = np.asarray(v - hi, np.float32).astype(BF).astype(np.float64)
        al[h, 0], al[h, 1], al[h, 2] = hi, mid, v - hi - mid
    d["C_alibi"] = al.astype(BF)
    m = np.zeros((128, 128), np.float32)
    m[np.triu_indices(128, 1)] = -30000.0
    d["C_mask"] = m.astype(BF)
    d["C_identf"] = np.eye(128, dtype=np.float32)
    ones2 = np.ones((2, 128), np.float32)
    d["C_ones2f"] = ones2
    h2 = np.zeros((128, 2), np.float32)
    h2[0:64, 0] = 1.0
    h2[64:128, 1] = 1.0
    d["C_h2"] = h2.astype(BF)
    ab2 = np.zeros((2, 16), np.float32)
    for h in range(16):
        ab2[h % 2, h] = -(0.5 ** h) * 2047.0 * 8.0 * 0.125
    d["C_ab2"] = ab2
    m2 = np.zeros((2, 16), np.float32)
    for h in range(16):
        m2[h % 2, h] = 1.0
    d["C_m2"] = m2
    sel = np.zeros((8, 8, 128), np.float32)
    for e in range(8):
        sel[e, e, :] = 1.0
    d["C_sel"] = np.ascontiguousarray(sel.transpose(1, 0, 2)).astype(BF)
    return d


def shard_x(x):
    return [np.ascontiguousarray(np.asarray(x[2 * c:2 * c + 2], np.float32).transpose(0, 2, 1))
            for c in range(NCORES)]


def unshard_y(parts):
    out = np.empty((16, SEQ, DM), np.float32)
    for c, p in enumerate(parts):
        out[2 * c:2 * c + 2] = np.asarray(p).transpose(0, 2, 1)
    return out



class Sem:
    def __init__(self, h, step=1):
        self.h = h
        self.n = 0
        self.step = step

    def inc(self, instr):
        instr.then_inc(self.h, self.step)
        self.n += self.step
        return (self.h, self.n)

    def tok(self):
        return (self.h, self.n)


class Plan:
    def __init__(self, engs):
        self.sems = dict(engs)
        self.base = {e: s.n for e, s in self.sems.items()}
        self.order = {e: [] for e in self.sems}
        self.idx = {}
        self.emitted = {e: 0 for e in self.sems}

    def add(self, eng, name):
        assert name not in self.idx, name
        self.idx[name] = (eng, len(self.order[eng]))
        self.order[eng].append(name)

    def tok(self, name):
        eng, i = self.idx[name]
        s = self.sems[eng]
        return (s.h, self.base[eng] + (i + 1) * s.step)

    def inc(self, eng, instr, name):
        s = self.sems[eng]
        i = self.emitted[eng]
        assert self.order[eng][i] == name, \
            f"plan order mismatch on {eng}: expected {self.order[eng][i]}, got {name}"
        self.emitted[eng] += 1
        t = s.inc(instr)
        assert t == self.tok(name)
        return t

    def done(self):
        for e in self.sems:
            assert self.emitted[e] == len(self.order[e]), \
                f"engine {e}: emitted {self.emitted[e]} of {len(self.order[e])}"


class Ctx:
    pass


def _selfsync(c, eng, instr):
    t = c.X.inc(instr)
    eng.wait_ge(t[0], t[1])


def _wait(eng, cache, tok):
    if tok is None:
        return
    h, v = tok
    if v <= 0:
        return
    if cache.get(h, 0) >= v:
        return
    eng.wait_ge(h, v)
    cache[h] = v


def weights_needed(sublayers):
    need = set()
    for sl in sublayers:
        if sl.startswith("moe"):
            need |= {"W_exp", "W_expb", "W_rw", "W_rb"}
        if sl == "attn0":
            for n in "qkvo":
                need |= {f"W_a0{n}", f"W_a0A{n}", f"W_a0B{n}", f"W_a0b{n}"}
            need.add("W_rez")
        if sl == "attn1":
            need |= {f"W_a1{n}" for n in "qkvo"}
        if sl.startswith("ffn1"):
            need |= {"W_fw1", "W_fw2", "W_fh1", "W_fh2", "W_fw3",
                     "W_fw1s", "W_fw2s", "W_fh1s", "W_fh2s", "W_fw3s"}
    return need


WSHAPES = {
    "W_exp": [8, 128, 8, 8, 128], "W_expb": [8, 8, 128], "W_rw": [128, 8, 8],
    "W_rb": [1, 8], "W_rez": [1, 1],
    "W_fw1": [32, 128, 8, 128], "W_fw2": [32, 128, 8, 128],
    "W_fh1": [32, 128, 32, 128], "W_fh2": [32, 128, 32, 128], "W_fw3": [8, 128, 32, 128],
    "W_fw1s": [128, 32], "W_fw2s": [128, 32], "W_fh1s": [128, 32],
    "W_fh2s": [128, 32], "W_fw3s": [128, 8],
}
for _n in "qkvo":
    WSHAPES[f"W_a0{_n}"] = [8, 128, 8, 128]
    WSHAPES[f"W_a0A{_n}"] = [128, 8, 8]
    WSHAPES[f"W_a0B{_n}"] = [8, 8, 128]
    WSHAPES[f"W_a0b{_n}"] = [128, 8]
    WSHAPES[f"W_a1{_n}"] = [8, 128, 8, 128]
CSHAPES = {"C_ones": ([128, 1], BF16), "C_ones1": ([1, 128], BF16),
           "C_ident": ([128, 128], BF16), "C_cos": ([128, SEQ], BF16),
           "C_sin": ([128, SEQ], BF16), "C_swapT": ([128, 128], BF16),
           "C_alibi": ([NH, 3, SEQ], BF16), "C_mask": ([128, 128], BF16),
           "C_sel": ([8, 8, 128], BF16), "C_ones1f": ([1, 128], F32),
           "C_identf": ([128, 128], F32), "C_ones2f": ([2, 128], F32),
           "C_h2": ([128, 2], BF16), "C_ab2": ([2, 16], F32), "C_m2": ([2, 16], F32)}
CAST_W = ["W_exp", "W_fw1", "W_fw2", "W_fh1", "W_fh2", "W_fw3",
          "W_a0q", "W_a0k", "W_a0v", "W_a0o", "W_a1q", "W_a1k", "W_a1v", "W_a1o"]


def build_program(sublayers):
    nc = bass.Bass("TRN2", target_bir_lowering=False, dynamic_dma_scratch_size=16384)
    c = Ctx()
    c.nc = nc
    need = weights_needed(sublayers)
    c.need = need

    c.x_in = nc.dram_tensor("x", [2, DM, SEQ], F32, kind="ExternalInput").ap()
    c.y_out = nc.dram_tensor("y", [2, DM, SEQ], F32, kind="ExternalOutput").ap()
    c.win = {n: nc.dram_tensor(n, s, F32, kind="ExternalInput").ap()
             for n, s in WSHAPES.items() if n in need}
    c.cin = {n: nc.dram_tensor(n, s, dt, kind="ExternalInput").ap()
             for n, (s, dt) in CSHAPES.items()}
    c.wsc = {n: nc.dram_tensor(n + "_b", WSHAPES[n], BF16).ap()
             for n in CAST_W if n in need}
    c.q_sp = nc.dram_tensor("q_sp", [2, NCH, 128, SEQ], BF16).ap()
    c.k_sp = nc.dram_tensor("k_sp", [2, NCH, 128, SEQ], BF16).ap()
    c.v_sp = nc.dram_tensor("v_sp", [2, NCH, 128, SEQ], BF16).ap()
    c.xln_spd = nc.dram_tensor("xln_spd", [2, NCH, 128, SEQ], BF16).ap()

    import contextlib
    with contextlib.ExitStack() as st:
        e = st.enter_context
        c.R = e(nc.sbuf_tensor("R", [128, NCH, NC_TOK], BF16))
        c.XLN = e(nc.sbuf_tensor("XLN", [128, NCH, 512], BF16))
        c.HA = e(nc.sbuf_tensor("HA", [128, 32, 512], BF16))
        c.HB = e(nc.sbuf_tensor("HB", [128, 32, 512], BF16))
        c.HC = e(nc.sbuf_tensor("HC", [128, 32, 512], BF16))
        c.WB = e(nc.sbuf_tensor("WB", [128, 2, 4096], BF16))
        c.XSQ = e(nc.sbuf_tensor("XSQ", [128, 2, 512], BF16))
        c.STT = e(nc.sbuf_tensor("STT", [1, 1024], F32))
        c.BCA = e(nc.sbuf_tensor("BCA", [128, 2, 512], BF16))
        c.POOL = e(nc.sbuf_tensor("POOL", [128, 5120], U8))
        c.LG = e(nc.sbuf_tensor("LG", [128, 4, 8], F32))
        c.LGS = e(nc.sbuf_tensor("LGS", [128, 4, 8], F32))
        c.CFB = e(nc.sbuf_tensor("CFB", [128, 4, 8], BF16))
        c.RWS = e(nc.sbuf_tensor("RWS", [128, 8, 8], BF16))
        c.RBS = e(nc.sbuf_tensor("RBS", [1, 8], BF16))
        c.ONES = e(nc.sbuf_tensor("ONES", [128, 1], BF16))
        c.ONES1 = e(nc.sbuf_tensor("ONES1", [1, 128], BF16))
        c.ONES1F = e(nc.sbuf_tensor("ONES1F", [1, 128], F32))
        c.IDENT = e(nc.sbuf_tensor("IDENT", [128, 128], BF16))
        c.MASKC = e(nc.sbuf_tensor("MASKC", [128, 128], BF16))
        c.SWT = e(nc.sbuf_tensor("SWT", [128, 128], BF16))
        c.SCL = e(nc.sbuf_tensor("SCL", [128, 5, 32], F32))
        c.ACC4 = e(nc.sbuf_tensor("ACC4", [128, 8], F32))
        c.RD = e(nc.sbuf_tensor("RD", [128, 4], F32))
        c.LORA = e(nc.sbuf_tensor("LORA", [128, 8, 8], BF16))
        c.LBIA = e(nc.sbuf_tensor("LBIA", [128, 8], F32))
        c.REZ = e(nc.sbuf_tensor("REZ", [128, 1], F32))
        c.OSTM = e(nc.sbuf_tensor("OSTM", [128, 2, 512], F32))
        c.ABIA = e(nc.sbuf_tensor("ABIA", [128, NH], F32))
        c.NRM2 = e(nc.sbuf_tensor("NRM2", [2, 64], F32))
        c.QM = e(nc.sbuf_tensor("QM", [2, 16], F32))
        c.BHE = e(nc.sbuf_tensor("BHE", [2, 16], F32))
        c.IDENTF = e(nc.sbuf_tensor("IDENTF", [128, 128], F32))
        c.ONES2F = e(nc.sbuf_tensor("ONES2F", [2, 128], F32))
        c.H2 = e(nc.sbuf_tensor("H2", [128, 2], BF16))
        c.AB2 = e(nc.sbuf_tensor("AB2", [2, 16], F32))
        c.M2 = e(nc.sbuf_tensor("M2", [2, 16], F32))
        c.REZI = e(nc.sbuf_tensor("REZI", [1, 1], F32))
        pv = c.POOL[:]
        c.CT8 = pv.bitcast(BF16)[0:8, 0:512]
        c.EBS = pv.bitcast(BF16)[0:8, 512:1536].rearrange("r (m o) -> r m o", m=8)
        c.SEL = pv.bitcast(BF16)[0:8, 1536:2560].rearrange("r (s o) -> r s o", s=8)
        c.LORB = pv.bitcast(BF16)[0:8, 0:1024].rearrange("r (m o) -> r m o", m=8)
        c.USB = pv.bitcast(BF16)[0:8, 1024:1536]
        c.AF_ = pv.bitcast(BF16)[:, 1536:2560].rearrange("p (a b) -> p a b", a=8)
        c.PB = [e(nc.psum_tensor(f"PBm{i}", [128, 512], F32)) for i in range(4)]
        c.PA4 = e(nc.psum_tensor("PA4", [128, 512], F32))
        c.PA5 = e(nc.psum_tensor("PA5", [128, 512], F32))
        c.PO3 = e(nc.psum_tensor("PO3", [128, 512], F32))
        c.PT2 = e(nc.psum_tensor("PT2", [128, 1024], BF16))
        c.X = Sem(e(nc.semaphore("s_x")))
        c.SY = Sem(e(nc.semaphore("s_sy")), 16)
        c.G = Sem(e(nc.semaphore("s_g")), 16)
        c.T = Sem(e(nc.semaphore("s_t")))
        c.V = Sem(e(nc.semaphore("s_v")))
        c.S = Sem(e(nc.semaphore("s_s")))
        c.wc = {k: {} for k in ["sync", "gpsimd", "scalar", "vector", "tensor"]}
        emit_all(c, sublayers)
    return nc


def emit_all(c, sublayers):
    nc = c.nc
    c.cast_tok = {}
    with nc.Block("init") as blk:
        def _g(g):
            for sq in range(2):
                for ch in range(NCH):
                    c.rload = c.G.inc(g.dma_start(
                        out=c.R[:, ch, sq * SEQ:(sq + 1) * SEQ],
                        in_=c.x_in[sq, ch * 128:(ch + 1) * 128, :]))
            if "W_rw" in c.win:
                c.G.inc(g.dma_start(out=c.RWS[:], in_=c.win["W_rw"]))
                c.G.inc(g.dma_start(out=c.RBS[:], in_=c.win["W_rb"]))
            if "W_fw1s" in c.win:
                for i, n in enumerate(["W_fw1s", "W_fw2s", "W_fh1s", "W_fh2s"]):
                    c.G.inc(g.dma_start(out=c.SCL[:, i, :], in_=c.win[n]))
                c.G.inc(g.dma_start(out=c.SCL[:, 4, 0:8], in_=c.win["W_fw3s"]))
            c.small_tok = c.G.tok()
            order = [n for n in CAST_W if n in c.wsc]
            for n in order:
                src, dst = c.win[n], c.wsc[n]
                per = int(np.prod(src.shape[1:])) * 4
                step = max(1, (8 << 20) // per)
                tk = None
                for i0 in range(0, src.shape[0], step):
                    i1 = min(src.shape[0], i0 + step)
                    tk = c.G.inc(g.dma_start(out=dst[i0:i1], in_=src[i0:i1]))
                c.cast_tok[n] = tk
        blk.gpsimd(_g)

        def _sy(sy):
            for dst, src in [(c.ONES, c.cin["C_ones"]), (c.ONES1, c.cin["C_ones1"]),
                             (c.ONES1F, c.cin["C_ones1f"]),
                             (c.IDENT, c.cin["C_ident"]), (c.MASKC, c.cin["C_mask"]),
                             (c.SWT, c.cin["C_swapT"]), (c.IDENTF, c.cin["C_identf"]),
                             (c.ONES2F, c.cin["C_ones2f"]), (c.H2, c.cin["C_h2"]),
                             (c.AB2, c.cin["C_ab2"]), (c.M2, c.cin["C_m2"])]:
                c.SY.inc(sy.dma_start(out=dst[:], in_=src))
            c.const_tok = c.SY.tok()
        blk.sync(_sy)

    gfin = c.G.tok()
    c.rload = gfin
    c.small_tok = gfin
    for k in list(c.cast_tok):
        c.cast_tok[k] = gfin
    c.rtok = [c.rload] * NT
    c.rfree = [None] * NT
    c.xln_free = None
    c.hbuf_free = None
    c.wb_free = None
    c.stat_free = None
    c.pool_free = None
    c.pb_free = [None] * 4
    c.pa_free = None
    c.pt_free = None
    c.po_free = None
    final = len(sublayers) - 1

    for i, sl in enumerate(sublayers):
        if sl.startswith("moe"):
            emit_moe(c, is_final=(i == final))
        elif sl.startswith("ffn1"):
            emit_ffn1(c, is_final=(i == final))
        elif sl == "attn0":
            emit_attn(c, 0)
        elif sl == "attn1":
            emit_attn(c, 1)
    if sublayers[-1].startswith("attn"):
        emit_store(c)

    with nc.Block("fin") as blk:
        def _g(g):
            _wait(g, c.wc["gpsimd"], c.G.tok())
        blk.gpsimd(_g)

        def _sy(sy):
            _wait(sy, c.wc["sync"], c.SY.tok())
        blk.sync(_sy)



def plan_ln(P, p, tt):
    for ch in range(NCH):
        P.add("S", f"{p}sq{tt}_{ch}")
    P.add("S", f"{p}rstd{tt}")
    for ch in range(NCH):
        P.add("T", f"{p}sqmm{tt}_{ch}")
    P.add("T", f"{p}bc{tt}")
    P.add("V", f"{p}comb1_{tt}")
    P.add("V", f"{p}comb2_{tt}")
    P.add("V", f"{p}xln{tt}")


def ln_scalar(c, P, p, s, tt, prev_sq_reader=None, rtok=None):
    wc = c.wc["scalar"]
    _wait(s, wc, rtok if rtok is not None else c.rtok[tt])
    if tt == 0 and c.stat_free is not None:
        _wait(s, wc, c.stat_free)
    for ch in range(NCH):
        j = tt * NCH + ch
        if j >= 2:
            t2, c2 = divmod(j - 2, NCH)
            _wait(s, wc, P.tok(f"{p}sqmm{t2}_{c2}"))
        elif prev_sq_reader is not None:
            _wait(s, wc, prev_sq_reader)
        ins = s.activation(out=c.XSQ[:, j % 2, :], in_=c.R[:, ch, tt * 512:(tt + 1) * 512],
                           func=AF.Square)
        P.inc("S", ins, f"{p}sq{tt}_{ch}")
    _wait(s, wc, P.tok(f"{p}comb1_{tt}"))
    if tt > 0:
        _wait(s, wc, P.tok(f"{p}bc{tt - 1}"))
    s.activation(out=c.STT[0:1, 512:1024], in_=c.STT[0:1, 512:1024], func=AF.Ln)
    ins = s.activation(out=c.STT[0:1, 512:1024], in_=c.STT[0:1, 512:1024], func=AF.Exp,
                       scale=-0.5)
    P.inc("S", ins, f"{p}rstd{tt}")


def ln_tensor(c, P, p, t, tt, aux_war=None, rtok=None):
    wc = c.wc["tensor"]
    _wait(t, wc, c.const_tok)
    _wait(t, wc, rtok if rtok is not None else c.rtok[tt])
    _wait(t, wc, aux_war)
    if tt == 0:
        _wait(t, wc, c.stat_free)
        _wait(t, wc, c.pa_free)
    for ch in range(NCH):
        t.matmul(out=c.PA4[0:1, :], lhsT=c.ONES[:], rhs=c.R[:, ch, tt * 512:(tt + 1) * 512],
                 start=(ch == 0), stop=(ch == NCH - 1))
        _wait(t, wc, P.tok(f"{p}sq{tt}_{ch}"))
        ins = t.matmul(out=c.PA5[0:1, :], lhsT=c.ONES[:],
                       rhs=c.XSQ[:, (tt * NCH + ch) % 2, :],
                       start=(ch == 0), stop=(ch == NCH - 1))
        P.inc("T", ins, f"{p}sqmm{tt}_{ch}")


def ln_tensor_bcast(c, P, p, t, tt):
    wc = c.wc["tensor"]
    _wait(t, wc, P.tok(f"{p}comb2_{tt}"))
    _wait(t, wc, P.tok(f"{p}rstd{tt}"))
    t.matmul(out=c.PA4[:], lhsT=c.ONES1F[:], rhs=c.STT[0:1, 512:1024], start=True, stop=True)
    ins = t.matmul(out=c.PA5[:], lhsT=c.ONES1F[:], rhs=c.STT[0:1, 0:512], start=True, stop=True)
    P.inc("T", ins, f"{p}bc{tt}")


def ln_vector(c, P, p, v, tt, xln_war=None, rsl=None):
    wc = c.wc["vector"]
    _wait(v, wc, P.tok(f"{p}sqmm{tt}_{NCH - 1}"))
    if tt > 0:
        _wait(v, wc, P.tok(f"{p}bc{tt - 1}"))
    v.tensor_scalar(out=c.STT[0:1, 0:512], in0=c.PA4[0:1, :], scalar1=1.0 / DM,
                    scalar2=None, op0=ALU.mult)
    v.tensor_scalar(out=c.STT[0:1, 512:1024], in0=c.PA5[0:1, :], scalar1=1.0 / DM,
                    scalar2=None, op0=ALU.mult)
    v.tensor_tensor(out=c.PA4[0:1, :], in0=c.STT[0:1, 0:512], in1=c.STT[0:1, 0:512],
                    op=ALU.mult)
    v.tensor_tensor(out=c.STT[0:1, 512:1024], in0=c.STT[0:1, 512:1024], in1=c.PA4[0:1, :],
                    op=ALU.subtract)
    ins = v.tensor_scalar(out=c.STT[0:1, 512:1024], in0=c.STT[0:1, 512:1024], scalar1=EPS,
                          scalar2=None, op0=ALU.add)
    P.inc("V", ins, f"{p}comb1_{tt}")
    _wait(v, wc, P.tok(f"{p}rstd{tt}"))
    v.tensor_tensor(out=c.STT[0:1, 0:512], in0=c.STT[0:1, 0:512], in1=c.STT[0:1, 512:1024],
                    op=ALU.mult)
    ins = v.tensor_scalar(out=c.STT[0:1, 0:512], in0=c.STT[0:1, 0:512], scalar1=-1.0,
                          scalar2=None, op0=ALU.mult)
    P.inc("V", ins, f"{p}comb2_{tt}")
    _wait(v, wc, P.tok(f"{p}bc{tt}"))
    _wait(v, wc, xln_war)
    v.tensor_copy(out=c.BCA[:, 0, :], in_=c.PA4[:])
    v.tensor_copy(out=c.BCA[:, 1, :], in_=c.PA5[:])
    rs = rsl if rsl is not None else (lambda ch: c.R[:, ch, tt * 512:(tt + 1) * 512])
    for ch in range(NCH):
        v.tensor_tensor(out=c.XLN[:, ch, :], in0=rs(ch), in1=c.BCA[:, 0, :], op=ALU.mult)
        ins = v.tensor_tensor(out=c.XLN[:, ch, :], in0=c.XLN[:, ch, :], in1=c.BCA[:, 1, :],
                              op=ALU.add)
    P.inc("V", ins, f"{p}xln{tt}")



FFN_NG = 136
FFN_NLD = 88


def ffn_group_stage(g):
    if g < 32:
        return "w1", g
    if g < 64:
        return "w2", g - 32
    if g < 96:
        return "h1", g - 64
    if g < 128:
        return "h2", g - 96
    return "w3", g - 128


def ffn_load_of_group(g):
    st, m = ffn_group_stage(g)
    if st == "w1":
        return m // 4
    if st == "w2":
        return 8 + m // 4
    if st == "h1":
        return 16 + m
    if st == "h2":
        return 48 + m
    return 80 + m


def ffn_load_spec(li):
    if li < 8:
        return "w1", list(range(li * 4, li * 4 + 4)), li
    if li < 16:
        j = li - 8
        return "w2", list(range(32 + j * 4, 32 + j * 4 + 4)), j
    if li < 48:
        j = li - 16
        return "h1", [64 + j], j
    if li < 80:
        j = li - 48
        return "h2", [96 + j], j
    j = li - 80
    return "w3", [128 + j], j


def emit_ffn1(c, is_final):
    nc = c.nc
    P = Plan({"T": c.T, "S": c.S, "V": c.V, "SY": c.SY, "G": c.G})
    emit_ffn1.count = getattr(emit_ffn1, "count", 0) + 1
    p = f"f{emit_ffn1.count}_"
    for tt in range(NT):
        plan_ln(P, p, tt)
        for li in range(FFN_NLD):
            P.add("SY", f"{p}ld{tt}_{li}")
        for g in range(FFN_NG):
            P.add("T", f"{p}g{tt}_{g}")
        for g in range(128):
            P.add("S", f"{p}ev{tt}_{g}")
        for m in range(32):
            P.add("V", f"{p}h0_{tt}_{m}")
        for m in range(8):
            P.add("V", f"{p}res{tt}_{m}")
        if is_final:
            for m in range(8):
                P.add("G", f"{p}out{tt}_{m}")

    prev_hfree = c.hbuf_free
    prev_xfree = c.xln_free
    prev_wbfree = c.wb_free
    prev_pbfree = list(c.pb_free)
    prev_poolfree = c.pool_free

    def reader_tok(tt, g):
        st, m = ffn_group_stage(g)
        if st == "w3":
            return P.tok(f"{p}res{tt}_{m}")
        return P.tok(f"{p}ev{tt}_{g}")

    with nc.Block(f"ffn{emit_ffn1.count}") as blk:
        def _sy(sy):
            wc = c.wc["sync"]
            for tt in range(NT):
                for li in range(FFN_NLD):
                    st, gs, j = ffn_load_spec(li)
                    gli = tt * FFN_NLD + li
                    if gli >= 2:
                        ptt, pli = divmod(gli - 2, FFN_NLD)
                        _, pgs, _ = ffn_load_spec(pli)
                        _wait(sy, wc, reader_tok(ptt, pgs[-1]))
                    else:
                        _wait(sy, wc, prev_wbfree)
                    _wait(sy, wc, c.cast_tok["W_f" + st])
                    src = c.wsc["W_f" + st]
                    if st in ("w1", "w2"):
                        sl = src[j * 4:(j + 1) * 4]
                        dst = c.WB[:, gli % 2].rearrange("p (m ko) -> p m ko", m=4)
                    else:
                        sl = src[j:j + 1]
                        dst = c.WB[:, gli % 2].rearrange("p (m ko) -> p m ko", m=1)
                    ins = sy.dma_start(out=dst, in_=sl.rearrange("m p k o -> p m (k o)"))
                    P.inc("SY", ins, f"{p}ld{tt}_{li}")
        blk.sync(_sy)

        def _s(s):
            wc = c.wc["scalar"]
            _wait(s, wc, c.small_tok)
            for tt in range(NT):
                ln_scalar(c, P, p, s, tt,
                          prev_sq_reader=(None if tt > 0 else c.stat_free))
                for g in range(128):
                    st, m = ffn_group_stage(g)
                    _wait(s, wc, P.tok(f"{p}g{tt}_{g}"))
                    pb = c.PB[g % 4]
                    if st == "w1":
                        ins = s.activation(out=c.HA[:, m, :], in_=pb[:], func=AF.Silu,
                                           scale=c.SCL[:, 0, m:m + 1])
                    elif st == "w2":
                        ins = s.activation(out=c.HB[:, m, :], in_=pb[:], func=AF.Identity,
                                           scale=c.SCL[:, 1, m:m + 1])
                    elif st == "h1":
                        _wait(s, wc, P.tok(f"{p}h0_{tt}_{m}"))
                        ins = s.activation(out=c.HA[:, m, :], in_=pb[:], func=AF.Gelu,
                                           scale=c.SCL[:, 2, m:m + 1])
                    else:
                        ins = s.activation(out=c.HB[:, m, :], in_=pb[:], func=AF.Gelu,
                                           scale=c.SCL[:, 3, m:m + 1])
                    P.inc("S", ins, f"{p}ev{tt}_{g}")
        blk.scalar(_s)

        def _v(v):
            wc = c.wc["vector"]
            for tt in range(NT):
                xw = prev_xfree if tt == 0 else P.tok(f"{p}g{tt - 1}_{63}")
                ln_vector(c, P, p, v, tt, xln_war=xw)
                for m in range(32):
                    _wait(v, wc, P.tok(f"{p}ev{tt}_{m}"))
                    _wait(v, wc, P.tok(f"{p}ev{tt}_{32 + m}"))
                    if tt > 0:
                        _wait(v, wc, P.tok(f"{p}g{tt - 1}_{95}"))
                    else:
                        _wait(v, wc, prev_hfree)
                    ins = v.tensor_tensor(out=c.HC[:, m, :], in0=c.HA[:, m, :],
                                          in1=c.HB[:, m, :], op=ALU.mult)
                    P.inc("V", ins, f"{p}h0_{tt}_{m}")
                for m in range(8):
                    g = 128 + m
                    _wait(v, wc, P.tok(f"{p}g{tt}_{g}"))
                    _wait(v, wc, c.rfree[tt])
                    pb = c.PB[g % 4]
                    if is_final:
                        if tt == 0 and m < 2:
                            _wait(v, wc, prev_poolfree)
                        if m >= 2:
                            _wait(v, wc, P.tok(f"{p}out{tt}_{m - 2}"))
                        elif tt > 0:
                            _wait(v, wc, P.tok(f"{p}out{tt - 1}_{m + 6}"))
                        v.tensor_scalar(out=c.OSTM[:, m % 2, :], in0=pb[:],
                                        scalar1=c.SCL[:, 4, m:m + 1], scalar2=0.5,
                                        op0=ALU.mult, op1=ALU.mult)
                        ins = v.tensor_tensor(out=c.OSTM[:, m % 2, :], in0=c.OSTM[:, m % 2, :],
                                              in1=c.R[:, m, tt * 512:(tt + 1) * 512],
                                              op=ALU.add)
                    else:
                        v.tensor_scalar(out=pb[:], in0=pb[:],
                                        scalar1=c.SCL[:, 4, m:m + 1], scalar2=0.5,
                                        op0=ALU.mult, op1=ALU.mult)
                        ins = v.tensor_tensor(out=c.R[:, m, tt * 512:(tt + 1) * 512],
                                              in0=c.R[:, m, tt * 512:(tt + 1) * 512],
                                              in1=pb[:], op=ALU.add)
                    P.inc("V", ins, f"{p}res{tt}_{m}")
        blk.vector(_v)

        def _t(t):
            wc = c.wc["tensor"]
            for tt in range(NT):
                aux_war = P.tok(f"{p}xln{tt - 1}") if tt > 0 else c.stat_free
                ln_tensor(c, P, p, t, tt, aux_war=aux_war)
                ln_tensor_bcast(c, P, p, t, tt)
                for g in range(FFN_NG):
                    st, m = ffn_group_stage(g)
                    li = ffn_load_of_group(g)
                    _wait(t, wc, P.tok(f"{p}ld{tt}_{li}"))
                    pg = g - 4
                    if pg >= 0:
                        _wait(t, wc, reader_tok(tt, pg))
                    elif tt > 0:
                        _wait(t, wc, reader_tok(tt - 1, FFN_NG - 4 + g))
                    else:
                        _wait(t, wc, prev_pbfree[g % 4])
                    if st in ("w1", "w2"):
                        _wait(t, wc, P.tok(f"{p}xln{tt}"))
                        rhs = [c.XLN[:, k, :] for k in range(8)]
                    elif st == "h1":
                        _wait(t, wc, P.tok(f"{p}h0_{tt}_{31}"))
                        rhs = [c.HC[:, k, :] for k in range(32)]
                    elif st == "h2":
                        _wait(t, wc, P.tok(f"{p}ev{tt}_{64 + 31}"))
                        rhs = [c.HA[:, k, :] for k in range(32)]
                    else:
                        _wait(t, wc, P.tok(f"{p}ev{tt}_{96 + 31}"))
                        rhs = [c.HB[:, k, :] for k in range(32)]
                    W = c.WB[:, (tt * FFN_NLD + li) % 2]
                    if st in ("w1", "w2"):
                        off = (m % 4) * 1024
                    else:
                        off = 0
                    pb = c.PB[g % 4]
                    nk = len(rhs)
                    for k in range(nk):
                        ins = t.matmul(out=pb[:], lhsT=W[:, off + k * 128: off + (k + 1) * 128],
                                       rhs=rhs[k], start=(k == 0), stop=(k == nk - 1))
                    P.inc("T", ins, f"{p}g{tt}_{g}")
        blk.tensor(_t)

        if is_final:
            def _g(g):
                wc = c.wc["gpsimd"]
                yv = c.y_out.rearrange("s (ch q) t -> q ch s t", q=128)
                for tt in range(NT):
                    for m in range(8):
                        _wait(g, wc, P.tok(f"{p}res{tt}_{m}"))
                        dst = yv[:, m, tt // 4, (tt % 4) * 512:(tt % 4) * 512 + 512]
                        ins = g.dma_start(out=dst, in_=c.OSTM[:, m % 2, :])
                        P.inc("G", ins, f"{p}out{tt}_{m}")
            blk.gpsimd(_g)

    P.done()
    for tt in range(NT):
        c.rtok[tt] = P.tok(f"{p}res{tt}_{7}")
        c.rfree[tt] = P.tok(f"{p}res{tt}_{7}")
    c.xln_free = P.tok(f"{p}g{NT - 1}_{63}")
    c.hbuf_free = P.tok(f"{p}res{NT - 1}_{7}")
    c.wb_free = P.tok(f"{p}res{NT - 1}_{7}")
    c.stat_free = P.tok(f"{p}xln{NT - 1}")
    for b in range(4):
        g_last = max(g for g in range(FFN_NG) if g % 4 == b)
        c.pb_free[b] = reader_tok(NT - 1, g_last)
    c.pa_free = P.tok(f"{p}xln{NT - 1}")
    if is_final:
        c.pool_free = P.tok(f"{p}out{NT - 1}_{7}")



def emit_moe(c, is_final):
    nc = c.nc
    P = Plan({"T": c.T, "S": c.S, "V": c.V, "SY": c.SY, "G": c.G})
    emit_moe.count = getattr(emit_moe, "count", 0) + 1
    p = f"m{emit_moe.count}_"
    P.add("G", f"{p}poolA")
    P.add("G", f"{p}pool")
    for tt in range(NT):
        plan_ln(P, p, tt)
        for gr in range(4):
            P.add("T", f"{p}lg{tt}_{gr}")
        for gr in range(4):
            P.add("V", f"{p}cfc{tt}_{gr}")
        P.add("V", f"{p}cfA{tt}")
        for gr in range(4):
            P.add("S", f"{p}cex{tt}_{gr}")
        P.add("V", f"{p}cf2A{tt}")
        for gr in range(4):
            P.add("T", f"{p}ct{tt}_{gr}")
        for gr in range(4):
            P.add("V", f"{p}cte{tt}_{gr}")
        for e in range(8):
            P.add("T", f"{p}cb{tt}_{e}")
            P.add("V", f"{p}cbe{tt}_{e}")
        for mh in range(2):
            for e in range(8):
                P.add("SY", f"{p}ld{tt}_{mh}_{e}")
                P.add("V", f"{p}xe{tt}_{mh}_{e}")
                P.add("T", f"{p}em{tt}_{mh}_{e}")
            for m in range(4):
                P.add("V", f"{p}res{tt}_{mh}_{m}")
                if is_final:
                    P.add("G", f"{p}out{tt}_{mh}_{m}")

    prev_hfree = c.hbuf_free
    prev_xfree = c.xln_free
    prev_wbfree = c.wb_free
    prev_pbfree = list(c.pb_free)
    prev_pafree = c.pa_free
    prev_ptfree = c.pt_free
    prev_poolfree = c.pool_free

    with nc.Block(f"moe{emit_moe.count}") as blk:
        def _g(g):
            wc = c.wc["gpsimd"]
            _wait(g, wc, c.G.tok())
            _wait(g, wc, prev_poolfree)
            ins = g.dma_start(out=c.EBS, in_=c.win["W_expb"].rearrange("m r o -> r m o"))
            P.inc("G", ins, f"{p}poolA")
            ins = g.dma_start(out=c.SEL, in_=c.cin["C_sel"])
            P.inc("G", ins, f"{p}pool")
        blk.gpsimd(_g)

        def _sy(sy):
            wc = c.wc["sync"]
            for tt in range(NT):
                for mh in range(2):
                    for e in range(8):
                        gli = (tt * 2 + mh) * 8 + e
                        if gli >= 2:
                            ptt, r = divmod(gli - 2, 16)
                            pmh, pe = divmod(r, 8)
                            _wait(sy, wc, P.tok(f"{p}em{ptt}_{pmh}_{pe}"))
                        else:
                            _wait(sy, wc, prev_wbfree)
                        _wait(sy, wc, c.cast_tok["W_exp"])
                        src = c.wsc["W_exp"][e, :, mh * 4:(mh + 1) * 4]
                        ins = sy.dma_start(out=c.WB[:, gli % 2],
                                           in_=src.rearrange("p m k o -> p (m k o)"))
                        P.inc("SY", ins, f"{p}ld{tt}_{mh}_{e}")
        blk.sync(_sy)

        def _s(s):
            wc = c.wc["scalar"]
            for tt in range(NT):
                ln_scalar(c, P, p, s, tt,
                          prev_sq_reader=(None if tt > 0 else c.stat_free))
                for gr in range(4):
                    _wait(s, wc, P.tok(f"{p}cfA{tt}"))
                    ins = s.activation(out=c.LGS[:, gr, :], in_=c.LGS[:, gr, :], func=AF.Exp,
                                       accum_out=c.ACC4[:, gr:gr + 1])
                    P.inc("S", ins, f"{p}cex{tt}_{gr}")
        blk.scalar(_s)

        def _v(v):
            wc = c.wc["vector"]
            for tt in range(NT):
                xw = prev_xfree if tt == 0 else P.tok(f"{p}lg{tt - 1}_{3}")
                ln_vector(c, P, p, v, tt, xln_war=xw)
                for gr in range(4):
                    _wait(v, wc, P.tok(f"{p}lg{tt}_{gr}"))
                    pa = c.PA4 if gr % 2 == 0 else c.PA5
                    ins = v.tensor_copy(out=c.LG[:, gr, :], in_=pa[:, 0:8])
                    P.inc("V", ins, f"{p}cfc{tt}_{gr}")
                _wait(v, wc, P.tok(f"{p}cfc{tt}_{3}"))
                for gr in range(4):
                    ins = v.reduce_max(out=c.RD[:, gr:gr + 1], in_=c.LG[:, gr, :],
                                       axis=AX.X, negate=True)
                _selfsync(c, v, ins)
                for gr in range(4):
                    v.tensor_scalar(out=c.LGS[:, gr, :], in0=c.LG[:, gr, :],
                                    scalar1=c.RD[:, gr:gr + 1], scalar2=None, op0=ALU.add)
                for gr in range(4):
                    ins = v.tensor_scalar(out=c.LG[:, gr, :], in0=c.LG[:, gr, :],
                                          scalar1=c.RD[:, gr:gr + 1], scalar2=0.0,
                                          op0=ALU.add, op1=ALU.is_ge)
                P.inc("V", ins, f"{p}cfA{tt}")
                for gr in range(4):
                    _wait(v, wc, P.tok(f"{p}cex{tt}_{gr}"))
                ins = v.reciprocal(out=c.RD[:, 0:4], in_=c.ACC4[:, 0:4])
                _selfsync(c, v, ins)
                for gr in range(4):
                    ins = v.tensor_scalar(out=c.LG[:, gr, :], in0=c.LG[:, gr, :],
                                          scalar1=c.RD[:, gr:gr + 1], scalar2=None,
                                          op0=ALU.mult)
                _selfsync(c, v, ins)
                for gr in range(4):
                    ins = v.tensor_copy(out=c.CFB[:, gr, :], in_=c.LG[:, gr, :])
                P.inc("V", ins, f"{p}cf2A{tt}")
                for gr in range(4):
                    _wait(v, wc, P.tok(f"{p}ct{tt}_{gr}"))
                    if tt > 0:
                        _wait(v, wc, P.tok(f"{p}em{tt - 1}_{1}_{0}"))
                    else:
                        _wait(v, wc, prev_poolfree)
                    ins = v.tensor_copy(out=c.CT8[:, gr * 128:(gr + 1) * 128],
                                        in_=c.PT2[0:8, gr * 128:(gr + 1) * 128])
                    P.inc("V", ins, f"{p}cte{tt}_{gr}")
                for e in range(8):
                    _wait(v, wc, P.tok(f"{p}cb{tt}_{e}"))
                    pa = c.PA4 if e % 2 == 0 else c.PA5
                    if tt > 0 and e < 2:
                        _wait(v, wc, P.tok(f"{p}xe{tt - 1}_{1}_{7}"))
                    elif tt == 0 and e < 2:
                        _wait(v, wc, prev_hfree)
                    ins = v.tensor_copy(out=c.HB[:, e, :], in_=pa[:])
                    P.inc("V", ins, f"{p}cbe{tt}_{e}")
                for mh in range(2):
                    for e in range(8):
                        _wait(v, wc, P.tok(f"{p}cbe{tt}_{e}"))
                        gx = (tt * 2 + mh) * 8 + e
                        if gx >= 2:
                            ptt, r = divmod(gx - 2, 16)
                            pmh, pe = divmod(r, 8)
                            _wait(v, wc, P.tok(f"{p}em{ptt}_{pmh}_{pe}"))
                        else:
                            _wait(v, wc, prev_hfree)
                        xb = (gx % 2) * 8
                        for ch in range(8):
                            ins = v.tensor_tensor(out=c.HA[:, xb + ch, :],
                                                  in0=c.XLN[:, ch, :],
                                                  in1=c.HB[:, e, :], op=ALU.mult)
                        P.inc("V", ins, f"{p}xe{tt}_{mh}_{e}")
                    for m in range(4):
                        _wait(v, wc, P.tok(f"{p}em{tt}_{mh}_{7}"))
                        _wait(v, wc, c.rfree[tt])
                        cglob = mh * 4 + m
                        if is_final:
                            jj = (tt * 2 + mh) * 4 + m
                            if jj >= 2:
                                pj = jj - 2
                                ptt2, r2 = divmod(pj, 8)
                                pmh2, pm2 = divmod(r2, 4)
                                _wait(v, wc, P.tok(f"{p}out{ptt2}_{pmh2}_{pm2}"))
                            ins = v.scalar_tensor_tensor(
                                out=c.OSTM[:, m % 2, :], in0=c.PB[m][:],
                                scalar=0.5, in1=c.R[:, cglob, tt * 512:(tt + 1) * 512],
                                op0=ALU.mult, op1=ALU.add)
                        else:
                            ins = v.scalar_tensor_tensor(
                                out=c.R[:, cglob, tt * 512:(tt + 1) * 512], in0=c.PB[m][:],
                                scalar=0.5, in1=c.R[:, cglob, tt * 512:(tt + 1) * 512],
                                op0=ALU.mult, op1=ALU.add)
                        P.inc("V", ins, f"{p}res{tt}_{mh}_{m}")
        blk.vector(_v)

        def _t(t):
            wc = c.wc["tensor"]
            _wait(t, wc, c.small_tok)
            for tt in range(NT):
                aux_war = P.tok(f"{p}cbe{tt - 1}_{7}") if tt > 0 else c.stat_free
                ln_tensor(c, P, p, t, tt, aux_war=aux_war)
                ln_tensor_bcast(c, P, p, t, tt)
                for gr in range(4):
                    _wait(t, wc, P.tok(f"{p}xln{tt}"))
                    pa = c.PA4 if gr % 2 == 0 else c.PA5
                    if gr >= 2:
                        _wait(t, wc, P.tok(f"{p}cfc{tt}_{gr - 2}"))
                    for ch in range(8):
                        t.matmul(out=pa[:, 0:8],
                                 lhsT=c.XLN[:, ch, gr * 128:(gr + 1) * 128],
                                 rhs=c.RWS[:, ch, :], start=(ch == 0), stop=False)
                    ins = t.matmul(out=pa[:, 0:8], lhsT=c.ONES1[:], rhs=c.RBS[:],
                                   start=False, stop=True)
                    P.inc("T", ins, f"{p}lg{tt}_{gr}")
                for gr in range(4):
                    _wait(t, wc, P.tok(f"{p}cf2A{tt}"))
                    if gr > 0:
                        _wait(t, wc, P.tok(f"{p}cte{tt}_{gr - 1}"))
                    elif tt > 0:
                        _wait(t, wc, P.tok(f"{p}cte{tt - 1}_{3}"))
                    else:
                        _wait(t, wc, prev_ptfree)
                    ins = t.transpose(out=c.PT2[0:8, gr * 128:(gr + 1) * 128],
                                      in_=c.CFB[:, gr, :], identity=c.IDENT[:])
                    P.inc("T", ins, f"{p}ct{tt}_{gr}")
                for e in range(8):
                    _wait(t, wc, P.tok(f"{p}cte{tt}_{3}"))
                    _wait(t, wc, P.tok(f"{p}pool"))
                    pa = c.PA4 if e % 2 == 0 else c.PA5
                    if e >= 2:
                        _wait(t, wc, P.tok(f"{p}cbe{tt}_{e - 2}"))
                    else:
                        _wait(t, wc, P.tok(f"{p}cfA{tt}"))
                    ins = t.matmul(out=pa[:], lhsT=c.SEL[:, e, :], rhs=c.CT8[:],
                                   start=True, stop=True)
                    P.inc("T", ins, f"{p}cb{tt}_{e}")
                for mh in range(2):
                    for e in range(8):
                        _wait(t, wc, P.tok(f"{p}ld{tt}_{mh}_{e}"))
                        _wait(t, wc, P.tok(f"{p}xe{tt}_{mh}_{e}"))
                        gli = (tt * 2 + mh) * 8 + e
                        if e == 0:
                            if mh == 1:
                                for m in range(4):
                                    _wait(t, wc, P.tok(f"{p}res{tt}_{0}_{m}"))
                            elif tt > 0:
                                for m in range(4):
                                    _wait(t, wc, P.tok(f"{p}res{tt - 1}_{1}_{m}"))
                            else:
                                for b in range(4):
                                    _wait(t, wc, prev_pbfree[b])
                            for m in range(4):
                                t.matmul(out=c.PB[m][:], lhsT=c.EBS[:, mh * 4 + m, :],
                                         rhs=c.CT8[:], start=True, stop=False)
                        W = c.WB[:, gli % 2].rearrange("p (m k o) -> p m k o", m=4, k=8)
                        xb = (gli % 2) * 8
                        for m in range(4):
                            for k in range(8):
                                ins = t.matmul(out=c.PB[m][:], lhsT=W[:, m, k],
                                               rhs=c.HA[:, xb + k, :],
                                               start=False, stop=(e == 7 and k == 7))
                        P.inc("T", ins, f"{p}em{tt}_{mh}_{e}")
        blk.tensor(_t)

        if is_final:
            def _go(g):
                wc = c.wc["gpsimd"]
                yv = c.y_out.rearrange("s (ch q) t -> q ch s t", q=128)
                for tt in range(NT):
                    for mh in range(2):
                        for m in range(4):
                            _wait(g, wc, P.tok(f"{p}res{tt}_{mh}_{m}"))
                            cglob = mh * 4 + m
                            dst = yv[:, cglob, tt // 4, (tt % 4) * 512:(tt % 4) * 512 + 512]
                            ins = g.dma_start(out=dst, in_=c.OSTM[:, m % 2, :])
                            P.inc("G", ins, f"{p}out{tt}_{mh}_{m}")
            blk.gpsimd(_go)

    P.done()
    for tt in range(NT):
        c.rtok[tt] = P.tok(f"{p}res{tt}_{1}_{3}")
        c.rfree[tt] = P.tok(f"{p}res{tt}_{1}_{3}")
    c.xln_free = P.tok(f"{p}xe{NT - 1}_{1}_{7}")
    c.hbuf_free = P.tok(f"{p}em{NT - 1}_{1}_{7}")
    c.wb_free = P.tok(f"{p}em{NT - 1}_{1}_{7}")
    c.stat_free = P.tok(f"{p}xln{NT - 1}")
    for b in range(4):
        c.pb_free[b] = P.tok(f"{p}res{NT - 1}_{1}_{b}")
    c.pa_free = P.tok(f"{p}cbe{NT - 1}_{7}")
    c.pt_free = P.tok(f"{p}cte{NT - 1}_{3}")
    c.pool_free = P.tok(f"{p}em{NT - 1}_{1}_{0}")



def emit_store(c):
    nc = c.nc
    P = Plan({"V": c.V, "G": c.G})
    emit_store.count = getattr(emit_store, "count", 0) + 1
    p = f"st{emit_store.count}_"
    for tt in range(NT):
        for ch in range(NCH):
            P.add("V", f"{p}c{tt}_{ch}")
            P.add("G", f"{p}o{tt}_{ch}")
    with nc.Block(f"store{emit_store.count}") as blk:
        def _v(v):
            wc = c.wc["vector"]
            for tt in range(NT):
                _wait(v, wc, c.rtok[tt])
                for ch in range(NCH):
                    j = tt * NCH + ch
                    if j >= 2:
                        pt, pc = divmod(j - 2, NCH)
                        _wait(v, wc, P.tok(f"{p}o{pt}_{pc}"))
                    ins = v.tensor_copy(out=c.OSTM[:, j % 2, :],
                                        in_=c.R[:, ch, tt * 512:(tt + 1) * 512])
                    P.inc("V", ins, f"{p}c{tt}_{ch}")
        blk.vector(_v)

        def _g(g):
            wc = c.wc["gpsimd"]
            _wait(g, wc, c.G.tok())
            yv = c.y_out.rearrange("s (ch q) t -> q ch s t", q=128)
            for tt in range(NT):
                for ch in range(NCH):
                    j = tt * NCH + ch
                    _wait(g, wc, P.tok(f"{p}c{tt}_{ch}"))
                    dst = yv[:, ch, tt // 4, (tt % 4) * 512:(tt % 4) * 512 + 512]
                    ins = g.dma_start(out=dst, in_=c.OSTM[:, j % 2, :])
                    P.inc("G", ins, f"{p}o{tt}_{ch}")
        blk.gpsimd(_g)
    P.done()


def attn_kbs(layer, qi):
    if layer == 0:
        return [list(range(4 * kb, 4 * kb + 4)) for kb in range(4)]
    kcs = list(range(qi + 1))
    return [kcs[i:i + 4] for i in range(0, len(kcs), 4)]


def emit_attn(c, layer):
    nc = c.nc
    P = Plan({"T": c.T, "S": c.S, "V": c.V, "SY": c.SY, "G": c.G})
    emit_attn.count = getattr(emit_attn, "count", 0) + 1
    p = f"a{layer}_{emit_attn.count}_"
    L0 = (layer == 0)
    projs = ["q", "k", "v"]
    KR = 67 if L0 else 64
    wsc = {n: c.wsc[f"W_a{layer}{n}"] for n in "qkvo"}

    QAv = c.HC[:, 0:8, :].rearrange("p a b -> p (a b)").rearrange("p (b t) -> p b t", b=2)
    KAv = c.HC[:, 8:16, :].rearrange("p a b -> p (a b)").rearrange("p (b t) -> p b t", b=2)
    CS2v = c.HC[:, 16:24, :].rearrange("p a b -> p (a b)").rearrange("p (b t) -> p b t", b=2)
    STG = c.HC[:, 24:32, :]
    VT = c.HB[:].rearrange("p a b -> p (a b)").rearrange("p (h k d) -> p h k d", h=16, k=16)
    ATT = c.HA[:].rearrange("p a b -> p (a b)").rearrange("p (q d) -> p q d", q=16)
    OSB = c.OSTM[:].rearrange("p a b -> p (a b)").bitcast(BF16)
    PSCP = [c.WB[:, 0, 0:2048], c.WB[:, 0, 2048:4096]]
    PTSP = [c.WB[:, 1, 0:2048], c.WB[:, 1, 2048:4096]]
    PTB = [c.PT2[:, 0:512], c.PB[2][:].bitcast(BF16)[:, 0:512]]
    PVB = [c.PO3, c.PA5]
    PTD = [c.PT2[:, 0:1024], c.PB[2][:].bitcast(BF16)[:, 0:1024]] if False else None

    ld_counter = [0]
    ld_slot = {}
    ld_war = {}

    def plan_load(name, reader_name):
        i = ld_counter[0]
        ld_slot[name] = i % 2
        ld_war[name] = None if i < 2 else plan_load.hist[i - 2][1]
        plan_load.hist.append((name, reader_name))
        ld_counter[0] += 1
        P.add("SY", name)
    plan_load.hist = []

    kbctr = [0]
    kb_bank = {}
    pvb_war = {}
    _pvb_prev = {}
    lora_prev = []
    psc_war = {}
    acc_war = {}
    pts_war = {}
    ptc_war = {}
    _psc_prev = {}
    _acc_prev = {}
    _pts_prev = {}
    _ptc_prev = {}

    if L0:
        P.add("G", p + "rez_ld")
        P.add("T", p + "rez_mm")
        P.add("V", p + "rez_ev")
        P.add("V", p + "ones0")
        P.add("V", p + "ones1")
    lwo_waits = {}
    lw_waits = {}
    for s in range(2):
        if not L0:
            P.add("SY", p + f"cs2a_{s}")
            P.add("SY", p + f"cs2_{s}")
        for tl in range(4):
            gt = s * 4 + tl
            plan_ln(P, p, gt)
            for proj in projs:
                if L0:
                    P.add("G", p + f"lwA{gt}_{proj}")
                    P.add("G", p + f"lwB{gt}_{proj}")
                    P.add("G", p + f"lw{gt}_{proj}")
                    lw_waits[p + f"lw{gt}_{proj}"] = list(lora_prev)
                    lora_prev = [p + f"ev{gt}_{proj}_{7}", p + f"g{gt}_{proj}_{7}"]
                for hf in range(2):
                    plan_load(p + f"wl{gt}_{proj}_{hf}", p + f"g{gt}_{proj}_{3 + 4 * hf}")
                if L0:
                    P.add("T", p + f"u{gt}_{proj}")
                    P.add("S", p + f"ue{gt}_{proj}")
                for m in range(8):
                    P.add("T", p + f"g{gt}_{proj}_{m}")
                    P.add("S", p + f"ev{gt}_{proj}_{m}")
                    if ATTN_STAGE < 1:
                        continue
                    if (not L0) and proj != "v" and "r" in ATTN_FEAT:
                        P.add("T", p + f"sw{gt}_{proj}_{m}")
                        P.add("V", p + f"rp{gt}_{proj}_{m}")
                    if proj != "v":
                        if "n" in ATTN_FEAT:
                            P.add("S", p + f"nsq{gt}_{proj}_{m}")
                            P.add("T", p + f"nmm{gt}_{proj}_{m}")
                            P.add("V", p + f"nrd{gt}_{proj}_{m}")
                    elif "t" in ATTN_FEAT:
                        P.add("T", p + f"vt{gt}_{m}")
                        P.add("V", p + f"vte{gt}_{m}")
                if proj != "v" and ATTN_STAGE >= 1 and "s" in ATTN_FEAT:
                    for ch in range(NCH - 1):
                        P.add("G", p + f"sp{gt}_{proj}c{ch}")
                    P.add("G", p + f"sp{gt}_{proj}")
            if L0 and ATTN_STAGE >= 1 and "s" in ATTN_FEAT:
                for ch in range(NCH - 1):
                    P.add("G", p + f"spx{gt}c{ch}")
                P.add("G", p + f"spx{gt}")
        if ATTN_STAGE >= 1 and "b" in ATTN_FEAT:
            P.add("V", p + f"nb{s}")
            P.add("T", p + f"nbm{s}")
            P.add("V", p + f"nbe{s}")
        if ATTN_STAGE < 2:
            continue
        for h in range(NH):
            P.add("SY", p + f"hq{s}_{h}")
            P.add("SY", p + f"hk{s}_{h}")
            if L0:
                P.add("SY", p + f"ha{s}_{h}")
            for qi in range(16):
                kbs = attn_kbs(layer, qi)
                Lq = len(kbs)
                par = qi % 2
                pvb_war[(s, h, qi)] = _pvb_prev.get(par)
                _pvb_prev[par] = p + f"o{s}_{h}_{qi}"
                for kb in range(Lq):
                    kb_bank[(s, h, qi, kb)] = kbctr[0] % 2
                    kbctr[0] += 1
                for kb in range(min(2, Lq)):
                    P.add("T", p + f"qk{s}_{h}_{qi}_{kb}")
                for kb, kcs in enumerate(kbs):
                    nm = p + f"ex{s}_{h}_{qi}_{kb}"
                    tb = kb_bank[(s, h, qi, kb)]
                    if (not L0) and kb == Lq - 1:
                        P.add("V", p + f"dg{s}_{h}_{qi}")
                    P.add("S", nm)
                    psc_war[nm] = _psc_prev.get((kb, par))
                    acc_war[nm] = _acc_prev.get((kb, par))
                    _psc_prev[(kb, par)] = p + f"pt{s}_{h}_{qi}_{kb}"
                    _acc_prev[(kb, par)] = p + f"o{s}_{h}_{qi}"
                    ptn = p + f"pt{s}_{h}_{qi}_{kb}"
                    P.add("T", ptn)
                    ptc_war[ptn] = _ptc_prev.get(tb)
                    _ptc_prev[tb] = p + f"pte{s}_{h}_{qi}_{kb}"
                    pte = p + f"pte{s}_{h}_{qi}_{kb}"
                    P.add("V", pte)
                    pts_war[pte] = _pts_prev.get((kb, par))
                    _pts_prev[(kb, par)] = p + f"pv{s}_{h}_{qi}_{kb}"
                    if kb + 2 < Lq:
                        P.add("T", p + f"qk{s}_{h}_{qi}_{kb + 2}")
                    P.add("T", p + f"pv{s}_{h}_{qi}_{kb}")
                P.add("V", p + f"o{s}_{h}_{qi}")
        if ATTN_STAGE < 3:
            continue
        for hf in range(2):
            plan_load(p + f"wo{s}_{hf}", p + f"go{s}_{15}_{3 + 4 * hf}")
        if L0 and ATTN_STAGE >= 3:
            P.add("G", p + f"lwoA{s}")
            P.add("G", p + f"lwoB{s}")
            P.add("G", p + f"lwo{s}")
            lwo_waits[p + f"lwo{s}"] = list(lora_prev)
            lora_prev = [p + f"uoe{s}_{15}", p + f"go{s}_{15}_{7}"]
        for qi in range(16):
            if L0:
                P.add("SY", p + f"xr{s}_{qi}")
            P.add("T", p + f"at{s}_{qi}")
            P.add("V", p + f"ate{s}_{qi}")
            if L0:
                P.add("T", p + f"uo{s}_{qi}")
                P.add("S", p + f"uoe{s}_{qi}")
            for m in range(8):
                P.add("T", p + f"go{s}_{qi}_{m}")
                P.add("V", p + f"ro{s}_{qi}_{m}")
    at_war = {}
    for s in range(2):
        for qi in range(16):
            nm = p + f"at{s}_{qi}"
            at_war[nm] = [_ptc_prev.get(0), _ptc_prev.get(1)] if qi == 0 and s == 0 else None

    prev_hfree = c.hbuf_free
    prev_xfree = c.xln_free
    prev_wbfree = c.wb_free
    prev_pbfree = list(c.pb_free)
    prev_pafree = c.pa_free
    prev_ptfree = c.pt_free
    prev_pofree = c.po_free
    prev_poolfree = c.pool_free
    prev_statfree = c.stat_free
    prev_lorafree = getattr(c, "lora_free", None)
    rfree0 = list(c.rfree)

    pbseq = {0: [], 1: []}
    for s in range(2):
        gidx = 0
        for tl in range(4):
            gt = s * 4 + tl
            for proj in projs:
                for m in range(8):
                    gi = (gt * 24 + projs.index(proj) * 8 + m)
                    pbseq[gi % 2].append((p + f"g{gt}_{proj}_{m}", p + f"ev{gt}_{proj}_{m}"))
        if ATTN_STAGE >= 2:
            for h in range(NH):
                for qi in range(16):
                    for kb in range(len(attn_kbs(layer, qi))):
                        pbseq[kb_bank[(s, h, qi, kb)]].append(
                            (p + f"qk{s}_{h}_{qi}_{kb}", p + f"ex{s}_{h}_{qi}_{kb}"))
        if ATTN_STAGE >= 3:
            for qi in range(16):
                for m in range(8):
                    pbseq[(qi * 8 + m) % 2].append((p + f"go{s}_{qi}_{m}",
                                                    p + f"ro{s}_{qi}_{m}"))
    pb_war = {}
    for b in (0, 1):
        prev = None
        for w, r in pbseq[b]:
            pb_war[w] = prev
            prev = r
    pb_last_reader = {b: pbseq[b][-1][1] for b in (0, 1)}

    def tokm(name):
        return None if name is None else P.tok(name)

    def pbwar_tok(writer, bank):
        r = pb_war.get(writer)
        if r is None:
            return prev_pbfree[bank]
        return P.tok(r)

    def pbwar_tok_pb2(s):
        return prev_pbfree[2]

    def proj_bank(gt, proj, m):
        return (gt * 24 + projs.index(proj) * 8 + m) % 2

    def proj_bank(gt, proj, m):
        return (gt * 24 + projs.index(proj) * 8 + m) % 2

    with nc.Block(f"attn{layer}_{emit_attn.count}") as blk:
        def _g(g):
            wc = c.wc["gpsimd"]
            _wait(g, wc, c.G.tok())
            if L0:
                _wait(g, wc, prev_poolfree)
                _wait(g, wc, prev_lorafree)
                ins = g.dma_start(out=c.REZI[:], in_=c.win["W_rez"])
                P.inc("G", ins, p + "rez_ld")
            for s in range(2):
                for tl in range(4):
                    gt = s * 4 + tl
                    for proj in projs:
                        if L0:
                            for wn in lw_waits[p + f"lw{gt}_{proj}"]:
                                _wait(g, wc, P.tok(wn))
                            if gt == 0 and proj == "q":
                                _wait(g, wc, prev_lorafree)
                                _wait(g, wc, prev_poolfree)
                            ins = g.dma_start(out=c.LORA[:], in_=c.win[f"W_a0A{proj}"])
                            P.inc("G", ins, p + f"lwA{gt}_{proj}")
                            ins = g.dma_start(out=c.LORB, in_=c.win[f"W_a0B{proj}"])
                            P.inc("G", ins, p + f"lwB{gt}_{proj}")
                            ins = g.dma_start(out=c.LBIA[:], in_=c.win[f"W_a0b{proj}"])
                            P.inc("G", ins, p + f"lw{gt}_{proj}")
                        if proj != "v" and ATTN_STAGE >= 1 and "s" in ATTN_FEAT:
                            for m in range(8):
                                if (not L0) and "r" in ATTN_FEAT:
                                    _wait(g, wc, P.tok(p + f"rp{gt}_{proj}_{m}"))
                                else:
                                    _wait(g, wc, P.tok(p + f"ev{gt}_{proj}_{m}"))
                            dst = (c.q_sp if proj == "q" else c.k_sp)[s]
                            for ch in range(NCH):
                                ins = g.dma_start(
                                    out=dst[ch, :, tl * 512:(tl + 1) * 512],
                                    in_=STG[:, ch, :])
                                if ch < NCH - 1:
                                    P.inc("G", ins, p + f"sp{gt}_{proj}c{ch}")
                            P.inc("G", ins, p + f"sp{gt}_{proj}")
                    if L0 and ATTN_STAGE >= 1 and "s" in ATTN_FEAT:
                        _wait(g, wc, P.tok(p + f"xln{gt}"))
                        for ch in range(NCH):
                            ins = g.dma_start(
                                out=c.xln_spd[s][ch, :, tl * 512:(tl + 1) * 512],
                                in_=c.XLN[:, ch, :])
                            if ch < NCH - 1:
                                P.inc("G", ins, p + f"spx{gt}c{ch}")
                        P.inc("G", ins, p + f"spx{gt}")
                if L0 and ATTN_STAGE >= 3:
                    for wn in lwo_waits[p + f"lwo{s}"]:
                        _wait(g, wc, P.tok(wn))
                    ins = g.dma_start(out=c.LORA[:], in_=c.win["W_a0Ao"])
                    P.inc("G", ins, p + f"lwoA{s}")
                    ins = g.dma_start(out=c.LORB, in_=c.win["W_a0Bo"])
                    P.inc("G", ins, p + f"lwoB{s}")
                    ins = g.dma_start(out=c.LBIA[:], in_=c.win["W_a0bo"])
                    P.inc("G", ins, p + f"lwo{s}")
        blk.gpsimd(_g)

        def _sy(sy):
            wc = c.wc["sync"]
            for s in range(2):
                if not L0:
                    if s > 0 and ATTN_STAGE >= 1 and "r" in ATTN_FEAT:
                        _wait(sy, wc, P.tok(p + f"rp{(s - 1) * 4 + 3}_k_{7}"))
                    else:
                        _wait(sy, wc, prev_hfree)
                    ins = sy.dma_start(out=CS2v[:, 0, :], in_=c.cin["C_cos"])
                    P.inc("SY", ins, p + f"cs2a_{s}")
                    ins = sy.dma_start(out=CS2v[:, 1, :], in_=c.cin["C_sin"])
                    P.inc("SY", ins, p + f"cs2_{s}")
                for tl in range(4):
                    gt = s * 4 + tl
                    for proj in projs:
                        for hf in range(2):
                            lname = p + f"wl{gt}_{proj}_{hf}"
                            war = ld_war[lname]
                            _wait(sy, wc, P.tok(war) if war else prev_wbfree)
                            _wait(sy, wc, c.cast_tok[f"W_a{layer}{proj}"])
                            sl = wsc[proj][hf * 4:(hf + 1) * 4]
                            dst = c.WB[:, ld_slot[lname]].rearrange("p (m ko) -> p m ko", m=4)
                            ins = sy.dma_start(out=dst,
                                               in_=sl.rearrange("m p k o -> p m (k o)"))
                            P.inc("SY", ins, lname)
                for h in (range(NH) if ATTN_STAGE >= 2 else []):
                    if h >= 2:
                        pq = len(attn_kbs(layer, 15))
                        _wait(sy, wc, P.tok(p + f"qk{s}_{h - 2}_{15}_{pq - 1}"))
                    elif s > 0:
                        pq = len(attn_kbs(layer, 15))
                        _wait(sy, wc, P.tok(p + f"qk{s - 1}_{14 + h}_{15}_{pq - 1}"))
                    else:
                        _wait(sy, wc, prev_hfree)
                    _wait(sy, wc, P.tok(p + f"sp{s * 4 + 3}_q"))
                    ins = sy.dma_start(out=QAv[0:64, h % 2, :],
                                       in_=c.q_sp[s][h // 2, 64 * (h % 2):64 * (h % 2) + 64, :])
                    P.inc("SY", ins, p + f"hq{s}_{h}")
                    _wait(sy, wc, P.tok(p + f"sp{s * 4 + 3}_k"))
                    ins = sy.dma_start(out=KAv[0:64, h % 2, :],
                                       in_=c.k_sp[s][h // 2, 64 * (h % 2):64 * (h % 2) + 64, :])
                    P.inc("SY", ins, p + f"hk{s}_{h}")
                    if L0:
                        ins = sy.dma_start(out=KAv[64:67, h % 2, :], in_=c.cin["C_alibi"][h])
                        P.inc("SY", ins, p + f"ha{s}_{h}")
                for hf in (range(2) if ATTN_STAGE >= 3 else []):
                    lname = p + f"wo{s}_{hf}"
                    war = ld_war[lname]
                    _wait(sy, wc, P.tok(war) if war else prev_wbfree)
                    if ATTN_STAGE >= 2:
                        lkb = len(attn_kbs(layer, 15)) - 1
                        _wait(sy, wc, P.tok(p + f"pv{s}_{15}_{15}_{lkb}"))
                    _wait(sy, wc, c.cast_tok[f"W_a{layer}o"])
                    sl = wsc["o"][hf * 4:(hf + 1) * 4]
                    dst = c.WB[:, ld_slot[lname]].rearrange("p (m ko) -> p m ko", m=4)
                    ins = sy.dma_start(out=dst, in_=sl.rearrange("m p k o -> p m (k o)"))
                    P.inc("SY", ins, lname)
                if L0 and ATTN_STAGE >= 3:
                    for qi in range(16):
                        if qi > 0:
                            _wait(sy, wc, P.tok(p + f"ro{s}_{qi - 1}_{7}"))
                        else:
                            _wait(sy, wc, P.tok(p + f"spx{s * 4 + 3}"))
                            _wait(sy, wc, P.tok(p + f"g{s * 4 + 3}_v_{7}"))
                            if L0:
                                _wait(sy, wc, P.tok(p + f"u{s * 4 + 3}_v"))
                        ins = sy.dma_start(
                            out=c.XLN[:, :, 0:128],
                            in_=c.xln_spd[s][:, :, qi * 128:(qi + 1) * 128].rearrange(
                                "m q t -> q m t"))
                        P.inc("SY", ins, p + f"xr{s}_{qi}")
        blk.sync(_sy)

        def _s(sc):
            wc = c.wc["scalar"]
            for s in range(2):
                for tl in range(4):
                    gt = s * 4 + tl
                    ln_scalar(c, P, p, sc, gt,
                              prev_sq_reader=(None if gt > 0 else prev_statfree))
                    for proj in projs:
                        if L0:
                            _wait(sc, wc, P.tok(p + f"u{gt}_{proj}"))
                            ins = sc.activation(out=c.USB[:, :], in_=c.PA4[0:8, :],
                                                func=AF.Identity, scale=0.125)
                            P.inc("S", ins, p + f"ue{gt}_{proj}")
                        for m in range(8):
                            _wait(sc, wc, P.tok(p + f"g{gt}_{proj}_{m}"))
                            if ATTN_STAGE >= 1 and "s" in ATTN_FEAT:
                                if proj == "k":
                                    _wait(sc, wc, P.tok(p + f"sp{gt}_q"))
                                elif proj == "v":
                                    _wait(sc, wc, P.tok(p + f"sp{gt}_k"))
                            pb = c.PB[proj_bank(gt, proj, m)]
                            if L0:
                                _wait(sc, wc, P.tok(p + f"lw{gt}_{proj}"))
                                ins = sc.activation(out=STG[:, m, :], in_=pb[:],
                                                    func=AF.Identity,
                                                    bias=c.LBIA[:, m:m + 1])
                            else:
                                ins = sc.activation(out=STG[:, m, :], in_=pb[:],
                                                    func=AF.Identity)
                            P.inc("S", ins, p + f"ev{gt}_{proj}_{m}")
                            if proj != "v" and ATTN_STAGE >= 1 and "n" in ATTN_FEAT:
                                if (not L0) and "r" in ATTN_FEAT:
                                    _wait(sc, wc, P.tok(p + f"rp{gt}_{proj}_{m}"))
                                if m > 0:
                                    _wait(sc, wc, P.tok(p + f"nmm{gt}_{proj}_{m - 1}"))
                                ins = sc.activation(out=OSB[:, 1024:1536],
                                                    in_=STG[:, m, :], func=AF.Square)
                                P.inc("S", ins, p + f"nsq{gt}_{proj}_{m}")
                for h in (range(NH) if ATTN_STAGE >= 2 else []):
                    for qi in range(16):
                        kbs = attn_kbs(layer, qi)
                        for kb, kcs in enumerate(kbs):
                            nm = p + f"ex{s}_{h}_{qi}_{kb}"
                            Lkb = len(kcs) * 128
                            _wait(sc, wc, P.tok(p + f"nbe{s}"))
                            if (not L0) and (qi in kcs):
                                _wait(sc, wc, P.tok(p + f"dg{s}_{h}_{qi}"))
                            else:
                                _wait(sc, wc, P.tok(p + f"qk{s}_{h}_{qi}_{kb}"))
                            if psc_war[nm] is None:
                                _wait(sc, wc, prev_wbfree)
                            _wait(sc, wc, P.tok(p + f"g{s * 4 + 3}_v_{7}"))
                            _wait(sc, wc, tokm(psc_war[nm]))
                            _wait(sc, wc, tokm(acc_war[nm]))
                            par = qi % 2
                            ins = sc.activation(out=PSCP[par][:, kb * 512:kb * 512 + Lkb],
                                                in_=c.PB[kb_bank[(s, h, qi, kb)]][:, 0:Lkb],
                                                func=AF.Exp,
                                                scale=0.125, bias=c.ABIA[:, h:h + 1],
                                                accum_out=c.ACC4[:, 4 * par + kb:
                                                                 4 * par + kb + 1])
                            P.inc("S", ins, nm)
                if L0 and ATTN_STAGE >= 3:
                    for qi in range(16):
                        _wait(sc, wc, P.tok(p + f"uo{s}_{qi}"))
                        ins = sc.activation(out=c.USB[:, 0:128], in_=c.PA4[0:8, 0:128],
                                            func=AF.Identity, scale=0.125)
                        P.inc("S", ins, p + f"uoe{s}_{qi}")
        blk.scalar(_s)

        def _v(v):
            wc = c.wc["vector"]
            if L0:
                _wait(v, wc, P.tok(p + "rez_mm"))
                ins = v.tensor_copy(out=c.REZ[:], in_=c.PA4[:, 0:1])
                P.inc("V", ins, p + "rez_ev")
                _wait(v, wc, prev_hfree)
                for b in range(2):
                    ins = v.memset(QAv[64:67, b, :], 1.0)
                    P.inc("V", ins, p + f"ones{b}")
            for s in range(2):
                for tl in range(4):
                    gt = s * 4 + tl
                    if gt == 0:
                        xw = [prev_xfree]
                    else:
                        xw = [P.tok(p + f"g{gt - 1}_v_{7}")]
                        if ATTN_STAGE >= 1 and "s" in ATTN_FEAT:
                            xw.append(P.tok(p + f"spx{gt - 1}") if L0
                                      else P.tok(p + f"sp{gt - 1}_k"))
                        if L0:
                            xw.append(P.tok(p + f"u{gt - 1}_v"))
                    for w in xw:
                        _wait(v, wc, w)
                    ln_vector(c, P, p, v, gt, xln_war=None)
                    for proj in (projs if ATTN_STAGE >= 1 else []):
                        for m in range(8):
                            if (not L0) and proj != "v" and "r" in ATTN_FEAT:
                                _wait(v, wc, P.tok(p + f"sw{gt}_{proj}_{m}"))
                                _wait(v, wc, P.tok(p + f"ev{gt}_{proj}_{m}"))
                                tloc = tl
                                cosl = CS2v[:, 0, tloc * 512:(tloc + 1) * 512]
                                sinl = CS2v[:, 1, tloc * 512:(tloc + 1) * 512]
                                v.tensor_tensor(out=OSB[:, 0:512], in0=STG[:, m, :],
                                                in1=cosl, op=ALU.mult)
                                v.tensor_tensor(out=OSB[:, 512:1024], in0=c.PA5[:],
                                                in1=sinl, op=ALU.mult)
                                ins = v.tensor_tensor(out=STG[:, m, :], in0=OSB[:, 0:512],
                                                      in1=OSB[:, 512:1024], op=ALU.add)
                                P.inc("V", ins, p + f"rp{gt}_{proj}_{m}")
                            if proj != "v":
                                if "n" not in ATTN_FEAT:
                                    continue
                                _wait(v, wc, P.tok(p + f"nmm{gt}_{proj}_{m}"))
                                col = projs.index(proj) * 32 + m * 4 + tl
                                ins = v.reduce_max(out=c.NRM2[0:2, col:col + 1],
                                                   in_=c.PO3[0:2, :], axis=AX.X)
                                P.inc("V", ins, p + f"nrd{gt}_{proj}_{m}")
                            else:
                                if "t" not in ATTN_FEAT:
                                    continue
                                _wait(v, wc, P.tok(p + f"vt{gt}_{m}"))
                                dstv = VT[:, 2 * m:2 * m + 2, tl * 4:(tl + 1) * 4, :]
                                ins = v.tensor_copy(
                                    out=dstv,
                                    in_=c.PT2[:, (m % 2) * 512:(m % 2) * 512 + 512]
                                    .rearrange("p (k h d) -> p h k d", k=4, h=2))
                                P.inc("V", ins, p + f"vte{gt}_{m}")
                if ATTN_STAGE < 1 or "b" not in ATTN_FEAT:
                    continue
                for proj in ["q", "k"]:
                    for m in range(8):
                        _wait(v, wc, P.tok(p + f"nrd{s * 4 + 3}_{proj}_{m}"))
                ins = v.engine_nop()
                _selfsync(c, v, ins)
                for j in range(16):
                    pi, m = divmod(j, 8)
                    ins = v.reduce_max(out=c.QM[0:2, j:j + 1],
                                       in_=c.NRM2[0:2, pi * 32 + m * 4:pi * 32 + m * 4 + 4],
                                       axis=AX.X)
                _selfsync(c, v, ins)
                v.tensor_tensor(out=c.BHE[0:2, 0:8], in0=c.QM[0:2, 0:8],
                                in1=c.QM[0:2, 8:16], op=ALU.add)
                ins = v.tensor_scalar(out=c.BHE[0:2, 0:8], in0=c.BHE[0:2, 0:8],
                                      scalar1=-1.0 / 16.0, scalar2=None, op0=ALU.mult)
                _selfsync(c, v, ins)
                ins = v.tensor_copy(out=c.QM[0:2, 0:16].rearrange("p (a b) -> p a b", b=2),
                                    in_=c.BHE[0:2, 0:8].unsqueeze(2).broadcast_to([2, 8, 2]))
                _selfsync(c, v, ins)
                if L0:
                    ins = v.tensor_tensor(out=c.QM[0:2, 0:16], in0=c.QM[0:2, 0:16],
                                          in1=c.M2[0:2, :], op=ALU.mult)
                    _selfsync(c, v, ins)
                    ins = v.tensor_tensor(out=c.QM[0:2, 0:16], in0=c.QM[0:2, 0:16],
                                          in1=c.AB2[0:2, :], op=ALU.add)
                else:
                    ins = v.tensor_tensor(out=c.QM[0:2, 0:16], in0=c.QM[0:2, 0:16],
                                          in1=c.M2[0:2, :], op=ALU.mult)
                P.inc("V", ins, p + f"nb{s}")
                _wait(v, wc, P.tok(p + f"nbm{s}"))
                ins = v.tensor_copy(out=c.ABIA[:], in_=c.PA4[:, 0:16])
                P.inc("V", ins, p + f"nbe{s}")
                for h in (range(NH) if ATTN_STAGE >= 2 else []):
                    for qi in range(16):
                        kbs = attn_kbs(layer, qi)
                        for kb, kcs in enumerate(kbs):
                            if (not L0) and kb == len(kbs) - 1:
                                dkb = qi // 4
                                db = kb_bank[(s, h, qi, dkb)]
                                dcol = (qi - 4 * dkb) * 128
                                _wait(v, wc, P.tok(p + f"qk{s}_{h}_{qi}_{dkb}"))
                                ins = v.tensor_tensor(out=c.PB[db][:, dcol:dcol + 128],
                                                      in0=c.PB[db][:, dcol:dcol + 128],
                                                      in1=c.MASKC[:], op=ALU.add)
                                P.inc("V", ins, p + f"dg{s}_{h}_{qi}")
                            pte = p + f"pte{s}_{h}_{qi}_{kb}"
                            Lkb = len(kcs) * 128
                            _wait(v, wc, P.tok(p + f"pt{s}_{h}_{qi}_{kb}"))
                            _wait(v, wc, tokm(pts_war[pte]))
                            ins = v.tensor_copy(
                                out=PTSP[qi % 2][:, kb * 512:kb * 512 + Lkb],
                                in_=PTB[kb_bank[(s, h, qi, kb)]][:, 0:Lkb])
                            P.inc("V", ins, pte)
                        _wait(v, wc, P.tok(p + f"pv{s}_{h}_{qi}_{len(kbs) - 1}"))
                        if h == 0 and qi == 0 and s == 0:
                            _wait(v, wc, prev_hfree)
                        par = qi % 2
                        ins = v.reduce_sum(out=c.RD[:, 2 * par:2 * par + 1],
                                           in_=c.ACC4[:, 4 * par:4 * par + len(kbs)],
                                           axis=AX.X)
                        _selfsync(c, v, ins)
                        ins = v.reciprocal(out=c.RD[:, 2 * par + 1:2 * par + 2],
                                           in_=c.RD[:, 2 * par:2 * par + 1])
                        _selfsync(c, v, ins)
                        ins = v.tensor_scalar(out=ATT[:, qi, h * 64:(h + 1) * 64],
                                              in0=PVB[par][:, 0:64],
                                              scalar1=c.RD[:, 2 * par + 1:2 * par + 2],
                                              scalar2=None, op0=ALU.mult)
                        P.inc("V", ins, p + f"o{s}_{h}_{qi}")
                for qi in (range(16) if ATTN_STAGE >= 3 else []):
                    _wait(v, wc, P.tok(p + f"at{s}_{qi}"))
                    if qi >= 1:
                        _wait(v, wc, P.tok(p + f"go{s}_{qi - 1}_{7}"))
                    elif s == 0:
                        _wait(v, wc, prev_poolfree)
                    ins = v.tensor_copy(out=c.AF_[:], in_=c.PT2[:].rearrange(
                        "p (a b) -> p a b", a=8))
                    P.inc("V", ins, p + f"ate{s}_{qi}")
                    goff = s * SEQ + qi * 128
                    for m in range(8):
                        _wait(v, wc, P.tok(p + f"go{s}_{qi}_{m}"))
                        _wait(v, wc, rfree0[goff // 512])
                        pb = c.PB[(qi * 8 + m) % 2]
                        if L0:
                            _wait(v, wc, P.tok(p + f"xr{s}_{qi}"))
                            v.tensor_tensor(out=c.R[:, m, goff:goff + 128],
                                            in0=c.R[:, m, goff:goff + 128],
                                            in1=c.XLN[:, m, 0:128], op=ALU.add)
                            ins = v.scalar_tensor_tensor(
                                out=c.R[:, m, goff:goff + 128], in0=pb[:, 0:128],
                                scalar=c.REZ[:, 0:1], in1=c.R[:, m, goff:goff + 128],
                                op0=ALU.mult, op1=ALU.add)
                        else:
                            ins = v.tensor_tensor(out=c.R[:, m, goff:goff + 128],
                                                  in0=c.R[:, m, goff:goff + 128],
                                                  in1=pb[:, 0:128], op=ALU.add)
                        P.inc("V", ins, p + f"ro{s}_{qi}_{m}")
        blk.vector(_v)

        def _t(t):
            wc = c.wc["tensor"]
            if L0:
                _wait(t, wc, P.tok(p + "rez_ld"))
                _wait(t, wc, prev_pafree)
                ins = t.matmul(out=c.PA4[:, 0:1], lhsT=c.ONES1F[:], rhs=c.REZI[:],
                               start=True, stop=True)
                P.inc("T", ins, p + "rez_mm")
            for s in range(2):
                for tl in range(4):
                    gt = s * 4 + tl
                    aux = [P.tok(p + "rez_ev")] if (L0 and gt == 0) else []
                    if gt > 0:
                        if L0:
                            aux.append(P.tok(p + f"ue{gt - 1}_v"))
                        else:
                            aux.append(P.tok(p + f"xln{gt - 1}"))
                        if (not L0) and ATTN_STAGE >= 1 and "r" in ATTN_FEAT:
                            aux.append(P.tok(p + f"rp{gt - 1}_k_{7}"))
                    elif s == 0:
                        aux.append(prev_pafree)
                        aux.append(prev_statfree)
                    else:
                        pass
                    if s > 0 and tl == 0:
                        if L0 and ATTN_STAGE >= 3:
                            aux.append(P.tok(p + f"uoe{s - 1}_{15}"))
                        if ATTN_STAGE >= 1 and "b" in ATTN_FEAT:
                            aux.append(P.tok(p + f"nbe{s - 1}"))
                        if ATTN_STAGE >= 2:
                            aux.append(P.tok(p + f"o{s - 1}_{15}_{15}"))
                    for a in aux:
                        _wait(t, wc, a)
                    ln_tensor(c, P, p, t, gt, aux_war=None)
                    ln_tensor_bcast(c, P, p, t, gt)
                    for proj in projs:
                        if L0:
                            _wait(t, wc, P.tok(p + f"lw{gt}_{proj}"))
                            _wait(t, wc, P.tok(p + f"xln{gt}"))
                            if projs.index(proj) > 0:
                                _wait(t, wc, P.tok(p + f"ue{gt}_{projs[projs.index(proj) - 1]}"))
                            for k in range(8):
                                ins = t.matmul(out=c.PA4[0:8, :], lhsT=c.LORA[:, k, :],
                                               rhs=c.XLN[:, k, :], start=(k == 0),
                                               stop=(k == 7))
                            P.inc("T", ins, p + f"u{gt}_{proj}")
                        for m in range(8):
                            gname = p + f"g{gt}_{proj}_{m}"
                            hf = m // 4
                            lname = p + f"wl{gt}_{proj}_{hf}"
                            _wait(t, wc, P.tok(lname))
                            _wait(t, wc, P.tok(p + f"xln{gt}"))
                            _wait(t, wc, pbwar_tok(gname, proj_bank(gt, proj, m)))
                            if L0:
                                _wait(t, wc, P.tok(p + f"ue{gt}_{proj}"))
                            lhs = wl_lhsT_mk(c, ld_slot[lname], m)
                            pb = c.PB[proj_bank(gt, proj, m)]
                            for k in range(8):
                                ins = t.matmul(out=pb[:], lhsT=lhs[k], rhs=c.XLN[:, k, :],
                                               start=(k == 0), stop=(False if L0 else k == 7))
                            if L0:
                                ins = t.matmul(out=pb[:], lhsT=c.LORB[:, m, :],
                                               rhs=c.USB[:, :], start=False, stop=True)
                            P.inc("T", ins, gname)
                            if ATTN_STAGE < 1:
                                continue
                            if (not L0) and proj != "v" and "r" in ATTN_FEAT:
                                _wait(t, wc, P.tok(p + f"ev{gt}_{proj}_{m}"))
                                _wait(t, wc, P.tok(p + f"cs2_{s}"))
                                if m > 0:
                                    _wait(t, wc, P.tok(p + f"rp{gt}_{proj}_{m - 1}"))
                                elif not (gt == 0 and proj == "q"):
                                    pp = "q" if proj == "k" else "k"
                                    pgt = gt if proj == "k" else gt - 1
                                    _wait(t, wc, P.tok(p + f"rp{pgt}_{pp}_{7}"))
                                if s > 0 and ATTN_STAGE >= 2 and gt == s * 4 and                                         proj == "q" and m == 0:
                                    _wait(t, wc, P.tok(p + f"o{s - 1}_{15}_{15}"))
                                ins = t.matmul(out=c.PA5[:], lhsT=c.SWT[:],
                                               rhs=STG[:, m, :], start=True, stop=True)
                                P.inc("T", ins, p + f"sw{gt}_{proj}_{m}")
                            if proj != "v":
                                if "n" not in ATTN_FEAT:
                                    continue
                                _wait(t, wc, P.tok(p + f"nsq{gt}_{proj}_{m}"))
                                if m > 0:
                                    _wait(t, wc, P.tok(p + f"nrd{gt}_{proj}_{m - 1}"))
                                else:
                                    prevnrd = None
                                    if proj == "k":
                                        prevnrd = p + f"nrd{gt}_q_{7}"
                                    elif gt > 0:
                                        prevnrd = p + f"nrd{gt - 1}_k_{7}"
                                    elif s > 0:
                                        prevnrd = None
                                    if prevnrd:
                                        _wait(t, wc, P.tok(prevnrd))
                                    elif s > 0:
                                        _wait(t, wc, P.tok(p + f"nb{s - 1}"))
                                    else:
                                        _wait(t, wc, prev_pofree)
                                ins = t.matmul(out=c.PO3[0:2, :], lhsT=c.H2[:],
                                               rhs=OSB[:, 1024:1536], start=True, stop=True)
                                P.inc("T", ins, p + f"nmm{gt}_{proj}_{m}")
                            else:
                                if "t" not in ATTN_FEAT:
                                    continue
                                _wait(t, wc, P.tok(p + f"ev{gt}_v_{m}"))
                                if m >= 1:
                                    _wait(t, wc, P.tok(p + f"vte{gt}_{m - 1}"))
                                elif gt > 0:
                                    _wait(t, wc, P.tok(p + f"vte{gt - 1}_{7}"))
                                elif s == 0:
                                    _wait(t, wc, prev_ptfree)
                                for kc in range(4):
                                    ins = t.transpose(
                                        out=c.PT2[:, (m % 2) * 512 + kc * 128:
                                                  (m % 2) * 512 + (kc + 1) * 128],
                                        in_=STG[:, m, kc * 128:(kc + 1) * 128],
                                        identity=c.IDENT[:])
                                P.inc("T", ins, p + f"vt{gt}_{m}")
                if ATTN_STAGE < 1 or "b" not in ATTN_FEAT:
                    continue
                _wait(t, wc, P.tok(p + f"nb{s}"))
                if L0:
                    _wait(t, wc, P.tok(p + f"ue{s * 4 + 3}_v"))
                ins = t.matmul(out=c.PA4[:, 0:16], lhsT=c.ONES2F[:], rhs=c.QM[0:2, 0:16],
                               start=True, stop=True)
                P.inc("T", ins, p + f"nbm{s}")
                for h in (range(NH) if ATTN_STAGE >= 2 else []):
                    for qi in range(16):
                        kbs = attn_kbs(layer, qi)
                        Lq = len(kbs)

                        def issue_qk(kb, _s=s, _h=h, _qi=qi, _kbs=kbs):
                            kcs = _kbs[kb]
                            Lkb = len(kcs) * 128
                            qkn = p + f"qk{_s}_{_h}_{_qi}_{kb}"
                            _wait(t, wc, P.tok(p + f"hq{_s}_{_h}"))
                            _wait(t, wc, P.tok(p + f"hk{_s}_{_h}"))
                            if L0:
                                _wait(t, wc, P.tok(p + f"ha{_s}_{_h}"))
                                _wait(t, wc, P.tok(p + "ones0"))
                                _wait(t, wc, P.tok(p + "ones1"))
                            _wait(t, wc, pbwar_tok(qkn, kb_bank[(_s, _h, _qi, kb)]))
                            ins = t.matmul(
                                out=c.PB[kb_bank[(_s, _h, _qi, kb)]][:, 0:Lkb],
                                lhsT=QAv[0:KR, _h % 2, _qi * 128:(_qi + 1) * 128],
                                rhs=KAv[0:KR, _h % 2,
                                        kcs[0] * 128:kcs[0] * 128 + Lkb],
                                start=True, stop=True)
                            P.inc("T", ins, qkn)

                        for kb in range(min(2, Lq)):
                            issue_qk(kb)
                        for kb, kcs in enumerate(kbs):
                            ptn = p + f"pt{s}_{h}_{qi}_{kb}"
                            _wait(t, wc, P.tok(p + f"ex{s}_{h}_{qi}_{kb}"))
                            w_ = ptc_war[ptn]
                            _wait(t, wc, P.tok(p + f"vte{s * 4 + 3}_{7}"))
                            if w_ is None:
                                _wait(t, wc, prev_ptfree)
                                _wait(t, wc, pbwar_tok_pb2(s))
                            else:
                                _wait(t, wc, P.tok(w_))
                            dstb = PTB[kb_bank[(s, h, qi, kb)]]
                            for j in range(len(kcs)):
                                ins = t.transpose(
                                    out=dstb[:, j * 128:(j + 1) * 128],
                                    in_=PSCP[qi % 2][:, kb * 512 + j * 128:
                                                     kb * 512 + j * 128 + 128],
                                    identity=c.IDENT[:])
                            P.inc("T", ins, ptn)
                            if kb + 2 < Lq:
                                issue_qk(kb + 2)
                            _wait(t, wc, P.tok(p + f"pte{s}_{h}_{qi}_{kb}"))
                            par = qi % 2
                            if kb == 0:
                                _wait(t, wc, P.tok(p + f"vte{s * 4 + 3}_{7}"))
                                _wait(t, wc, P.tok(p + f"nrd{s * 4 + 3}_k_{7}"))
                                if (not L0) and par == 1:
                                    _wait(t, wc, P.tok(p + f"rp{s * 4 + 3}_k_{7}"))
                                w_ = pvb_war.get((s, h, qi))
                                if w_ is not None:
                                    _wait(t, wc, P.tok(w_))
                                else:
                                    _wait(t, wc, prev_pofree)
                                    _wait(t, wc, prev_pafree)
                            for j, kc in enumerate(kcs):
                                ins = t.matmul(out=PVB[par][:, 0:64],
                                               lhsT=PTSP[par][:, (kb * 4 + j) * 128:
                                                              (kb * 4 + j) * 128 + 128],
                                               rhs=VT[:, h, kc, :],
                                               start=(kb == 0 and j == 0),
                                               stop=(kb == len(kbs) - 1 and
                                                     j == len(kcs) - 1))
                            P.inc("T", ins, p + f"pv{s}_{h}_{qi}_{kb}")
                for qi in (range(16) if ATTN_STAGE >= 3 else []):
                    atn = p + f"at{s}_{qi}"
                    _wait(t, wc, P.tok(p + f"o{s}_{15}_{qi}"))
                    if qi > 0:
                        _wait(t, wc, P.tok(p + f"ate{s}_{qi - 1}"))
                    else:
                        lkb = len(attn_kbs(layer, 15)) - 1
                        _wait(t, wc, P.tok(p + f"pte{s}_{15}_{15}_{lkb}"))
                    for m in range(8):
                        ins = t.transpose(out=c.PT2[:, m * 128:(m + 1) * 128],
                                          in_=ATT[:, qi, m * 128:(m + 1) * 128],
                                          identity=c.IDENT[:])
                    P.inc("T", ins, atn)
                    if L0:
                        _wait(t, wc, P.tok(p + f"lwo{s}"))
                        _wait(t, wc, P.tok(p + f"ate{s}_{qi}"))
                        if qi > 0:
                            _wait(t, wc, P.tok(p + f"uoe{s}_{qi - 1}"))
                        else:
                            _wait(t, wc, P.tok(p + f"nbe{s}"))
                        for k in range(8):
                            ins = t.matmul(out=c.PA4[0:8, 0:128], lhsT=c.LORA[:, k, :],
                                           rhs=c.AF_[:, k, :], start=(k == 0), stop=(k == 7))
                        P.inc("T", ins, p + f"uo{s}_{qi}")
                    for m in range(8):
                        gname = p + f"go{s}_{qi}_{m}"
                        hf = m // 4
                        lname = p + f"wo{s}_{hf}"
                        _wait(t, wc, P.tok(lname))
                        _wait(t, wc, P.tok(p + f"ate{s}_{qi}"))
                        _wait(t, wc, pbwar_tok(gname, (qi * 8 + m) % 2))
                        if L0:
                            _wait(t, wc, P.tok(p + f"uoe{s}_{qi}"))
                        lhs = wl_lhsT_mk(c, ld_slot[lname], m)
                        pb = c.PB[(qi * 8 + m) % 2]
                        for k in range(8):
                            ins = t.matmul(out=pb[:, 0:128], lhsT=lhs[k],
                                           rhs=c.AF_[:, k, :], start=(k == 0),
                                           stop=(False if L0 else k == 7))
                        if L0:
                            ins = t.matmul(out=pb[:, 0:128], lhsT=c.LORB[:, m, :],
                                           rhs=c.USB[:, 0:128], start=False, stop=True)
                        P.inc("T", ins, gname)
        blk.tensor(_t)

    P.done()
    if ATTN_STAGE >= 3:
        last = P.tok(p + f"ro{1}_{15}_{7}")
    elif ATTN_STAGE == 2:
        last = P.tok(p + f"o{1}_{15}_{15}")
    elif ATTN_STAGE == 1:
        last = P.tok(p + f"nbe{1}") if "b" in ATTN_FEAT else P.tok(p + f"ev{7}_v_{7}")
    else:
        last = P.tok(p + f"ev{7}_v_{7}")
    for tt in range(NT):
        c.rtok[tt] = last
        c.rfree[tt] = last
    c.xln_free = last
    c.hbuf_free = last
    c.wb_free = last
    c.stat_free = P.tok(p + f"xln{7}")
    for b in range(3):
        c.pb_free[b] = last
    c.pa_free = last
    c.pt_free = last
    c.po_free = last
    c.pool_free = last
    c.lora_free = last if L0 else getattr(c, "lora_free", None)


def wl_lhsT_mk(c, slot, m):
    off = (m % 4) * 1024
    W = c.WB[:, slot]
    return [W[:, off + k * 128: off + (k + 1) * 128] for k in range(8)]




def run_group(group, x_cur, wd):
    nc = build_program(group)
    need = weights_needed(group)
    in_maps = []
    for ci in range(NCORES):
        m = {"x": x_cur[ci]}
        for n in WSHAPES:
            if n in need:
                m[n] = wd[n]
        for n in CSHAPES:
            m[n] = wd[n]
        in_maps.append(m)
    res = run_bass_kernel_spmd(nc, in_maps, core_ids=list(range(NCORES)))
    return [np.asarray(res.results[ci]["y"]) for ci in range(NCORES)]


def kernel(**inputs):
    wd = host_prepare(inputs)
    xs = shard_x(np.asarray(inputs["L0_attn_x"], np.float32))
    groups = [SUBLAYERS]
    x_cur = xs
    for group in groups:
        x_cur = run_group(group, x_cur, wd)
    return unshard_y(x_cur)
